# version 11
# baseline (speedup 1.0000x reference)
import numpy as np
import ml_dtypes
import concourse.bass as bass
import concourse.mybir as mybir
from concourse.bass_utils import run_bass_kernel_spmd

F32 = mybir.dt.float32
BF16 = mybir.dt.bfloat16
AF = mybir.ActivationFunctionType
ALU = mybir.AluOpType
AXX = mybir.AxisListType.X

MAGIC = float(1.5 * 2 ** 23)
BIG = 57344.0
LN_EPS = 1e-5

D_MODEL = 1024
D_INNER = 2048
NH = 64
HD = 32
DS = 128
DCONV = 4
CONV_DIM = D_INNER + 2 * DS
D_IN = 2 * D_INNER + 2 * DS + NH
FFN_I = 3072
NZC = D_INNER // 128
NXC = CONV_DIM // 128
NDC = D_MODEL // 128
QMAX = 32767.0
U16S = 65536.0

ENGS = ("PE", "DVE", "ACT", "POOL", "SP")


class Cfg:
    def __init__(self, L=2048, nlayer=4, do_ffn=True, n_cores=8, stage="all"):
        self.stage = stage
        self.L = L
        self.C = 128
        self.TSEG = min(512, L)
        assert L % self.TSEG == 0 and self.TSEG % self.C == 0
        self.NSEG = L // self.TSEG
        self.NCH = self.TSEG // self.C
        self.nlayer = nlayer
        self.do_ffn = do_ffn
        self.n_cores = n_cores


class Sched:

    NLANES = 6

    def __init__(self, nc):
        self.nc = nc
        self.sem = {e: nc.alloc_semaphore(f"sem_{e}") for e in ENGS}
        self.dsem = {}
        self.cnt = {k: 0 for k in ENGS}
        for qn in ("SP", "ACT", "POOL"):
            for ln in range(self.NLANES):
                key = f"d{qn}:{ln}"
                self.dsem[key] = nc.alloc_semaphore(f"dma_{qn}_{ln}")
                self.cnt[key] = 0
        self.ccsem = nc.alloc_semaphore("sem_cc")
        self.cnt["CC"] = 0
        self.dma_issue = {qn: 0 for qn in ("SP", "ACT", "POOL")}
        self.q = {e: [] for e in ENGS}
        self.waited = {e: {} for e in ENGS}
        self.last_w = {}
        self.readers = {}
        self.nops = 0

    def _sobj(self, semkey):
        if semkey in self.sem:
            return self.sem[semkey]
        if semkey == "CC":
            return self.ccsem
        return self.dsem[semkey]

    def _wait(self, eng, semkey, val):
        if val <= self.waited[eng].get(semkey, 0):
            return
        assert val <= self.cnt[semkey], f"wait on future {semkey}:{val}"
        self.waited[eng][semkey] = val
        sobj = self._sobj(semkey)
        self.q[eng].append(lambda e, s=sobj, v=val: e.wait_ge(s, v))

    def _edges(self, eng, reads, writes):
        for k in reads:
            lw = self.last_w.get(k)
            if lw is not None and not (lw[0] == "PE" == eng):
                self._wait(eng, lw[0], lw[1])
        for k in writes:
            lw = self.last_w.get(k)
            if lw is not None and not (lw[0] == "PE" == eng):
                self._wait(eng, lw[0], lw[1])
            for skey, v in self.readers.get(k, ()):
                if not (skey == "PE" == eng):
                    self._wait(eng, skey, v)

    def _record(self, semkey, val, reads, writes):
        for k in writes:
            self.last_w[k] = (semkey, val)
            self.readers[k] = []
        for k in reads:
            self.readers.setdefault(k, []).append((semkey, val))

    def op(self, eng, fn, reads=(), writes=()):
        self._edges(eng, reads, writes)
        self.cnt[eng] += 1
        sobj = self.sem[eng]
        self.q[eng].append(lambda e, f=fn, s=sobj: f(e).then_inc(s, 1))
        self._record(eng, self.cnt[eng], reads, writes)
        self.nops += 1

    def ops(self, eng, fns, reads=(), writes=()):
        self._edges(eng, reads, writes)
        self.cnt[eng] += 1
        sobj = self.sem[eng]
        fl = list(fns)

        def run(e, fl=fl, s=sobj):
            for f in fl[:-1]:
                f(e)
            fl[-1](e).then_inc(s, 1)

        self.q[eng].append(run)
        self._record(eng, self.cnt[eng], reads, writes)
        self.nops += len(fl)

    def dma(self, queue, fn, reads=(), writes=()):
        lane = self.dma_issue[queue] % self.NLANES
        self.dma_issue[queue] += 1
        semkey = f"d{queue}:{lane}"
        self._wait(queue, semkey, self.cnt[semkey])
        self._edges(queue, reads, writes)
        self.cnt[semkey] += 16
        sobj = self.dsem[semkey]
        self.q[queue].append(lambda e, f=fn, s=sobj: f(e).then_inc(s, 16))
        self._record(semkey, self.cnt[semkey], reads, writes)
        self.nops += 1

    def cc(self, fn, reads=(), writes=()):
        self._wait("POOL", "CC", self.cnt["CC"])
        self._edges("POOL", reads, writes)
        self.cnt["CC"] += 16
        sobj = self.ccsem
        self.q["POOL"].append(lambda e, f=fn, s=sobj: f(e).then_inc(s, 16))
        self._record("CC", self.cnt["CC"], reads, writes)
        self.nops += 1

    def barrier(self):
        semkeys = list(self.sem) + list(self.dsem) + ["CC"]
        for consumer in ENGS:
            for sk in semkeys:
                if self.cnt.get(sk, 0) > 0:
                    self._wait(consumer, sk, self.cnt[sk])

    def finish(self, keys):
        self._edges("SP", keys, ())

    def emit(self, block):
        qs = self.q

        @block.tensor
        def _(e):
            for f in qs["PE"]:
                f(e)

        @block.vector
        def _(e):
            for f in qs["DVE"]:
                f(e)

        @block.scalar
        def _(e):
            for f in qs["ACT"]:
                f(e)

        @block.gpsimd
        def _(e):
            for f in qs["POOL"]:
                f(e)

        @block.sync
        def _(e):
            for f in qs["SP"]:
                f(e)


def bc_in(ap, n):
    return ap.unsqueeze(2).to_broadcast((ap.shape[0], ap.shape[1], n))


class Builder:
    def __init__(self, cfg: Cfg):
        self.cfg = cfg
        self.nc = bass.Bass("TRN2", target_bir_lowering=False, debug=False)
        self.declare_io()
        self.declare_sbuf()
        self.sc = Sched(self.nc)

    def declare_io(self):
        nc, cfg = self.nc, self.cfg
        L, NL = cfg.L, cfg.nlayer

        def D(name, shape, dt, kind=None):
            if kind:
                return nc.dram_tensor(name, shape, dt, kind=kind).ap()
            return nc.dram_tensor(name, shape, dt).ap()

        self.x_fm = D("x_fm", [D_MODEL, L], F32, "ExternalInput")
        self.wi = D("wi", [NL, D_MODEL, D_IN], F32, "ExternalInput")
        self.wo = D("wo", [NL, D_INNER, D_MODEL], F32, "ExternalInput")
        self.dcwL = D("dcwL", [NL, NXC * DCONV, 128, 128], F32, "ExternalInput")
        self.cbL = D("cbL", [NL, 128, NXC], F32, "ExternalInput")
        self.dtbL = D("dtbL", [NL, NH, 1], F32, "ExternalInput")
        self.alogL = D("alogL", [NL, NH, 1], F32, "ExternalInput")
        self.dL = D("dL", [NL, NH, 1], F32, "ExternalInput")
        self.gnwR = D("gnwR", [NL, 128, NZC], F32, "ExternalInput")
        self.bnwR = D("bnwR", [NL, 128, NDC], F32, "ExternalInput")
        self.fnwR = D("fnwR", [128, NDC], F32, "ExternalInput")
        self.w_gate = D("w_gate", [FFN_I, D_MODEL], F32, "ExternalInput")
        self.w_up = D("w_up", [FFN_I, D_MODEL], F32, "ExternalInput")
        self.w_down = D("w_down", [D_MODEL, FFN_I], F32, "ExternalInput")
        self.c_e64x = D("c_e64x", [NH, 2048], BF16, "ExternalInput")
        self.c_e64xb = D("c_e64xb", [NH, 2048], BF16, "ExternalInput")
        self.c_trid = D("c_trid", [32, 2048], BF16, "ExternalInput")
        self.c_tri4 = D("c_tri4", [4, 32, 128], BF16, "ExternalInput")
        self.c_esel = D("c_esel", [NH, D_INNER], BF16, "ExternalInput")
        self.c_idb = D("c_idb", [128, 128], BF16, "ExternalInput")
        self.c_ones2 = D("c_ones2", [2, 128], BF16, "ExternalInput")
        self.c_ones64 = D("c_ones64", [NH, 128], BF16, "ExternalInput")
        self.c_onescol = D("c_onescol", [128, 1], BF16, "ExternalInput")
        self.out_fm = D("out_fm", [D_MODEL, L], F32, "ExternalOutput")
        st = cfg.stage
        if st in ("s1",):
            self.y_out = D("y_out", [D_MODEL, L], BF16, "ExternalOutput")
        if st in ("s1", "s2", "s3", "s4"):
            self.mx_out = D("mx_out", [1, 6], F32, "ExternalOutput")
        if st in ("s2", "s3", "s4", "s5"):
            self.mxcat = D("mxcat", [6, cfg.n_cores], F32, "ExternalInput")
        if st == "s2":
            self.y_in = D("y_in", [D_MODEL, L], BF16, "ExternalInput")
            self.upraw_out = D("upraw_out", [FFN_I, L], BF16, "ExternalOutput")
            self.gateraw_out = D("gateraw_out", [FFN_I, L], BF16, "ExternalOutput")
        if st == "s3":
            self.gateraw_in = D("gateraw_in", [FFN_I, L], BF16, "ExternalInput")
            self.t4_out = D("t4_out", [FFN_I, L], BF16, "ExternalOutput")
        if st == "s4":
            self.t4_in = D("t4_in", [FFN_I, L], BF16, "ExternalInput")
            self.upraw_in = D("upraw_in", [FFN_I, L], BF16, "ExternalInput")
            self.t5_out = D("t5_out", [FFN_I, L], BF16, "ExternalOutput")
        if st == "s5":
            self.t5_in = D("t5_in", [FFN_I, L], BF16, "ExternalInput")
            self.x1_in = D("x1_in", [D_MODEL, L], F32, "ExternalInput")
        self.gmax_d = D("gmax_d", [1, 6], F32)
        self.mxr_d = D("mxr_d", [6, cfg.n_cores], F32)
        self.wi_bf = D("wi_bf", [NL, D_MODEL, D_IN], BF16)
        self.wo_bf = D("wo_bf", [NL, D_INNER, D_MODEL], BF16)
        self.dcw_bf = D("dcw_bf", [NL, NXC * DCONV, 128, 128], BF16)
        self.xs_b = [D(f"xs_b{i}", [D_INNER, 128], BF16) for i in range(2)]
        self.lt_b = [D(f"lt_b{i}", [2, 128, 16], BF16) for i in range(2)]
        self.red_b = D("red_b", [128, 1], F32)
        self.cc_in = D("cc_in", [1, 4], F32)
        self.cc_out = D("cc_out", [1, 4], F32)
        if cfg.do_ffn:
            self.wg_dq = D("wg_dq", [FFN_I, D_MODEL], BF16)
            self.wu_dq = D("wu_dq", [FFN_I, D_MODEL], BF16)
            self.wd_dq = D("wd_dq", [D_MODEL, FFN_I], BF16)
            self.up_d = D("up_d", [FFN_I, L], BF16)
            self.gate_d = D("gate_d", [FFN_I, L], BF16)

    def declare_sbuf(self):
        nc, cfg = self.nc, self.cfg
        L, T = cfg.L, cfg.TSEG
        self._off = 16640

        def A(name, shape, dtype, at=None):
            if at is None:
                nbytes = shape[1] * mybir.dt.size(dtype)
                t = nc.alloc_sbuf_tensor_at(name, shape, dtype, offset=self._off)
                self._off += (nbytes + 31) // 32 * 32
                return t
            return nc.alloc_sbuf_tensor_at(name, shape, dtype, offset=at)

        self.h = A("h", [128, NDC * L], F32)
        self.sE64X = [A(f"sE64X{i}", [98, 2048], BF16) for i in range(2)]
        self.sESEL = A("sESEL", [NH, D_INNER], BF16)
        self.sIDB = A("sIDB", [128, 128], BF16)
        self.sONES2 = A("sONES2", [2, 128], BF16)
        self.sONES64 = A("sONES64", [NH, 128], BF16)
        self.sONESC = A("sONESC", [128, 1], BF16)
        self.sCB = A("sCB", [128, NXC], F32)
        self.sGNWC = A("sGNWC", [128, NZC], F32)
        self.sBNWC = A("sBNWC", [128, NDC], F32)
        self.sFNWC = A("sFNWC", [128, NDC], F32)
        self.sDTB = A("sDTB", [NH, 1], F32)
        self.sALOG = A("sALOG", [NH, 1], F32)
        self.sACOL = A("sACOL", [NH, 1], F32)
        self.sDCOL = A("sDCOL", [NH, 1], F32)
        self.sDBF = A("sDBF", [NH, 1], BF16)
        self.sDEXP = A("sDEXP", [128, NZC], F32)
        self.sCARRY = A("sCARRY", [128, NXC * (DCONV - 1)], BF16)
        self.sZERO64 = A("sZERO64", [NH, 128], F32)
        self._sEPS1 = A("sEPS1", [128, 2], F32)
        self.sEPS = self._sEPS1.ap()[:, 0:1]
        self.sONE1 = self._sEPS1.ap()[:, 1:2]
        self.sEVAC = A("sEVAC", [128, 516], F32)
        self.sXB = A("sXB", [128, 512], BF16)
        self.sXB2 = A("sXB2", [128, 512], BF16)
        self.sROW = A("sROW", [1, 512], F32)
        self.sROW2 = A("sROW2", [1, 512], F32)
        self.sROW3 = A("sROW3", [1, 512], F32)
        self.sRH = A("sRH", [1, 512], BF16)
        self.sRL = A("sRL", [1, 512], BF16)
        self.sRED = A("sRED", [1, 128], F32)
        self.sSCR = A("sSCR", [1, 16], F32)
        self.sWIB = [A(f"sWIB{i}", [128, 2048], BF16) for i in range(2)]
        self.sWOB = self.sWIB
        self.sXTILE = A("sXTILE", [128, 512], F32)
        self.sXBCB = A("sXBCB", [128, 515], BF16)
        self.sCACC = A("sCACC", [128, 512], F32)
        self.sCH1 = self.sCACC.ap()[:, 0:256].bitcast(BF16)
        self.sCH2 = self.sCACC.ap()[:, 256:512].bitcast(BF16)
        self.sMXC = A("sMXC", [128, 8], F32)
        self.sSC = A("sSC", [128, 8], F32)
        seg_base = self._off
        self.sS = A("sS", [128, D_INNER], F32)
        self.sSBF = A("sSBF", [128, D_INNER], BF16)
        self.sHN = A("sHN", [128, NDC * T], BF16)
        self.sZ = A("sZ", [128, NZC * T], BF16)
        self.sCO = A("sCO", [128, NXC * T], BF16)
        self.sY = A("sY", [128, NZC * T], BF16)
        self.sDT = A("sDT", [NH, T], F32)
        self.sAA = A("sAA", [NH, T], F32)
        self.sL = [A(f"sL{i}", [NH, 128], F32) for i in range(2)]
        self.sHL = [A(f"sHL{i}", [128, 128], BF16) for i in range(2)]
        self.sMLH = [A(f"sMLH{i}", [98, 128], BF16) for i in range(16)]
        self.sLEND = [A(f"sLEND{i}", [NH, 1], F32) for i in range(2)]
        self.sGCOL = [A(f"sGCOL{i}", [NH, 1], F32) for i in range(2)]
        self.sEV2 = A("sEV2", [NH, 128], F32)
        self.sDV = A("sDV", [128, 128], BF16)
        self.sDVT = A("sDVT", [128, 128], BF16)
        self.sEXPL = [A(f"sEXPL{i}", [NH, 128], BF16) for i in range(2)]
        self.sGAM = [A(f"sGAM{i}", [NH, 128], BF16) for i in range(2)]
        self.sG = [A(f"sG{i}", [128, 128], BF16) for i in range(2)]
        self.sBT = [A(f"sBT{i}", [128, 128], BF16) for i in range(2)]
        self.sLTB = [A(f"sLTB{i}", [128, 128], BF16) for i in range(2)]
        xt_base = self._off
        self.sXT = [A(f"sXT{i}", [128, D_INNER], BF16) for i in range(2)]
        self.sWXT = A("sWXT", [128, D_INNER], BF16)
        self.sMG2 = [A(f"sMG{i}", [128, 16 * 128], BF16) for i in range(2)]
        self.sDCW = A("sDCW", [128, NXC * DCONV * 128], BF16, at=xt_base)
        seg_top = self._off
        if cfg.do_ffn:
            off = seg_base
            def F(name, shape, dtype):
                nonlocal off
                t = nc.alloc_sbuf_tensor_at(name, shape, dtype, offset=off)
                off += (shape[1] * mybir.dt.size(dtype) + 31) // 32 * 32
                return t
            self.sFW = [F(f"sFW{i}", [128, 4096], BF16) for i in range(2)]
            self.sBLK = F("sBLK", [128, 96], F32)
            self.sRS = F("sRS", [128, 96], F32)
            self.sDQ1 = F("sDQ1", [128, 3072], F32)
            self.sO = F("sO", [128, 24 * 512], BF16)
            self.sT1 = F("sT1", [128, 512], BF16)
            self.sT2 = F("sT2", [128, 512], BF16)
            self.sT3 = F("sT3", [128, 512], BF16)
            assert off <= seg_top, (off, seg_top)

        P = nc.alloc_psum_tensor
        self.pPS = P("pPS", [128, 4096], F32)

    def pe_transpose_bf(self, src_ap, dst_sbuf_ap, rkeys, wkeys):
        sc = self.sc
        n_p, n_f = src_ap.shape[0], src_ap.shape[1]
        sc.ops("PE", [lambda e: e.transpose(out=self.pPS.ap()[:, 3968:4096].bitcast(BF16)[0:n_f, 0:n_p], in_=src_ap,
                                            identity=self.sIDB[0:n_p, 0:n_p])],
               reads=list(rkeys) + ["cIDB"], writes=["P7"])
        sc.op("DVE", lambda e: e.tensor_copy(out=dst_sbuf_ap, in_=self.pPS.ap()[:, 3968:4096].bitcast(BF16)[0:n_f, 0:n_p]),
              reads=["P7"], writes=list(wkeys))

    def row_hilo(self, row_ap, T, rkey):
        sc = self.sc
        sc.op("DVE", lambda e: e.tensor_copy(out=self.sRH[0:1, 0:T], in_=row_ap),
              reads=[rkey], writes=["rhl"])
        sc.op("DVE", lambda e: e.tensor_tensor(out=self.sRL[0:1, 0:T], in0=row_ap,
                                               in1=self.sRH[0:1, 0:T], op=ALU.subtract),
              reads=[rkey, "rhl"], writes=["rhl"])

    def bcast_row(self, lhsT_ap, lkeys, T, psum_ap, pkey):
        sc = self.sc
        sc.ops("PE", [lambda e: e.matmul(out=psum_ap, lhsT=lhsT_ap,
                                         rhs=self.sRH[0:1, 0:T], start=True, stop=False),
                      lambda e: e.matmul(out=psum_ap, lhsT=lhsT_ap,
                                         rhs=self.sRL[0:1, 0:T], start=False, stop=True)],
               reads=list(lkeys) + ["rhl"], writes=[pkey])

    def load_consts(self):
        sc = self.sc
        pairs = [(self.sESEL, self.c_esel, "cESEL"),
                 (self.sIDB, self.c_idb, "cIDB"), (self.sONES2, self.c_ones2, "cONES2"),
                 (self.sONES64, self.c_ones64, "cONES64"),
                 (self.sONESC, self.c_onescol, "cONESC")]
        for dst, src, key in pairs:
            sc.dma("SP", lambda e, d=dst, s=src: e.dma_start(out=d[:], in_=s), writes=[key])
        for i, csrc in ((0, self.c_e64x), (1, self.c_e64xb)):
            sc.dma("SP", lambda e, i=i, csrc=csrc: e.dma_start(
                out=self.sE64X[i].ap()[0:64, :], in_=csrc[:]), writes=["cE64X"])
            sc.dma("SP", lambda e, i=i: e.dma_start(
                out=self.sE64X[i].ap()[66:98, :], in_=self.c_trid[:]), writes=["cE64X"])
        for i in range(16):
            sl = i % 4
            sc.dma("SP", lambda e, i=i, sl=sl: e.dma_start(
                out=self.sMLH[i].ap()[66:98, :], in_=self.c_tri4[sl]), writes=["cMLH"])
            sc.op("DVE", lambda e, i=i: e.memset(self.sMLH[i].ap()[64:66, :], 1.0),
                  writes=["cMLH"])
        sc.dma("POOL", lambda e: e.dma_start(out=self.sFNWC[:], in_=self.fnwR[:]),
               writes=["fnwb"])
        sc.op("DVE", lambda e: e.memset(self.sZERO64[:], 0.0), writes=["zero64"])
        sc.op("DVE", lambda e: e.memset(self.sEPS[:], LN_EPS), writes=["ceps"])
        sc.op("DVE", lambda e: e.memset(self.sONE1[:], 1.0), writes=["cone"])
        sc.op("DVE", lambda e: e.memset(self.sCARRY[:], 0.0), writes=["carry"])
        sc.op("DVE", lambda e: e.memset(self.sMXC[:], 0.0), writes=["mxc"])
        sc.op("DVE", lambda e: e.memset(self.sSCR[:], 0.0), writes=["scr"])
        sc.dma("ACT", lambda e: e.dma_start(out=self.cc_in, in_=self.sSCR[0:1, 0:4]),
               reads=["scr"], writes=["ccin"])

    def preconvert_weights(self):
        sc = self.sc
        for l in range(self.cfg.nlayer):
            sc.dma("POOL", lambda e, l=l: e.dma_start(out=self.wi_bf[l], in_=self.wi[l]),
                   writes=[f"wi_bf:{l}"])
            sc.dma("POOL", lambda e, l=l: e.dma_start(out=self.wo_bf[l], in_=self.wo[l]),
                   writes=[f"wo_bf:{l}"])
            sc.dma("POOL", lambda e, l=l: e.dma_start(out=self.dcw_bf[l], in_=self.dcwL[l]),
                   writes=[f"dcw_bf:{l}"])

    def load_layer_params(self, l):
        sc = self.sc
        for dst, src, key in [(self.sCB, self.cbL[l], "cb"),
                              (self.sDTB, self.dtbL[l], "dtb"), (self.sALOG, self.alogL[l], "alog"),
                              (self.sDCOL, self.dL[l], "dcol")]:
            sc.dma("SP", lambda e, d=dst, s=src: e.dma_start(out=d[:], in_=s), writes=[key])
        sc.dma("POOL", lambda e, l=l: e.dma_start(out=self.sGNWC[:], in_=self.gnwR[l]),
               writes=["gnwb"])
        sc.dma("POOL", lambda e, l=l: e.dma_start(out=self.sBNWC[:], in_=self.bnwR[l]),
               writes=["bnwb"])
        sc.op("ACT", lambda e: e.activation(out=self.sACOL[:], in_=self.sALOG[:], func=AF.Exp),
              reads=["alog"], writes=["acol"])
        sc.op("DVE", lambda e: e.tensor_scalar(out=self.sACOL[:], in0=self.sACOL[:],
                                               scalar1=-1.0, scalar2=None, op0=ALU.mult),
              reads=["acol"], writes=["acol"])
        sc.op("DVE", lambda e: e.tensor_copy(out=self.sDBF[:], in_=self.sDCOL[:]),
              reads=["dcol"], writes=["dbf"])
        sc.ops("PE", [(lambda e, k=k: e.matmul(out=self.pPS[:, 3072 + k:3072 + k + 1],
                                               lhsT=self.sESEL[:, 128 * k:128 * (k + 1)],
                                               rhs=self.sDBF[:], start=True, stop=True))
                      for k in range(NZC)],
               reads=["dbf", "cESEL"], writes=["P6"])
        sc.op("DVE", lambda e: e.tensor_copy(out=self.sDEXP[:], in_=self.pPS[:, 3072 + 0:3072 + NZC]),
              reads=["P6"], writes=["dexp"])
        sc.op("DVE", lambda e: e.memset(self.sS[:], 0.0), writes=["S"])
        sc.op("DVE", lambda e: e.memset(self.sCARRY[:], 0.0), writes=["carry"])

    def rms_rows(self, T, get_tile, rkeys, nchunks, mean_div, prep=None):
        sc = self.sc
        for k in range(nchunks):
            if prep is not None:
                prep(k)
            ap = get_tile(k)
            sc.op("ACT", lambda e, ap=ap: e.activation(out=self.sXB[:, 0:T], in_=ap,
                                                       func=AF.Square),
                  reads=rkeys(k), writes=["sXB"])
            sc.ops("PE", [lambda e, k=k: e.matmul(out=self.pPS[0:1, 3072 + 0:3072 + T],
                                                  lhsT=self.sONESC[:], rhs=self.sXB[:, 0:T],
                                                  start=(k == 0), stop=(k == nchunks - 1))],
                   reads=["sXB", "cONESC"], writes=["P6"])
        sc.op("ACT", lambda e: e.activation(out=self.sROW2[0:1, 0:T], in_=self.pPS[0:1, 3072 + 0:3072 + T],
                                            func=AF.Ln, scale=1.0 / mean_div,
                                            bias=self.sEPS[0:1]),
              reads=["P6", "ceps"], writes=["rowB0"])
        sc.op("ACT", lambda e: e.activation(out=self.sROW[0:1, 0:T], in_=self.sROW2[0:1, 0:T],
                                            func=AF.Exp, scale=-0.5),
              reads=["rowB0"], writes=["rowA0"])

    def ln_rows(self, T, get_tile, rkeys, prep=None):
        sc = self.sc
        for k in range(NDC):
            if prep is not None:
                prep(k)
            ap = get_tile(k)
            sc.op("DVE", lambda e, ap=ap: e.tensor_copy(out=self.sXB2[:, 0:T], in_=ap),
                  reads=rkeys(k), writes=["sXB2"])
            sc.op("ACT", lambda e, ap=ap: e.activation(out=self.sXB[:, 0:T], in_=ap,
                                                       func=AF.Square),
                  reads=rkeys(k), writes=["sXB"])
            sc.ops("PE", [lambda e, k=k: e.matmul(out=self.pPS[0:1, 3072 + 0:3072 + T], lhsT=self.sONESC[:],
                                                  rhs=self.sXB2[:, 0:T], start=(k == 0),
                                                  stop=(k == NDC - 1))],
                   reads=["sXB2", "cONESC"], writes=["P6"])
            sc.ops("PE", [lambda e, k=k: e.matmul(out=self.pPS[0:1, 3584:3584 + T],
                                                  lhsT=self.sONESC[:],
                                                  rhs=self.sXB[:, 0:T], start=(k == 0),
                                                  stop=(k == NDC - 1))],
                   reads=["sXB", "cONESC"], writes=["P7"])
        sc.op("DVE", lambda e: e.tensor_scalar(out=self.sROW3[0:1, 0:T],
                                               in0=self.pPS[0:1, 3072 + 0:3072 + T],
                                               scalar1=1.0 / D_MODEL, scalar2=None, op0=ALU.mult),
              reads=["P6"], writes=["rowA1"])
        sc.op("DVE", lambda e: e.tensor_scalar(out=self.sROW2[0:1, 0:T],
                                               in0=self.pPS[0:1, 3584:3584 + T],
                                               scalar1=1.0 / D_MODEL, scalar2=None, op0=ALU.mult),
              reads=["P7"], writes=["rowB0"])
        sc.op("DVE", lambda e: e.tensor_tensor(out=self.sROW[0:1, 0:T],
                                               in0=self.sROW3[0:1, 0:T],
                                               in1=self.sROW3[0:1, 0:T], op=ALU.mult),
              reads=["rowA1"], writes=["rowA0"])
        sc.op("DVE", lambda e: e.tensor_tensor(out=self.sROW2[0:1, 0:T],
                                               in0=self.sROW2[0:1, 0:T],
                                               in1=self.sROW[0:1, 0:T], op=ALU.subtract),
              reads=["rowB0", "rowA0"], writes=["rowB0"])
        sc.op("ACT", lambda e: e.activation(out=self.sROW2[0:1, 0:T], in_=self.sROW2[0:1, 0:T],
                                            func=AF.Ln, bias=self.sEPS[0:1]),
              reads=["rowB0", "ceps"], writes=["rowB0"])
        sc.op("ACT", lambda e: e.activation(out=self.sROW[0:1, 0:T], in_=self.sROW2[0:1, 0:T],
                                            func=AF.Exp, scale=-0.5),
              reads=["rowB0"], writes=["rowA0"])
        sc.op("DVE", lambda e: e.tensor_tensor(out=self.sROW2[0:1, 0:T],
                                               in0=self.sROW3[0:1, 0:T],
                                               in1=self.sROW[0:1, 0:T], op=ALU.mult),
              reads=["rowA1", "rowA0"], writes=["rowB0"])

    def ln_bcasts(self, T):
        self.row_hilo(self.sROW[0:1, 0:T], T, "rowA0")
        self.bcast_row(self.sONES2[0:1, :], ["cONES2"], T, self.pPS[:, 1024 + 0:1024 + T], "P2")
        self.row_hilo(self.sROW2[0:1, 0:T], T, "rowB0")
        self.bcast_row(self.sONES2[0:1, :], ["cONES2"], T, self.pPS[:, 1024 + 512:1024 + 512 + T], "P3")

    def phase_init_ln(self):
        cfg, sc = self.cfg, self.sc
        L = cfg.L
        for j in range(L // 512):
            t0 = 512 * j

            def prep(k, t0=t0):
                sc.dma("SP", lambda e, k=k: e.dma_start(
                    out=self.sXTILE[:],
                    in_=self.x_fm[128 * k:128 * (k + 1), t0:t0 + 512]),
                    writes=["xt"])
            self.ln_rows(512, lambda k: self.sXTILE[:], lambda k: ["xt"], prep=prep)
            self.ln_bcasts(512)
            for k in range(NDC):
                prep(k)
                hsl = self.h[:, k * L + t0: k * L + t0 + 512]
                sc.op("DVE", lambda e: e.tensor_tensor(
                    out=self.sEVAC[:, 0:512], in0=self.sXTILE[:],
                    in1=self.pPS[:, 1024 + 0:1024 + 512], op=ALU.mult),
                    reads=["xt", "P2"], writes=["evac"])
                sc.op("DVE", lambda e, hsl=hsl: e.tensor_tensor(
                    out=hsl, in0=self.sEVAC[:, 0:512], in1=self.pPS[:, 1024 + 512:1024 + 1024],
                    op=ALU.subtract),
                    reads=["evac", "P3"], writes=[f"h:{k}"])

    def phase_A(self, l, seg):
        cfg, sc = self.cfg, self.sc
        T, L = cfg.TSEG, cfg.L
        t0 = seg * T
        hsl = lambda k: self.h[:, k * L + t0: k * L + t0 + T]
        self.rms_rows(T, hsl, lambda k: [f"h:{k}"], NDC, D_MODEL)
        self.row_hilo(self.sROW[0:1, 0:T], T, "rowA0")
        self.bcast_row(self.sONES2[0:1, :], ["cONES2"], T,
                       self.pPS[:, 1024 + 0:1024 + T], "P2")
        for k in range(NDC):
            hap = hsl(k)
            sc.op("DVE", lambda e, k=k, hap=hap: e.scalar_tensor_tensor(
                out=self.sHN[:, k * T:(k + 1) * T], in0=hap,
                scalar=self.sBNWC[:, k:k + 1], in1=self.pPS[:, 1024 + 0:1024 + T],
                op0=ALU.mult, op1=ALU.mult),
                reads=[f"h:{k}", "P2", "bnwb"], writes=[f"hn:{k}"])
        hn_keys = [f"hn:{k}" for k in range(NDC)]
        sc.dma("SP", lambda e, l=l: e.dma_start(
            out=self.sDCW.ap().rearrange("p (m n) -> p m n", m=NXC * DCONV),
            in_=self.dcw_bf[l].rearrange("m p n -> p m n")),
            reads=[f"dcw_bf:{l}", "xst:0", "xst:1", "dxt", "wxt", "mg:0", "mg:1"],
            writes=["dcw"])
        for np_ in range(18):
            buf = np_ % 2
            wkey = f"wib:{buf}"
            w = 256 if np_ < 17 else 64
            nof = 256 * np_
            sc.dma("SP", lambda e, buf=buf, w=w, nof=nof, l=l: e.dma_start(
                out=self.sWIB[buf].ap()[:, 0:NDC * w].rearrange(
                    "p (k n) -> p k n", k=NDC),
                in_=self.wi_bf[l][:, nof:nof + w].rearrange("(k p) n -> p k n", p=128)),
                reads=[f"wi_bf:{l}"], writes=[wkey])
            for sub in range(2 if np_ < 17 else 1):
                n = 2 * np_ + sub
                ncols = 128 if n < 34 else 64
                pb = n % 2
                pk = f"P{pb}"
                psl = self.pPS[:, 512 * pb:512 * pb + T]
                sc.ops("PE", [(lambda e, k=k, buf=buf, w=w, sub=sub, ncols=ncols, psl=psl:
                               e.matmul(out=psl[0:ncols, :] if ncols < 128 else psl,
                                        lhsT=self.sWIB[buf][:, w * k + 128 * sub:
                                                            w * k + 128 * sub + ncols],
                                        rhs=self.sHN[:, k * T:(k + 1) * T],
                                        start=(k == 0), stop=(k == NDC - 1)))
                              for k in range(NDC)],
                       reads=[wkey] + hn_keys, writes=[pk])
                if n < NZC:
                    sc.op("ACT", lambda e, n=n, psl=psl: e.activation(
                        out=self.sZ[:, n * T:(n + 1) * T], in_=psl, func=AF.Silu),
                        reads=[pk], writes=[f"z:{n}"])
                elif n < 34:
                    self.conv_chunk(n - NZC, psl, pk)
                else:
                    self.dt_chunk(psl, pk)
        sc.op("DVE", lambda e: e.tensor_scalar(out=self.sAA[:, 0:T], in0=self.sDT[:, 0:T],
                                               scalar1=self.sACOL[:], scalar2=None,
                                               op0=ALU.mult),
              reads=["dt", "acol"], writes=["aa"])

    def conv_chunk(self, j, psl, pk):
        sc, T = self.sc, self.cfg.TSEG
        xb = self.sXBCB
        sc.op("DVE", lambda e, j=j: e.tensor_copy(out=xb[:, 0:3],
                                                  in_=self.sCARRY[:, 3 * j:3 * j + 3]),
              reads=["carry"], writes=["xbcb"])
        sc.op("DVE", lambda e: e.tensor_copy(out=xb[:, 3:3 + T], in_=psl),
              reads=[pk], writes=["xbcb"])
        sc.op("DVE", lambda e, j=j: e.tensor_copy(out=self.sCARRY[:, 3 * j:3 * j + 3],
                                                  in_=xb[:, T:T + 3]),
              reads=["xbcb"], writes=["carry"])
        cb_bank = 2 + (j % 2)
        cps = self.pPS[:, 512 * cb_bank:512 * cb_bank + T]

        def mk(e, j=j, cps=cps):
            r = None
            for kk in range(DCONV):
                r = e.matmul(out=cps, lhsT=self.sDCW[:, 128 * (4 * j + kk):
                                                     128 * (4 * j + kk) + 128],
                             rhs=xb[:, kk:kk + T],
                             start=(kk == 0), stop=(kk == DCONV - 1))
            return r
        sc.ops("PE", [mk], reads=["dcw", "xbcb"], writes=[f"P{cb_bank}"])
        sc.op("ACT", lambda e, j=j, cps=cps: e.activation(
            out=self.sCO[:, j * T:(j + 1) * T], in_=cps, func=AF.Silu,
            bias=self.sCB[:, j:j + 1]),
            reads=[f"P{cb_bank}", "cb"], writes=[f"co:{j}"])

    def dt_chunk(self, psl, pk):
        sc, T = self.sc, self.cfg.TSEG
        t1 = self.sEVAC[0:64, 0:T]
        t4 = self.sCACC[0:64, 0:T]
        ps = psl[0:64, :]
        sc.op("ACT", lambda e: e.activation(out=t1, in_=ps, func=AF.Abs, bias=self.sDTB[:]),
              reads=[pk, "dtb"], writes=["evac"])
        sc.op("ACT", lambda e: e.activation(out=t1, in_=t1, func=AF.Exp, scale=-1.0),
              reads=["evac"], writes=["evac"])
        sc.op("ACT", lambda e: e.activation(out=t1, in_=t1, func=AF.Ln,
                                            bias=self.sONE1[0:64]),
              reads=["evac", "cone"], writes=["evac"])
        sc.op("ACT", lambda e: e.activation(out=t4, in_=ps, func=AF.Relu, bias=self.sDTB[:]),
              reads=[pk, "dtb", "cacc"], writes=["cacc"])
        sc.op("DVE", lambda e: e.tensor_tensor(out=self.sDT[:, 0:T], in0=t1, in1=t4,
                                               op=ALU.add),
              reads=["evac", "cacc"], writes=["dt"])

    def phase_B_prep_front(self, l, seg, c):
        cfg, sc = self.cfg, self.sc
        T, C = cfg.TSEG, cfg.C
        cof = c * C
        cbuf = c % 2
        sL, sHL = self.sL[cbuf], self.sHL[cbuf]
        kL, kHL = f"Lc:{cbuf}", f"hl:{cbuf}"
        kEND, kGC = f"Lend:{cbuf}", f"gcol:{cbuf}"
        sc.op("DVE", lambda e, cof=cof, sL=sL: e.tensor_tensor_scan(
            out=sL[:], data0=self.sAA[:, cof:cof + C], data1=self.sZERO64[:],
            initial=0.0, op0=ALU.add, op1=ALU.add),
            reads=["aa", "zero64"], writes=[kL])
        sc.op("DVE", lambda e, sL=sL, sHL=sHL: e.tensor_copy(
            out=sHL[0:64, :], in_=sL[:]), reads=[kL], writes=[kHL])
        sc.op("DVE", lambda e, sL=sL, sHL=sHL: e.tensor_tensor(
            out=sHL[64:128, :], in0=sL[:], in1=sHL[0:64, :], op=ALU.subtract),
            reads=[kL, kHL], writes=[kHL])
        for i in range(8):
            h2 = i // 4
            idx = 8 * cbuf + i
            sc.dma("SP", lambda e, idx=idx, h2=h2, sHL=sHL: e.dma_start(
                out=self.sMLH[idx].ap()[0:32, :], in_=sHL[32 * h2:32 * h2 + 32, :]),
                reads=[kHL], writes=[f"mlh:{idx}"])
            sc.dma("SP", lambda e, idx=idx, h2=h2, sHL=sHL: e.dma_start(
                out=self.sMLH[idx].ap()[32:64, :], in_=sHL[64 + 32 * h2:64 + 32 * h2 + 32, :]),
                reads=[kHL], writes=[f"mlh:{idx}"])
        sc.op("DVE", lambda e, sL=sL, cbuf=cbuf: e.tensor_copy(
            out=self.sLEND[cbuf][:], in_=sL[:, C - 1:C]), reads=[kL], writes=[kEND])
        sc.op("ACT", lambda e, cbuf=cbuf: e.activation(
            out=self.sGCOL[cbuf][:], in_=self.sLEND[cbuf][:], func=AF.Exp),
            reads=[kEND], writes=[kGC])
        sc.op("DVE", lambda e, cbuf=cbuf: e.tensor_scalar(
            out=self.sGAM[cbuf][:], in0=self.sONES64[:],
            scalar1=self.sGCOL[cbuf][:], scalar2=None, op0=ALU.mult),
            reads=["cONES64", kGC], writes=[f"gam:{cbuf}"])
        sc.op("ACT", lambda e, sL=sL, cbuf=cbuf: e.activation(
            out=self.sEXPL[cbuf][:], in_=sL[:], func=AF.Exp),
            reads=[kL], writes=[f"expl:{cbuf}"])
        sc.op("ACT", lambda e, sL=sL, cbuf=cbuf: e.activation(
            out=self.sEV2[:], in_=sL[:], func=AF.Exp,
            scale=-1.0, bias=self.sLEND[cbuf][:]),
            reads=[kL, kEND], writes=["ev2"])
        sc.op("DVE", lambda e, cof=cof: e.tensor_copy(
            out=self.sDV.ap()[0:64, :], in_=self.sDT[:, cof:cof + C]),
            reads=["dt"], writes=["dv"])
        sc.op("DVE", lambda e, cof=cof: e.tensor_tensor(
            out=self.sDV.ap()[64:128, :], in0=self.sEV2[:],
            in1=self.sDT[:, cof:cof + C], op=ALU.mult),
            reads=["ev2", "dt"], writes=["dv"])
        co3 = self.sCO.ap().rearrange("p (j t) -> p j t", j=NXC)
        xs_src = co3[:, 0:16, cof:cof + C]
        xbounce = self.xs_b[cbuf]
        co_keys = [f"co:{j}" for j in range(16)]
        sc.dma("ACT", lambda e, xs_src=xs_src, xbounce=xbounce: e.dma_start(
            out=xbounce.rearrange("(j p) t -> p j t", p=128), in_=xs_src),
            reads=co_keys, writes=[f"xsb:{cbuf}"])
        sc.dma("ACT", lambda e, cbuf=cbuf: e.dma_start_transpose(
            out=self.sXT[cbuf][:], in_=self.xs_b[cbuf]),
            reads=[f"xsb:{cbuf}"], writes=[f"xst:{cbuf}"])

    def phase_B_prep_back(self, l, seg, c):
        cfg, sc = self.cfg, self.sc
        T, C = cfg.TSEG, cfg.C
        cof = c * C
        cbuf = c % 2
        sHL = self.sHL[cbuf]
        kHL = f"hl:{cbuf}"
        co3 = self.sCO.ap().rearrange("p (j t) -> p j t", j=NXC)
        self.pe_transpose_bf(self.sDV[:], self.sDVT[:], ["dv"], ["dvt"])
        self.pe_transpose_bf(sHL[:], self.sLTB[cbuf][:], [kHL], [f"ltb:{cbuf}"])
        sc.op("DVE", lambda e, cbuf=cbuf: e.tensor_tensor(
            out=self.sWXT.ap().rearrange("p (h q) -> p h q", h=NH),
            in0=self.sXT[cbuf].ap().rearrange("p (h q) -> p h q", h=NH),
            in1=bc_in(self.sDVT.ap()[:, 64:128], HD), op=ALU.mult),
            reads=[f"xst:{cbuf}", "dvt"], writes=["wxt"])
        sc.op("DVE", lambda e, cbuf=cbuf: e.tensor_tensor(
            out=self.sXT[cbuf].ap().rearrange("p (h q) -> p h q", h=NH),
            in0=self.sXT[cbuf].ap().rearrange("p (h q) -> p h q", h=NH),
            in1=bc_in(self.sDVT.ap()[:, 0:64], HD), op=ALU.mult),
            reads=[f"xst:{cbuf}", "dvt", "wxt"], writes=["dxt", f"xst:{cbuf}"])
        bsl = co3[:, 16, cof:cof + C]
        csl = co3[:, 17, cof:cof + C]
        sc.ops("PE", [lambda e, bsl=bsl, csl=csl: e.matmul(
            out=self.pPS[:, 3584 + 0:3584 + 128], lhsT=bsl, rhs=csl, start=True, stop=True)],
            reads=["co:16", "co:17"], writes=["P7"])
        sc.op("DVE", lambda e, cbuf=cbuf: e.tensor_copy(
            out=self.sG[cbuf][:], in_=self.pPS[:, 3584 + 0:3584 + 128]),
            reads=["P7"], writes=[f"G:{cbuf}"])
        self.pe_transpose_bf(bsl, self.sBT[cbuf][:], ["co:16"], [f"BT:{cbuf}"])
        co_keys = [f"co:{j}" for j in range(16)]
        sc.op("DVE", lambda e, cof=cof, co3=co3: e.tensor_tensor(
            out=co3[:, 0:16, cof:cof + C], in0=co3[:, 0:16, cof:cof + C],
            in1=bc_in(self.sDEXP[:, 0:16], C), op=ALU.mult),
            reads=co_keys + ["dexp", f"xsb:{cbuf}"], writes=co_keys)

    def phase_B_compute(self, l, seg, c):
        cfg, sc = self.cfg, self.sc
        sc.op("DVE", lambda e: e.tensor_copy(out=self.sSBF[:], in_=self.sS[:]),
              reads=["S"], writes=["SBF"])
        self.state_update(c)
        if c + 1 < cfg.NCH:
            self.phase_B_prep_front(l, seg, c + 1)
        for hq in range(2):
            self.mask_quarter(c, hq)
            self.y_quarter(c, hq)
        if c + 1 < cfg.NCH:
            self.phase_B_prep_back(l, seg, c + 1)
        for hq in range(2, 4):
            self.mask_quarter(c, hq)
            self.y_quarter(c, hq)

    def mask_quarter(self, c, hq):
        sc = self.sc
        h2, s2 = hq // 2, hq % 2
        par = hq % 2
        cbuf = c % 2
        hgb = (4 * c + hq) % 2
        mgk = f"mg:{par}"
        sMG = self.sMG2[par]
        bb = self.lt_b[hgb]
        sLTB, kLTB = self.sLTB[cbuf], f"ltb:{cbuf}"
        sc.dma("ACT", lambda e, hq=hq, bb=bb, sLTB=sLTB: e.dma_start(
            out=bb[0], in_=sLTB[:, 16 * hq:16 * hq + 16]),
            reads=[kLTB], writes=[f"ltbd:{hgb}"])
        sc.dma("ACT", lambda e, hq=hq, bb=bb, sLTB=sLTB: e.dma_start(
            out=bb[1], in_=sLTB[:, 64 + 16 * hq:64 + 16 * hq + 16]),
            reads=[kLTB], writes=[f"ltbd:{hgb}"])
        sc.dma("ACT", lambda e, bb=bb, s2=s2: e.dma_start(
            out=self.sE64X[s2].ap()[64:66, :], in_=bb.rearrange("r t h -> r (t h)")),
            reads=[f"ltbd:{hgb}"], writes=[f"flat:{s2}"])
        for sl in range(4):
            po = self.pPS[:, 1024 + 512 * (sl % 2):1024 + 512 * (sl % 2) + 512]
            i = 8 * cbuf + 4 * h2 + sl
            sc.ops("PE", [lambda e, po=po, i=i, sl=sl, s2=s2: e.matmul(
                out=po, lhsT=self.sMLH[i][0:98, :],
                rhs=self.sE64X[s2][0:98, 512 * sl:512 * (sl + 1)],
                start=True, stop=True)],
                reads=[f"mlh:{i}", "cMLH", "cE64X", f"flat:{s2}"],
                writes=[f"P{2 + sl % 2}"])
            sc.op("ACT", lambda e, sl=sl, po=po, sMG=sMG: e.activation(
                out=sMG[:, 512 * sl:512 * (sl + 1)], in_=po, func=AF.Exp),
                reads=[f"P{2 + sl % 2}"], writes=[mgk])
            mg3s = sMG.ap()[:, 512 * sl:512 * (sl + 1)].rearrange(
                "p (t h) -> p t h", h=16)
            sc.op("POOL", lambda e, mg3s=mg3s, sl=sl, cbuf=cbuf: e.tensor_tensor(
                out=mg3s, in0=mg3s,
                in1=bc_in(self.sG[cbuf].ap()[:, 32 * sl:32 * (sl + 1)], 16),
                op=ALU.mult),
                reads=[mgk, f"G:{cbuf}"], writes=[mgk])

    def y_quarter(self, c, hq):
        sc, cfg = self.sc, self.cfg
        T, C = cfg.TSEG, cfg.C
        cof = c * C
        cbuf = c % 2
        par = hq % 2
        mgk = f"mg:{par}"
        mg3 = self.sMG2[par].ap().rearrange("p (t h) -> p t h", h=16)
        co3 = self.sCO.ap().rearrange("p (j t) -> p j t", j=NXC)
        z3 = self.sZ.ap().rearrange("p (j t) -> p j t", j=NZC)
        y3 = self.sY.ap().rearrange("p (j t) -> p j t", j=NZC)
        csl = co3[:, 17, cof:cof + C]
        ks = list(range(4 * hq, 4 * hq + 4))
        pI = self.pPS[:, 2048:2560]
        pN = self.pPS[:, 2560:3072]
        pX = self.pPS[:, 3072:3584]
        sc.ops("PE", [(lambda e, k=k, i=i: e.matmul(
            out=pX[:, 128 * i:128 * (i + 1)],
            lhsT=self.sESEL[:, 128 * k:128 * (k + 1)], rhs=self.sEXPL[cbuf][:],
            start=True, stop=True)) for i, k in enumerate(ks)],
            reads=["cESEL", f"expl:{cbuf}"], writes=["P6"])
        sc.ops("PE", [(lambda e, k=k, i=i: e.matmul(
            out=pN[:, 128 * i:128 * (i + 1)],
            lhsT=self.sSBF[:, 128 * k:128 * (k + 1)], rhs=csl,
            start=True, stop=True)) for i, k in enumerate(ks)],
            reads=["SBF", "co:17"], writes=["P5"])

        def intra(e, ks=ks, hq=hq, cbuf=cbuf):
            r = None
            for i, k in enumerate(ks):
                for jj in range(4):
                    h_loc = (4 * k + jj) - 16 * hq
                    r = e.matmul(out=pI[32 * jj:32 * jj + 32, 128 * i:128 * (i + 1)],
                                 lhsT=self.sXT[cbuf][:, 128 * k + 32 * jj:
                                                     128 * k + 32 * jj + 32],
                                 rhs=mg3[:, :, h_loc],
                                 start=True, stop=True, tile_position=(0, 32 * jj))
            return r
        sc.ops("PE", [intra], reads=["dxt", mgk, f"xst:{cbuf}"], writes=["P4"])
        k0 = ks[0]
        xs_sl = co3[:, k0:k0 + 4, cof:cof + C]
        z_sl = z3[:, k0:k0 + 4, cof:cof + C]
        y_sl = y3[:, k0:k0 + 4, cof:cof + C]
        ch13 = self.sCH1.rearrange("p (j t) -> p j t", j=4)
        ch23 = self.sCH2.rearrange("p (j t) -> p j t", j=4)
        xs_keys = [f"co:{k}" for k in ks]
        z_keys = [f"z:{k}" for k in ks]
        y_keys = [f"y:{k}" for k in ks]
        sc.op("ACT", lambda e, pX=pX: e.activation(out=self.sCH1, in_=pX, func=AF.Copy),
              reads=["P6", "cacc"], writes=["ch1", "cacc"])
        sc.op("DVE", lambda e, pN=pN: e.tensor_tensor(out=self.sCH2, in0=pN,
                                                      in1=self.sCH1, op=ALU.mult),
              reads=["P5", "ch1", "cacc"], writes=["ch2", "cacc"])
        sc.op("DVE", lambda e, pI=pI: e.tensor_tensor(out=self.sCH1, in0=pI,
                                                      in1=self.sCH2, op=ALU.add),
              reads=["P4", "ch2", "cacc"], writes=["ch1", "cacc"])
        sc.op("POOL", lambda e, xs_sl=xs_sl, ch13=ch13, ch23=ch23: e.tensor_tensor(
            out=ch23, in0=ch13, in1=xs_sl, op=ALU.add),
            reads=["ch1", "cacc"] + xs_keys, writes=["ch2", "cacc"])
        sc.op("POOL", lambda e, z_sl=z_sl, y_sl=y_sl, ch23=ch23: e.tensor_tensor(
            out=y_sl, in0=ch23, in1=z_sl, op=ALU.mult),
            reads=["ch2", "cacc"] + z_keys, writes=y_keys)

    def state_update(self, c):
        sc = self.sc
        cbuf = c % 2
        for sl in range(4):
            pA = self.pPS[:, 0:512]
            pB = self.pPS[:, 512:1024]
            esl = self.sESEL[:, 512 * sl:512 * (sl + 1)]
            ssl = self.sS[:, 512 * sl:512 * (sl + 1)]
            wsl = self.sWXT[:, 512 * sl:512 * (sl + 1)]
            sc.ops("PE", [lambda e, esl=esl, pA=pA, cbuf=cbuf: e.matmul(
                out=pA, lhsT=self.sGAM[cbuf][:], rhs=esl, start=True, stop=True)],
                reads=[f"gam:{cbuf}", "cESEL"], writes=["P0"])
            sc.ops("PE", [lambda e, wsl=wsl, pB=pB, cbuf=cbuf: e.matmul(
                out=pB, lhsT=self.sBT[cbuf][:], rhs=wsl, start=True, stop=True)],
                reads=[f"BT:{cbuf}", "wxt"], writes=["P1"])
            sc.op("DVE", lambda e, ssl=ssl, pA=pA: e.tensor_tensor(
                out=ssl, in0=pA, in1=ssl, op=ALU.mult),
                reads=["P0", "S"], writes=["S"])
            sc.op("DVE", lambda e, ssl=ssl, pB=pB: e.tensor_tensor(
                out=ssl, in0=ssl, in1=pB, op=ALU.add),
                reads=["P1", "S"], writes=["S"])

    def phase_C(self, l, seg):
        cfg, sc = self.cfg, self.sc
        T, L = cfg.TSEG, cfg.L
        t0 = seg * T
        ysl = lambda k: self.sY[:, k * T:(k + 1) * T]
        self.rms_rows(T, ysl, lambda k: [f"y:{k}"], NZC, D_INNER)
        self.row_hilo(self.sROW[0:1, 0:T], T, "rowA0")
        self.bcast_row(self.sONES2[0:1, :], ["cONES2"], T,
                       self.pPS[:, 3072 + 0:3072 + T], "P6")
        for k in range(NZC):
            sc.op("DVE", lambda e, k=k: e.scalar_tensor_tensor(
                out=self.sY[:, k * T:(k + 1) * T], in0=self.sY[:, k * T:(k + 1) * T],
                scalar=self.sGNWC[:, k:k + 1], in1=self.pPS[:, 3072 + 0:3072 + T],
                op0=ALU.mult, op1=ALU.mult),
                reads=[f"y:{k}", "P6", "gnwb"], writes=[f"y:{k}"])
        y_keys = [f"y:{k}" for k in range(NZC)]
        allp = [f"P{b}" for b in range(8)]
        for f in range(NZC):
            buf = f % 2
            sc.dma("SP", lambda e, f=f, buf=buf, l=l: e.dma_start(
                out=self.sWOB[buf][:, 0:1024], in_=self.wo_bf[l][128 * f:128 * (f + 1), :]),
                reads=[f"wo_bf:{l}"], writes=[f"wib:{buf}"])

            def mk(e, f=f, buf=buf):
                r = None
                for dk in range(NDC):
                    po = self.pPS[:, 512 * dk:512 * dk + 512]
                    r = e.matmul(out=po, lhsT=self.sWOB[buf][:, 128 * dk:128 * (dk + 1)],
                                 rhs=self.sY[:, f * T:f * T + 512],
                                 start=(f == 0), stop=(f == NZC - 1))
                return r
            sc.ops("PE", [mk], reads=[f"wib:{buf}"] + y_keys, writes=allp)
        for dk in range(NDC):
            po = self.pPS[:, 512 * dk:512 * dk + 512]
            hsl = self.h[:, dk * L + t0: dk * L + t0 + 512]
            sc.op("DVE", lambda e, po=po, hsl=hsl: e.tensor_tensor(
                out=hsl, in0=hsl, in1=po, op=ALU.add),
                reads=[f"P{dk}", f"h:{dk}"], writes=[f"h:{dk}"])

    def phase_x1(self):
        cfg, sc = self.cfg, self.sc
        L = cfg.L
        for j in range(L // 512):
            t0 = 512 * j
            hsl = lambda k, t0=t0: self.h[:, k * L + t0: k * L + t0 + 512]
            self.rms_rows(512, hsl, lambda k: [f"h:{k}"], NDC, D_MODEL)
            self.row_hilo(self.sROW[0:1, 0:512], 512, "rowA0")
            self.bcast_row(self.sONES2[0:1, :], ["cONES2"], 512,
                           self.pPS[:, 1024 + 0:1024 + 512], "P2")
            for k in range(NDC):
                sc.dma("SP", lambda e, k=k, t0=t0: e.dma_start(
                    out=self.sXTILE[:],
                    in_=self.x_fm[128 * k:128 * (k + 1), t0:t0 + 512]),
                    writes=["xt"])
                hap = hsl(k)
                sc.op("DVE", lambda e, k=k, hap=hap: e.scalar_tensor_tensor(
                    out=self.sEVAC[:, 0:512], in0=hap,
                    scalar=self.sFNWC[:, k:k + 1], in1=self.pPS[:, 1024 + 0:1024 + 512],
                    op0=ALU.mult, op1=ALU.mult),
                    reads=[f"h:{k}", "P2", "fnwb"], writes=["evac"])
                sc.op("DVE", lambda e, hap=hap: e.tensor_tensor(
                    out=hap, in0=self.sEVAC[:, 0:512],
                    in1=self.sXTILE[:], op=ALU.add),
                    reads=["evac", "xt"], writes=[f"h:{k}"])

    def dequant_matrix(self, wsrc, wdst, n_i, n_d, wkey):
        sc = self.sc
        nb = n_d // 32
        for i in range(n_i // 128):
            sc.dma("SP", lambda e, i=i, wsrc=wsrc: e.dma_start(
                out=self.sDQ1[:, 0:n_d], in_=wsrc[128 * i:128 * (i + 1), :]),
                writes=["dq1"])
            w3 = self.sDQ1.ap()[:, 0:n_d].rearrange("p (b k) -> p b k", k=32)
            sc.op("DVE", lambda e, w3=w3: e.reduce_max(
                out=self.sBLK[:, 0:nb].unsqueeze(2), in_=w3, axis=AXX,
                apply_absolute_value=True),
                reads=["dq1"], writes=["blk"])
            sc.op("DVE", lambda e: e.tensor_scalar(
                out=self.sBLK[:, 0:nb], in0=self.sBLK[:, 0:nb], scalar1=1.0 / 7.0,
                scalar2=1e-12, op0=ALU.mult, op1=ALU.max),
                reads=["blk"], writes=["blk"])
            sc.op("DVE", lambda e: e.reciprocal(out=self.sRS[:, 0:nb], in_=self.sBLK[:, 0:nb]),
                  reads=["blk"], writes=["rs"])
            sc.op("DVE", lambda e, w3=w3: e.tensor_tensor(
                out=w3, in0=w3, in1=bc_in(self.sRS[:, 0:nb], 32), op=ALU.mult),
                reads=["dq1", "rs"], writes=["dq1"])
            sc.op("DVE", lambda e: e.tensor_scalar(
                out=self.sDQ1[:, 0:n_d], in0=self.sDQ1[:, 0:n_d], scalar1=MAGIC,
                scalar2=MAGIC, op0=ALU.add, op1=ALU.subtract),
                reads=["dq1"], writes=["dq1"])
            sc.op("DVE", lambda e, w3=w3: e.tensor_tensor(
                out=w3, in0=w3, in1=bc_in(self.sBLK[:, 0:nb], 32), op=ALU.mult),
                reads=["dq1", "blk"], writes=["dq1"])
            sc.op("DVE", lambda e: e.tensor_copy(out=self.sFW[0][:, 0:n_d],
                                                 in_=self.sDQ1[:, 0:n_d]),
                  reads=["dq1"], writes=["fw:0"])
            sc.dma("SP", lambda e, i=i, wdst=wdst: e.dma_start(
                out=wdst[128 * i:128 * (i + 1), :], in_=self.sFW[0][:, 0:n_d]),
                reads=["fw:0"], writes=[wkey])

    def absmax_to_scalar(self, col_ap, rkey, idx):
        sc = self.sc
        sc.dma("ACT", lambda e, col_ap=col_ap: e.dma_start(out=self.red_b, in_=col_ap),
               reads=[rkey], writes=["redb"])
        sc.dma("ACT", lambda e: e.dma_start(
            out=self.sRED[:], in_=self.red_b.rearrange("p o -> (p o)").unsqueeze(0)),
            reads=["redb"], writes=["redrow"])
        sc.op("DVE", lambda e, idx=idx: e.reduce_max(
            out=self.sSCR[0:1, idx:idx + 1], in_=self.sRED[:], axis=AXX),
            reads=["redrow"], writes=["scr"])
        sc.dma("ACT", lambda e, idx=idx: e.dma_start(
            out=self.cc_in[0:1, idx:idx + 1], in_=self.sSCR[0:1, idx:idx + 1]),
            reads=["scr"], writes=["ccin"])

    def allreduce_max(self):
        sc = self.sc
        sc.cc(lambda e: e.collective_compute(
            "AllReduce", ALU.max, replica_groups=[list(range(self.cfg.n_cores))],
            ins=[self.cc_in.opt()], outs=[self.cc_out.opt()]),
            reads=["ccin"], writes=["ccout"])

    def scale_cols(self, idx, sidx, qmax):
        sc = self.sc
        sc.dma("ACT", lambda e, idx=idx, sidx=sidx: e.dma_start(
            out=self.sSC[:, sidx:sidx + 1],
            in_=self.cc_out[0:1, idx:idx + 1].to_broadcast((128, 1))),
            reads=["ccout"], writes=["ssc"])
        sc.op("DVE", lambda e, sidx=sidx, qmax=qmax: e.tensor_scalar(
            out=self.sSC[:, sidx:sidx + 1], in0=self.sSC[:, sidx:sidx + 1],
            scalar1=1.0 / qmax, scalar2=1e-12, op0=ALU.mult, op1=ALU.max),
            reads=["ssc"], writes=["ssc"])
        sc.op("DVE", lambda e, sidx=sidx: e.reciprocal(
            out=self.sSC[:, sidx + 1:sidx + 2], in_=self.sSC[:, sidx:sidx + 1]),
            reads=["ssc"], writes=["ssc"])

    def qdq_tile(self, out_ap, in_ap, sidx, rkeys, wkeys):
        sc = self.sc
        W = in_ap.shape[1]
        tmp = self.sEVAC[:, 0:W]
        sc.op("DVE", lambda e: e.tensor_scalar(
            out=tmp, in0=in_ap, scalar1=self.sSC[:, sidx + 1:sidx + 2],
            scalar2=MAGIC, op0=ALU.mult, op1=ALU.add),
            reads=list(rkeys) + ["ssc"], writes=["evac"])
        sc.op("DVE", lambda e: e.tensor_scalar(
            out=out_ap, in0=tmp, scalar1=MAGIC, scalar2=self.sSC[:, sidx:sidx + 1],
            op0=ALU.subtract, op1=ALU.mult),
            reads=["evac", "ssc"], writes=list(wkeys))

    def phase_ffn(self):
        cfg, sc = self.cfg, self.sc
        L = cfg.L
        assert L == 2048
        self.dequant_matrix(self.w_gate, self.wg_dq, FFN_I, D_MODEL, "wgdq")
        self.dequant_matrix(self.w_up, self.wu_dq, FFN_I, D_MODEL, "wudq")
        self.dequant_matrix(self.w_down, self.wd_dq, D_MODEL, FFN_I, "wddq")

        def yq_t(k):
            return (self.sY[:, (k % 4) * 2048:(k % 4 + 1) * 2048] if k < 4
                    else self.sZ[:, (k - 4) * 2048:(k - 3) * 2048])

        def yq_keys(k):
            return [f"yq:{k}", (f"y:{4 * (k % 4)}" if k < 4 else f"z:{4 * (k - 4)}")]

        for j in range(L // 512):
            t0 = 512 * j
            hsl = lambda k, t0=t0: self.h[:, k * L + t0: k * L + t0 + 512]
            self.ln_rows(512, hsl, lambda k: [f"h:{k}"])
            self.ln_bcasts(512)
            for k in range(NDC):
                ysl = yq_t(k)[:, t0:t0 + 512]
                hap = hsl(k)
                sc.op("DVE", lambda e, hap=hap: e.tensor_tensor(
                    out=self.sEVAC[:, 0:512], in0=hap, in1=self.pPS[:, 1024 + 0:1024 + 512],
                    op=ALU.mult),
                    reads=[f"h:{k}", "P2"], writes=["evac"])
                sc.op("DVE", lambda e, ysl=ysl: e.tensor_tensor(
                    out=ysl, in0=self.sEVAC[:, 0:512], in1=self.pPS[:, 1024 + 512:1024 + 1024],
                    op=ALU.subtract),
                    reads=["evac", "P3"], writes=yq_keys(k) + ["yq_all"])
                sc.op("DVE", lambda e, ysl=ysl: e.reduce_max(
                    out=self.sEVAC[:, 512:513], in_=ysl, axis=AXX,
                    apply_absolute_value=True),
                    reads=yq_keys(k), writes=["evac"])
                sc.op("DVE", lambda e: e.tensor_tensor(
                    out=self.sMXC[:, 0:1], in0=self.sMXC[:, 0:1],
                    in1=self.sEVAC[:, 512:513], op=ALU.max),
                    reads=["evac", "mxc"], writes=["mxc"])
        self.absmax_to_scalar(self.sMXC[:, 0:1], "mxc", 0)
        self.allreduce_max()
        self.scale_cols(0, 0, QMAX)
        for k in range(NDC):
            for j in range(L // 512):
                sl = yq_t(k)[:, 512 * j:512 * (j + 1)]
                self.qdq_tile(sl, sl, 0, yq_keys(k), yq_keys(k) + ["yq_all"])
        sc.op("DVE", lambda e: e.memset(self.sMXC[:, 1:3], 0.0),
              reads=["mxc"], writes=["mxc"])
        all_yq = ["yq_all"]
        for which, wdq, dst, mcol, wdqk, dkey in (
                (0, self.wu_dq, self.up_d, 1, "wudq", "uprawd"),
                (1, self.wg_dq, self.gate_d, 2, "wgdq", "gaterawd")):
            for ig in range(FFN_I // 512):
                buf = ig % 2
                for k in range(NDC):
                    sc.dma("SP", lambda e, ig=ig, buf=buf, k=k, wdq=wdq: e.dma_start_transpose(
                        out=self.sFW[buf][:, 512 * k:512 * (k + 1)],
                        in_=wdq[512 * ig:512 * (ig + 1), 128 * k:128 * (k + 1)]),
                        reads=[wdqk], writes=[f"fw:{buf}"])
                for isub in range(4):
                    iof = 512 * ig + 128 * isub
                    for j in range(L // 512):
                        pb = (4 * ig + isub + j) % 2
                        psl = self.pPS[:, 512 * pb:512 * (pb + 1)]
                        t0 = 512 * j

                        def mk(e, buf=buf, isub=isub, t0=t0, psl=psl):
                            r = None
                            for k in range(NDC):
                                r = e.matmul(out=psl,
                                             lhsT=self.sFW[buf][:, 512 * k + 128 * isub:
                                                                512 * k + 128 * isub + 128],
                                             rhs=yq_t(k)[:, t0:t0 + 512],
                                             start=(k == 0), stop=(k == NDC - 1))
                            return r
                        sc.ops("PE", [mk], reads=[f"fw:{buf}"] + all_yq,
                               writes=[f"P{pb}"])
                        sc.op("DVE", lambda e, psl=psl: e.tensor_copy(
                            out=self.sT1[:], in_=psl),
                            reads=[f"P{pb}"], writes=["t1"])
                        sc.op("DVE", lambda e: e.reduce_max(
                            out=self.sEVAC[:, 512:513], in_=self.sT1[:], axis=AXX,
                            apply_absolute_value=True),
                            reads=["t1"], writes=["evac"])
                        sc.op("DVE", lambda e, mcol=mcol: e.tensor_tensor(
                            out=self.sMXC[:, mcol:mcol + 1], in0=self.sMXC[:, mcol:mcol + 1],
                            in1=self.sEVAC[:, 512:513], op=ALU.max),
                            reads=["evac", "mxc"], writes=["mxc"])
                        sc.dma("SP", lambda e, iof=iof, t0=t0, dst=dst: e.dma_start(
                            out=dst[iof:iof + 128, t0:t0 + 512], in_=self.sT1[:]),
                            reads=["t1"], writes=[dkey])
        self.absmax_to_scalar(self.sMXC[:, 1:2], "mxc", 1)
        self.absmax_to_scalar(self.sMXC[:, 2:3], "mxc", 2)
        self.allreduce_max()
        self.scale_cols(1, 2, QMAX)
        self.scale_cols(2, 4, QMAX)
        sc.op("DVE", lambda e: e.memset(self.sMXC[:, 3:4], 0.0),
              reads=["mxc"], writes=["mxc"])
        for ic in range(FFN_I // 128):
            for j in range(L // 512):
                t0 = 512 * j
                sc.dma("SP", lambda e, ic=ic, t0=t0: e.dma_start(
                    out=self.sT1[:], in_=self.gate_d[128 * ic:128 * (ic + 1), t0:t0 + 512]),
                    reads=["gaterawd"], writes=["t1"])
                self.qdq_tile(self.sT1[:], self.sT1[:], 4, ["t1"], ["t1"])
                sc.op("ACT", lambda e: e.activation(out=self.sT2[:], in_=self.sT1[:],
                                                    func=AF.Sigmoid),
                      reads=["t1"], writes=["t2"])
                sc.op("DVE", lambda e: e.tensor_scalar(
                    out=self.sEVAC[:, 0:512], in0=self.sT2[:], scalar1=U16S, scalar2=MAGIC,
                    op0=ALU.mult, op1=ALU.add),
                    reads=["t2"], writes=["evac"])
                sc.op("DVE", lambda e: e.tensor_scalar(
                    out=self.sT2[:], in0=self.sEVAC[:, 0:512], scalar1=MAGIC,
                    scalar2=1.0 / U16S, op0=ALU.subtract, op1=ALU.mult),
                    reads=["evac"], writes=["t2"])
                sc.op("DVE", lambda e: e.tensor_tensor(out=self.sT3[:], in0=self.sT1[:],
                                                       in1=self.sT2[:], op=ALU.mult),
                      reads=["t1", "t2"], writes=["t3"])
                sc.op("DVE", lambda e: e.reduce_max(
                    out=self.sEVAC[:, 512:513], in_=self.sT3[:], axis=AXX,
                    apply_absolute_value=True),
                    reads=["t3"], writes=["evac"])
                sc.op("DVE", lambda e: e.tensor_tensor(
                    out=self.sMXC[:, 3:4], in0=self.sMXC[:, 3:4],
                    in1=self.sEVAC[:, 512:513], op=ALU.max),
                    reads=["evac", "mxc"], writes=["mxc"])
                sc.dma("SP", lambda e, ic=ic, t0=t0: e.dma_start(
                    out=self.gate_d[128 * ic:128 * (ic + 1), t0:t0 + 512], in_=self.sT3[:]),
                    reads=["t3"], writes=["t4d"])
                sc.dma("SP", lambda e, ic=ic, t0=t0: e.dma_start(
                    out=self.sT1[:], in_=self.up_d[128 * ic:128 * (ic + 1), t0:t0 + 512]),
                    reads=["uprawd"], writes=["t1"])
                self.qdq_tile(self.sT1[:], self.sT1[:], 2, ["t1"], ["t1"])
                sc.dma("SP", lambda e, ic=ic, t0=t0: e.dma_start(
                    out=self.up_d[128 * ic:128 * (ic + 1), t0:t0 + 512], in_=self.sT1[:]),
                    reads=["t1"], writes=["upqd"])
        self.absmax_to_scalar(self.sMXC[:, 3:4], "mxc", 3)
        self.allreduce_max()
        self.scale_cols(3, 6, QMAX)
        sc.op("DVE", lambda e: e.memset(self.sMXC[:, 4:5], 0.0),
              reads=["mxc"], writes=["mxc"])
        for ic in range(FFN_I // 128):
            for j in range(L // 512):
                t0 = 512 * j
                sc.dma("SP", lambda e, ic=ic, t0=t0: e.dma_start(
                    out=self.sT1[:], in_=self.gate_d[128 * ic:128 * (ic + 1), t0:t0 + 512]),
                    reads=["t4d"], writes=["t1"])
                self.qdq_tile(self.sT1[:], self.sT1[:], 6, ["t1"], ["t1"])
                sc.dma("SP", lambda e, ic=ic, t0=t0: e.dma_start(
                    out=self.sT2[:], in_=self.up_d[128 * ic:128 * (ic + 1), t0:t0 + 512]),
                    reads=["upqd"], writes=["t2"])
                sc.op("DVE", lambda e: e.tensor_tensor(out=self.sT3[:], in0=self.sT1[:],
                                                       in1=self.sT2[:], op=ALU.mult),
                      reads=["t1", "t2"], writes=["t3"])
                sc.op("DVE", lambda e: e.reduce_max(
                    out=self.sEVAC[:, 512:513], in_=self.sT3[:], axis=AXX,
                    apply_absolute_value=True),
                    reads=["t3"], writes=["evac"])
                sc.op("DVE", lambda e: e.tensor_tensor(
                    out=self.sMXC[:, 4:5], in0=self.sMXC[:, 4:5],
                    in1=self.sEVAC[:, 512:513], op=ALU.max),
                    reads=["evac", "mxc"], writes=["mxc"])
                sc.dma("SP", lambda e, ic=ic, t0=t0: e.dma_start(
                    out=self.gate_d[128 * ic:128 * (ic + 1), t0:t0 + 512], in_=self.sT3[:]),
                    reads=["t3"], writes=["t5d"])
        self.absmax_to_scalar(self.sMXC[:, 4:5], "mxc", 0)
        self.allreduce_max()
        self.scale_cols(0, 0, QMAX)
        allp = [f"P{b}" for b in range(8)]
        for j in range(L // 512):
            t0 = 512 * j
            for ic in range(FFN_I // 128):
                sc.dma("SP", lambda e, ic=ic, t0=t0: e.dma_start(
                    out=self.sT1[:],
                    in_=self.gate_d[128 * ic:128 * (ic + 1), t0:t0 + 512]),
                    reads=["t5d"], writes=["t1"])
                self.qdq_tile(self.sO[:, 512 * ic:512 * (ic + 1)], self.sT1[:],
                              0, ["t1"], [f"o:{ic}"])
            for ic in range(FFN_I // 128):
                buf = ic % 2
                for dh in range(2):
                    sc.dma("SP", lambda e, ic=ic, buf=buf, dh=dh: e.dma_start_transpose(
                        out=self.sFW[buf][:, 512 * dh:512 * (dh + 1)],
                        in_=self.wd_dq[512 * dh:512 * (dh + 1), 128 * ic:128 * (ic + 1)]),
                        reads=["wddq"], writes=[f"fw:{buf}"])

                def mk(e, ic=ic, buf=buf):
                    r = None
                    for dk in range(NDC):
                        po = self.pPS[:, 512 * dk:512 * dk + 512]
                        r = e.matmul(out=po, lhsT=self.sFW[buf][:, 128 * dk:128 * (dk + 1)],
                                     rhs=self.sO[:, 512 * ic:512 * (ic + 1)],
                                     start=(ic == 0), stop=(ic == FFN_I // 128 - 1))
                    return r
                sc.ops("PE", [mk], reads=[f"fw:{buf}", f"o:{ic}"], writes=allp)
            for dk in range(NDC):
                po = self.pPS[:, 512 * dk:512 * dk + 512]
                hsl = self.h[:, dk * L + t0: dk * L + t0 + 512]
                sc.op("DVE", lambda e, po=po, hsl=hsl: e.tensor_tensor(
                    out=self.sEVAC[:, 0:512], in0=hsl, in1=po, op=ALU.add),
                    reads=[f"P{dk}", f"h:{dk}"], writes=["evac"])
                sc.dma("SP", lambda e, dk=dk, t0=t0: e.dma_start(
                    out=self.out_fm[128 * dk:128 * (dk + 1), t0:t0 + 512],
                    in_=self.sEVAC[:, 0:512]),
                    reads=["evac"], writes=["OUT"])

    def output_no_ffn(self):
        sc, L = self.sc, self.cfg.L
        for k in range(NDC):
            for j in range(L // 512):
                sc.dma("SP", lambda e, k=k, j=j: e.dma_start(
                    out=self.out_fm[128 * k:128 * (k + 1), 512 * j:512 * (j + 1)],
                    in_=self.h[:, k * L + 512 * j: k * L + 512 * (j + 1)]),
                    reads=[f"h:{k}"], writes=["OUT"])

    def absmax_to_out(self, col_ap, rkey, idx):
        sc = self.sc
        sc.dma("ACT", lambda e, col_ap=col_ap: e.dma_start(out=self.red_b, in_=col_ap),
               reads=[rkey], writes=["redb"])
        sc.dma("ACT", lambda e: e.dma_start(
            out=self.sRED[:], in_=self.red_b.rearrange("p o -> (p o)").unsqueeze(0)),
            reads=["redb"], writes=["redrow"])
        sc.op("DVE", lambda e, idx=idx: e.reduce_max(
            out=self.sSCR[0:1, idx:idx + 1], in_=self.sRED[:], axis=AXX),
            reads=["redrow"], writes=["scr"])
        sc.dma("ACT", lambda e, idx=idx: e.dma_start(
            out=self.mx_out[0:1, idx:idx + 1], in_=self.sSCR[0:1, idx:idx + 1]),
            reads=["scr"], writes=["OUTMX"])

    def reduce_mxcat(self):
        sc, NCC = self.sc, self.cfg.n_cores
        sc.dma("ACT", lambda e: e.dma_start(out=self.mxr_d, in_=self.mxcat),
               writes=["mxrd"])
        sc.dma("ACT", lambda e: e.dma_start(
            out=self.sRED[0:1, 0:6 * NCC],
            in_=self.mxr_d.rearrange("a b -> (a b)").unsqueeze(0)),
            reads=["mxrd"], writes=["redrow"])
        sc.op("DVE", lambda e, NCC=NCC: e.reduce_max(
            out=self.sSCR[0:1, 8:14].unsqueeze(2),
            in_=self.sRED.ap()[0:1, 0:6 * NCC].rearrange("o (a b) -> o a b", a=6),
            axis=AXX),
            reads=["redrow"], writes=["scr"])
        sc.dma("ACT", lambda e: e.dma_start(out=self.gmax_d[0:1, 0:6],
                                            in_=self.sSCR[0:1, 8:14]),
               reads=["scr"], writes=["gmaxd"])

    def scale_cols_g(self, idx, sidx, qmax):
        sc = self.sc
        sc.dma("ACT", lambda e, idx=idx, sidx=sidx: e.dma_start(
            out=self.sSC[:, sidx:sidx + 1],
            in_=self.gmax_d[0:1, idx:idx + 1].to_broadcast((128, 1))),
            reads=["gmaxd"], writes=["ssc"])
        sc.op("DVE", lambda e, sidx=sidx, qmax=qmax: e.tensor_scalar(
            out=self.sSC[:, sidx:sidx + 1], in0=self.sSC[:, sidx:sidx + 1],
            scalar1=1.0 / qmax, scalar2=1e-12, op0=ALU.mult, op1=ALU.max),
            reads=["ssc"], writes=["ssc"])
        sc.op("DVE", lambda e, sidx=sidx: e.reciprocal(
            out=self.sSC[:, sidx + 1:sidx + 2], in_=self.sSC[:, sidx:sidx + 1]),
            reads=["ssc"], writes=["ssc"])

    def stage1_tail(self):
        cfg, sc = self.cfg, self.sc
        L = cfg.L
        for j in range(L // 512):
            t0 = 512 * j
            hsl = lambda k, t0=t0: self.h[:, k * L + t0: k * L + t0 + 512]
            self.ln_rows(512, hsl, lambda k: [f"h:{k}"])
            self.ln_bcasts(512)
            for k in range(NDC):
                hap = hsl(k)
                sc.op("DVE", lambda e, hap=hap: e.tensor_tensor(
                    out=self.sEVAC[:, 0:512], in0=hap, in1=self.pPS[:, 1024 + 0:1024 + 512],
                    op=ALU.mult),
                    reads=[f"h:{k}", "P2"], writes=["evac"])
                sc.op("DVE", lambda e: e.tensor_tensor(
                    out=self.sXB[:, 0:512], in0=self.sEVAC[:, 0:512],
                    in1=self.pPS[:, 1024 + 512:1024 + 1024], op=ALU.subtract),
                    reads=["evac", "P3"], writes=["sXB"])
                sc.op("DVE", lambda e: e.reduce_max(
                    out=self.sEVAC[:, 512:513], in_=self.sXB[:, 0:512], axis=AXX,
                    apply_absolute_value=True),
                    reads=["sXB"], writes=["evac"])
                sc.op("DVE", lambda e: e.tensor_tensor(
                    out=self.sMXC[:, 0:1], in0=self.sMXC[:, 0:1],
                    in1=self.sEVAC[:, 512:513], op=ALU.max),
                    reads=["evac", "mxc"], writes=["mxc"])
                sc.dma("SP", lambda e, k=k, t0=t0: e.dma_start(
                    out=self.y_out[128 * k:128 * (k + 1), t0:t0 + 512],
                    in_=self.sXB[:, 0:512]),
                    reads=["sXB"], writes=["OUT"])
        self.absmax_to_out(self.sMXC[:, 0:1], "mxc", 0)
        self.output_no_ffn()

    def stage2(self):
        cfg, sc = self.cfg, self.sc
        L = cfg.L
        self.dequant_matrix(self.w_gate, self.wg_dq, FFN_I, D_MODEL, "wgdq")
        self.dequant_matrix(self.w_up, self.wu_dq, FFN_I, D_MODEL, "wudq")
        self.reduce_mxcat()
        self.scale_cols_g(0, 0, QMAX)

        def yq_t(k):
            return (self.sY[:, (k % 4) * 2048:(k % 4 + 1) * 2048] if k < 4
                    else self.sZ[:, (k - 4) * 2048:(k - 3) * 2048])
        for k in range(NDC):
            for j in range(L // 512):
                t0 = 512 * j
                sl = yq_t(k)[:, t0:t0 + 512]
                sc.dma("SP", lambda e, k=k, t0=t0, sl=sl: e.dma_start(
                    out=sl, in_=self.y_in[128 * k:128 * (k + 1), t0:t0 + 512]),
                    writes=["yq_all"])
                self.qdq_tile(sl, sl, 0, ["yq_all"], ["yq_all"])
        for which, wdq, dst, mcol, wdqk, dkey in (
                (0, self.wu_dq, self.upraw_out, 1, "wudq", "uprawd"),
                (1, self.wg_dq, self.gateraw_out, 2, "wgdq", "gaterawd")):
            for ig in range(FFN_I // 512):
                buf = ig % 2
                for k in range(NDC):
                    sc.dma("SP", lambda e, ig=ig, buf=buf, k=k, wdq=wdq: e.dma_start_transpose(
                        out=self.sFW[buf][:, 512 * k:512 * (k + 1)],
                        in_=wdq[512 * ig:512 * (ig + 1), 128 * k:128 * (k + 1)]),
                        reads=[wdqk], writes=[f"fw:{buf}"])
                for isub in range(4):
                    iof = 512 * ig + 128 * isub
                    for j in range(L // 512):
                        pb = (4 * ig + isub + j) % 2
                        psl = self.pPS[:, 512 * pb:512 * (pb + 1)]
                        t0 = 512 * j

                        def mk(e, buf=buf, isub=isub, t0=t0, psl=psl):
                            r = None
                            for k in range(NDC):
                                r = e.matmul(out=psl,
                                             lhsT=self.sFW[buf][:, 512 * k + 128 * isub:
                                                                512 * k + 128 * isub + 128],
                                             rhs=yq_t(k)[:, t0:t0 + 512],
                                             start=(k == 0), stop=(k == NDC - 1))
                            return r
                        sc.ops("PE", [mk], reads=[f"fw:{buf}", "yq_all"],
                               writes=[f"P{pb}"])
                        sc.op("DVE", lambda e, psl=psl: e.tensor_copy(
                            out=self.sT1[:], in_=psl),
                            reads=[f"P{pb}"], writes=["t1"])
                        sc.op("DVE", lambda e: e.reduce_max(
                            out=self.sEVAC[:, 512:513], in_=self.sT1[:], axis=AXX,
                            apply_absolute_value=True),
                            reads=["t1"], writes=["evac"])
                        sc.op("DVE", lambda e, mcol=mcol: e.tensor_tensor(
                            out=self.sMXC[:, mcol:mcol + 1], in0=self.sMXC[:, mcol:mcol + 1],
                            in1=self.sEVAC[:, 512:513], op=ALU.max),
                            reads=["evac", "mxc"], writes=["mxc"])
                        sc.dma("SP", lambda e, iof=iof, t0=t0, dst=dst: e.dma_start(
                            out=dst[iof:iof + 128, t0:t0 + 512], in_=self.sT1[:]),
                            reads=["t1"], writes=["OUT"])
        self.absmax_to_out(self.sMXC[:, 1:2], "mxc", 1)
        self.absmax_to_out(self.sMXC[:, 2:3], "mxc", 2)

    def stage3(self):
        cfg, sc = self.cfg, self.sc
        L = cfg.L
        self.reduce_mxcat()
        self.scale_cols_g(2, 4, QMAX)
        for ic in range(FFN_I // 128):
            for j in range(L // 512):
                t0 = 512 * j
                sc.dma("SP", lambda e, ic=ic, t0=t0: e.dma_start(
                    out=self.sT1[:], in_=self.gateraw_in[128 * ic:128 * (ic + 1), t0:t0 + 512]),
                    writes=["t1"])
                self.qdq_tile(self.sT1[:], self.sT1[:], 4, ["t1"], ["t1"])
                sc.op("ACT", lambda e: e.activation(out=self.sT2[:], in_=self.sT1[:],
                                                    func=AF.Sigmoid),
                      reads=["t1"], writes=["t2"])
                sc.op("DVE", lambda e: e.tensor_scalar(
                    out=self.sEVAC[:, 0:512], in0=self.sT2[:], scalar1=U16S, scalar2=MAGIC,
                    op0=ALU.mult, op1=ALU.add),
                    reads=["t2"], writes=["evac"])
                sc.op("DVE", lambda e: e.tensor_scalar(
                    out=self.sT2[:], in0=self.sEVAC[:, 0:512], scalar1=MAGIC,
                    scalar2=1.0 / U16S, op0=ALU.subtract, op1=ALU.mult),
                    reads=["evac"], writes=["t2"])
                sc.op("DVE", lambda e: e.tensor_tensor(out=self.sT3[:], in0=self.sT1[:],
                                                       in1=self.sT2[:], op=ALU.mult),
                      reads=["t1", "t2"], writes=["t3"])
                sc.op("DVE", lambda e: e.reduce_max(
                    out=self.sEVAC[:, 512:513], in_=self.sT3[:], axis=AXX,
                    apply_absolute_value=True),
                    reads=["t3"], writes=["evac"])
                sc.op("DVE", lambda e: e.tensor_tensor(
                    out=self.sMXC[:, 3:4], in0=self.sMXC[:, 3:4],
                    in1=self.sEVAC[:, 512:513], op=ALU.max),
                    reads=["evac", "mxc"], writes=["mxc"])
                sc.dma("SP", lambda e, ic=ic, t0=t0: e.dma_start(
                    out=self.t4_out[128 * ic:128 * (ic + 1), t0:t0 + 512], in_=self.sT3[:]),
                    reads=["t3"], writes=["OUT"])
        self.absmax_to_out(self.sMXC[:, 3:4], "mxc", 3)

    def stage4(self):
        cfg, sc = self.cfg, self.sc
        L = cfg.L
        self.reduce_mxcat()
        self.scale_cols_g(3, 6, QMAX)
        self.scale_cols_g(1, 2, QMAX)
        for ic in range(FFN_I // 128):
            for j in range(L // 512):
                t0 = 512 * j
                sc.dma("SP", lambda e, ic=ic, t0=t0: e.dma_start(
                    out=self.sT1[:], in_=self.t4_in[128 * ic:128 * (ic + 1), t0:t0 + 512]),
                    writes=["t1"])
                self.qdq_tile(self.sT1[:], self.sT1[:], 6, ["t1"], ["t1"])
                sc.dma("SP", lambda e, ic=ic, t0=t0: e.dma_start(
                    out=self.sT2[:], in_=self.upraw_in[128 * ic:128 * (ic + 1), t0:t0 + 512]),
                    writes=["t2"])
                self.qdq_tile(self.sT2[:], self.sT2[:], 2, ["t2"], ["t2"])
                sc.op("DVE", lambda e: e.tensor_tensor(out=self.sT3[:], in0=self.sT1[:],
                                                       in1=self.sT2[:], op=ALU.mult),
                      reads=["t1", "t2"], writes=["t3"])
                sc.op("DVE", lambda e: e.reduce_max(
                    out=self.sEVAC[:, 512:513], in_=self.sT3[:], axis=AXX,
                    apply_absolute_value=True),
                    reads=["t3"], writes=["evac"])
                sc.op("DVE", lambda e: e.tensor_tensor(
                    out=self.sMXC[:, 4:5], in0=self.sMXC[:, 4:5],
                    in1=self.sEVAC[:, 512:513], op=ALU.max),
                    reads=["evac", "mxc"], writes=["mxc"])
                sc.dma("SP", lambda e, ic=ic, t0=t0: e.dma_start(
                    out=self.t5_out[128 * ic:128 * (ic + 1), t0:t0 + 512], in_=self.sT3[:]),
                    reads=["t3"], writes=["OUT"])
        self.absmax_to_out(self.sMXC[:, 4:5], "mxc", 4)

    def stage5(self):
        cfg, sc = self.cfg, self.sc
        L = cfg.L
        self.dequant_matrix(self.w_down, self.wd_dq, D_MODEL, FFN_I, "wddq")
        self.reduce_mxcat()
        self.scale_cols_g(4, 0, QMAX)
        allp = [f"P{b}" for b in range(8)]
        for j in range(L // 512):
            t0 = 512 * j
            for ic in range(FFN_I // 128):
                sc.dma("SP", lambda e, ic=ic, t0=t0: e.dma_start(
                    out=self.sT1[:], in_=self.t5_in[128 * ic:128 * (ic + 1), t0:t0 + 512]),
                    writes=["t1"])
                self.qdq_tile(self.sO[:, 512 * ic:512 * (ic + 1)], self.sT1[:],
                              0, ["t1"], [f"o:{ic}"])
            for ic in range(FFN_I // 128):
                buf = ic % 2
                for dh in range(2):
                    sc.dma("SP", lambda e, ic=ic, buf=buf, dh=dh: e.dma_start_transpose(
                        out=self.sFW[buf][:, 512 * dh:512 * (dh + 1)],
                        in_=self.wd_dq[512 * dh:512 * (dh + 1), 128 * ic:128 * (ic + 1)]),
                        reads=["wddq"], writes=[f"fw:{buf}"])

                def mk(e, ic=ic, buf=buf):
                    r = None
                    for dk in range(NDC):
                        po = self.pPS[:, 512 * dk:512 * dk + 512]
                        r = e.matmul(out=po, lhsT=self.sFW[buf][:, 128 * dk:128 * (dk + 1)],
                                     rhs=self.sO[:, 512 * ic:512 * (ic + 1)],
                                     start=(ic == 0), stop=(ic == FFN_I // 128 - 1))
                    return r
                sc.ops("PE", [mk], reads=[f"fw:{buf}", f"o:{ic}"], writes=allp)
            for dk in range(NDC):
                po = self.pPS[:, 512 * dk:512 * dk + 512]
                sc.dma("SP", lambda e, dk=dk, t0=t0: e.dma_start(
                    out=self.sXTILE[:], in_=self.x1_in[128 * dk:128 * (dk + 1), t0:t0 + 512]),
                    writes=["xt"])
                sc.op("DVE", lambda e, po=po: e.tensor_tensor(
                    out=self.sEVAC[:, 0:512], in0=self.sXTILE[:], in1=po, op=ALU.add),
                    reads=[f"P{dk}", "xt"], writes=["evac"])
                sc.dma("SP", lambda e, dk=dk, t0=t0: e.dma_start(
                    out=self.out_fm[128 * dk:128 * (dk + 1), t0:t0 + 512],
                    in_=self.sEVAC[:, 0:512]),
                    reads=["evac"], writes=["OUT"])

    def build(self):
        cfg, sc = self.cfg, self.sc
        st = cfg.stage
        self.load_consts()
        if st in ("all", "s1"):
            self.preconvert_weights()
            self.phase_init_ln()
            for l in range(cfg.nlayer):
                self.load_layer_params(l)
                for seg in range(cfg.NSEG):
                    self.phase_A(l, seg)
                    self.phase_B_prep_front(l, seg, 0)
                    self.phase_B_prep_back(l, seg, 0)
                    for c in range(cfg.NCH):
                        self.phase_B_compute(l, seg, c)
                    self.phase_C(l, seg)
            self.phase_x1()
        if st == "all":
            if cfg.do_ffn:
                sc.barrier()
                self.phase_ffn()
            else:
                self.output_no_ffn()
        elif st == "s1":
            sc.barrier()
            if cfg.do_ffn:
                self.stage1_tail()
            else:
                self.output_no_ffn()
        elif st == "s2":
            self.stage2()
        elif st == "s3":
            self.stage3()
        elif st == "s4":
            self.stage4()
        elif st == "s5":
            self.stage5()
        sc.finish(["OUT", "OUTMX"] if (cfg.do_ffn and st != "s5" and st != "all")
                  else ["OUT"])
        with self.nc.Block() as block:
            sc.emit(block)
        return self.nc


def host_constants():
    e64s = np.zeros((NH, 512), np.float32)
    e64sb = np.zeros((NH, 512), np.float32)
    for r in range(NH):
        hr = r % 32
        for t in range(32):
            if hr < 16:
                e64s[r, t * 16 + hr] = -1.0
            else:
                e64sb[r, t * 16 + (hr - 16)] = -1.0
    e64x = np.tile(e64s, (1, 4))
    e64xb = np.tile(e64sb, (1, 4))
    trid = np.zeros((32, 512), np.float32)
    for r in range(32):
        trid[r, r * 16:(r + 1) * 16] = -BIG
    trid = np.tile(trid, (1, 4))
    tri4 = np.zeros((4, 32, 128), np.float32)
    p = np.arange(128)
    for sl in range(4):
        for r in range(32):
            tri4[sl, r] = (p >= 32 * sl + r + 1).astype(np.float32)
    esel = np.zeros((NH, D_INNER), np.float32)
    for k in range(NZC):
        for q in range(128):
            esel[4 * k + q // 32, 128 * k + q] = 1.0
    idb = np.eye(128, dtype=np.float32)
    bf = lambda a: a.astype(ml_dtypes.bfloat16)
    return dict(c_e64x=bf(e64x), c_e64xb=bf(e64xb), c_trid=bf(trid), c_tri4=bf(tri4),
                c_esel=bf(esel), c_idb=bf(idb),
                c_ones2=bf(np.ones((2, 128), np.float32)),
                c_ones64=bf(np.ones((NH, 128), np.float32)),
                c_onescol=bf(np.ones((128, 1), np.float32)))


def host_layout(inputs, cfg: Cfg):
    NL = cfg.nlayer
    cw = np.asarray(inputs["L0_attn_conv_w"], np.float32)[:NL]
    cb = np.asarray(inputs["L0_attn_conv_b"], np.float32)[:NL]
    dcwL = np.zeros((NL, NXC * DCONV, 128, 128), np.float32)
    idx = np.arange(128)
    for l in range(NL):
        for j in range(NXC):
            for k in range(DCONV):
                dcwL[l, 4 * j + k, idx, idx] = cw[l, 128 * j + idx, k]
    cbL = np.ascontiguousarray(cb.reshape(NL, NXC, 128).transpose(0, 2, 1))
    colmaj = lambda v, n: np.ascontiguousarray(v.reshape(v.shape[:-1] + (n, 128)).swapaxes(-1, -2))
    gnwR = colmaj(np.asarray(inputs["L0_attn_gate_norm_w"], np.float32)[:NL], NZC)
    bnwR = colmaj(np.asarray(inputs["L0_attn_block_norm_w"], np.float32)[:NL], NDC)
    fnwR = colmaj(np.asarray(inputs["L0_attn_final_norm_w"], np.float32), NDC)
    col = lambda v: np.ascontiguousarray(np.asarray(v, np.float32)[:NL][:, :, None])
    m = dict(
        wi=np.ascontiguousarray(np.asarray(inputs["L0_attn_W_in"], np.float32)[:NL]),
        wo=np.ascontiguousarray(np.asarray(inputs["L0_attn_W_out"], np.float32)[:NL]),
        dcwL=dcwL, cbL=cbL,
        dtbL=col(inputs["L0_attn_dt_bias"]),
        alogL=col(inputs["L0_attn_A_log"]),
        dL=col(inputs["L0_attn_D"]),
        gnwR=gnwR, bnwR=bnwR, fnwR=fnwR,
        w_gate=np.ascontiguousarray(np.asarray(inputs["L0_ffn_w_gate"], np.float32)),
        w_up=np.ascontiguousarray(np.asarray(inputs["L0_ffn_w_up"], np.float32)),
        w_down=np.ascontiguousarray(np.asarray(inputs["L0_ffn_w_down"], np.float32)),
    )
    m.update(host_constants())
    return m


def kernel(**inputs):
    n_cores = 8
    x = np.asarray(inputs["L0_attn_x"], np.float32)
    L = x.shape[1]
    shared = host_layout(inputs, Cfg(L=L, n_cores=n_cores))
    x_fms = [np.ascontiguousarray(x[i].T) for i in range(n_cores)]
    mx = np.zeros((6, n_cores), np.float32)

    def run(stage, extra_names, per_core_extra):
        cfg = Cfg(L=L, nlayer=4, do_ffn=True, n_cores=n_cores, stage=stage)
        b = Builder(cfg)
        nc = b.build()
        in_names = {t.name for t in
                    (a for a in nc.m.functions[0].allocations
                     if getattr(a, "kind", None) == "ExternalInput")}
        maps = []
        for i in range(n_cores):
            m = {k: v for k, v in shared.items()}
            m["x_fm"] = x_fms[i]
            for name, arrs in per_core_extra.items():
                m[name] = arrs[i]
            for name in extra_names:
                m[name] = extra_names[name]
            maps.append(m)
        res = run_bass_kernel_spmd(nc, maps, core_ids=list(range(n_cores)))
        return res

    r1 = run("s1", {}, {})
    x1s = [np.asarray(r1.results[i]["out_fm"]) for i in range(n_cores)]
    ys = [np.asarray(r1.results[i]["y_out"]) for i in range(n_cores)]
    for i in range(n_cores):
        mx[0, i] = np.asarray(r1.results[i]["mx_out"]).ravel()[0]

    r2 = run("s2", {"mxcat": mx}, {"y_in": ys})
    ups = [np.asarray(r2.results[i]["upraw_out"]) for i in range(n_cores)]
    gates = [np.asarray(r2.results[i]["gateraw_out"]) for i in range(n_cores)]
    for i in range(n_cores):
        v = np.asarray(r2.results[i]["mx_out"]).ravel()
        mx[1, i] = v[1]
        mx[2, i] = v[2]

    r3 = run("s3", {"mxcat": mx}, {"gateraw_in": gates})
    t4s = [np.asarray(r3.results[i]["t4_out"]) for i in range(n_cores)]
    for i in range(n_cores):
        mx[3, i] = np.asarray(r3.results[i]["mx_out"]).ravel()[3]

    r4 = run("s4", {"mxcat": mx}, {"t4_in": t4s, "upraw_in": ups})
    t5s = [np.asarray(r4.results[i]["t5_out"]) for i in range(n_cores)]
    for i in range(n_cores):
        mx[4, i] = np.asarray(r4.results[i]["mx_out"]).ravel()[4]

    r5 = run("s5", {"mxcat": mx}, {"t5_in": t5s, "x1_in": x1s})
    out = np.stack([np.asarray(r5.results[i]["out_fm"]).T for i in range(n_cores)],
                   axis=0)
    return out.astype(np.float32)


# version 12
# speedup vs baseline: 1.0286x; 1.0286x over previous
import numpy as np
import ml_dtypes
import concourse.bass as bass
import concourse.mybir as mybir
from concourse.bass_utils import run_bass_kernel_spmd

F32 = mybir.dt.float32
BF16 = mybir.dt.bfloat16
AF = mybir.ActivationFunctionType
ALU = mybir.AluOpType
AXX = mybir.AxisListType.X

MAGIC = float(1.5 * 2 ** 23)
BIG = 57344.0
LN_EPS = 1e-5

D_MODEL = 1024
D_INNER = 2048
NH = 64
HD = 32
DS = 128
DCONV = 4
CONV_DIM = D_INNER + 2 * DS
D_IN = 2 * D_INNER + 2 * DS + NH
FFN_I = 3072
NZC = D_INNER // 128
NXC = CONV_DIM // 128
NDC = D_MODEL // 128
QMAX = 32767.0
U16S = 65536.0

ENGS = ("PE", "DVE", "ACT", "POOL", "SP")


class Cfg:
    def __init__(self, L=2048, nlayer=4, do_ffn=True, n_cores=8, stage="all"):
        self.stage = stage
        self.L = L
        self.C = 128
        self.TSEG = min(512, L)
        assert L % self.TSEG == 0 and self.TSEG % self.C == 0
        self.NSEG = L // self.TSEG
        self.NCH = self.TSEG // self.C
        self.nlayer = nlayer
        self.do_ffn = do_ffn
        self.n_cores = n_cores


class Sched:

    NLANES = 6

    def __init__(self, nc):
        self.nc = nc
        self.sem = {e: nc.alloc_semaphore(f"sem_{e}") for e in ENGS}
        self.dsem = {}
        self.cnt = {k: 0 for k in ENGS}
        for qn in ("SP", "ACT", "POOL"):
            for ln in range(self.NLANES):
                key = f"d{qn}:{ln}"
                self.dsem[key] = nc.alloc_semaphore(f"dma_{qn}_{ln}")
                self.cnt[key] = 0
        self.ccsem = nc.alloc_semaphore("sem_cc")
        self.cnt["CC"] = 0
        self.dma_issue = {qn: 0 for qn in ("SP", "ACT", "POOL")}
        self.q = {e: [] for e in ENGS}
        self.waited = {e: {} for e in ENGS}
        self.last_w = {}
        self.readers = {}
        self.nops = 0

    def _sobj(self, semkey):
        if semkey in self.sem:
            return self.sem[semkey]
        if semkey == "CC":
            return self.ccsem
        return self.dsem[semkey]

    def _wait(self, eng, semkey, val):
        if val <= self.waited[eng].get(semkey, 0):
            return
        assert val <= self.cnt[semkey], f"wait on future {semkey}:{val}"
        self.waited[eng][semkey] = val
        sobj = self._sobj(semkey)
        self.q[eng].append(lambda e, s=sobj, v=val: e.wait_ge(s, v))

    def _edges(self, eng, reads, writes):
        for k in reads:
            lw = self.last_w.get(k)
            if lw is not None and not (lw[0] == "PE" == eng):
                self._wait(eng, lw[0], lw[1])
        for k in writes:
            lw = self.last_w.get(k)
            if lw is not None and not (lw[0] == "PE" == eng):
                self._wait(eng, lw[0], lw[1])
            for skey, v in self.readers.get(k, ()):
                if not (skey == "PE" == eng):
                    self._wait(eng, skey, v)

    def _record(self, semkey, val, reads, writes):
        for k in writes:
            self.last_w[k] = (semkey, val)
            self.readers[k] = []
        for k in reads:
            self.readers.setdefault(k, []).append((semkey, val))

    def op(self, eng, fn, reads=(), writes=()):
        self._edges(eng, reads, writes)
        self.cnt[eng] += 1
        sobj = self.sem[eng]
        self.q[eng].append(lambda e, f=fn, s=sobj: f(e).then_inc(s, 1))
        self._record(eng, self.cnt[eng], reads, writes)
        self.nops += 1

    def ops(self, eng, fns, reads=(), writes=()):
        self._edges(eng, reads, writes)
        self.cnt[eng] += 1
        sobj = self.sem[eng]
        fl = list(fns)

        def run(e, fl=fl, s=sobj):
            for f in fl[:-1]:
                f(e)
            fl[-1](e).then_inc(s, 1)

        self.q[eng].append(run)
        self._record(eng, self.cnt[eng], reads, writes)
        self.nops += len(fl)

    def dma(self, queue, fn, reads=(), writes=()):
        lane = self.dma_issue[queue] % self.NLANES
        self.dma_issue[queue] += 1
        semkey = f"d{queue}:{lane}"
        self._wait(queue, semkey, self.cnt[semkey])
        self._edges(queue, reads, writes)
        self.cnt[semkey] += 16
        sobj = self.dsem[semkey]
        self.q[queue].append(lambda e, f=fn, s=sobj: f(e).then_inc(s, 16))
        self._record(semkey, self.cnt[semkey], reads, writes)
        self.nops += 1

    def cc(self, fn, reads=(), writes=()):
        self._wait("POOL", "CC", self.cnt["CC"])
        self._edges("POOL", reads, writes)
        self.cnt["CC"] += 16
        sobj = self.ccsem
        self.q["POOL"].append(lambda e, f=fn, s=sobj: f(e).then_inc(s, 16))
        self._record("CC", self.cnt["CC"], reads, writes)
        self.nops += 1

    def barrier(self):
        semkeys = list(self.sem) + list(self.dsem) + ["CC"]
        for consumer in ENGS:
            for sk in semkeys:
                if self.cnt.get(sk, 0) > 0:
                    self._wait(consumer, sk, self.cnt[sk])

    def finish(self, keys):
        self._edges("SP", keys, ())

    def emit(self, block):
        qs = self.q

        @block.tensor
        def _(e):
            for f in qs["PE"]:
                f(e)

        @block.vector
        def _(e):
            for f in qs["DVE"]:
                f(e)

        @block.scalar
        def _(e):
            for f in qs["ACT"]:
                f(e)

        @block.gpsimd
        def _(e):
            for f in qs["POOL"]:
                f(e)

        @block.sync
        def _(e):
            for f in qs["SP"]:
                f(e)


def bc_in(ap, n):
    return ap.unsqueeze(2).to_broadcast((ap.shape[0], ap.shape[1], n))


class Builder:
    def __init__(self, cfg: Cfg):
        self.cfg = cfg
        self.nc = bass.Bass("TRN2", target_bir_lowering=False, debug=False)
        self.declare_io()
        self.declare_sbuf()
        self.sc = Sched(self.nc)

    def declare_io(self):
        nc, cfg = self.nc, self.cfg
        L, NL = cfg.L, cfg.nlayer

        def D(name, shape, dt, kind=None):
            if kind:
                return nc.dram_tensor(name, shape, dt, kind=kind).ap()
            return nc.dram_tensor(name, shape, dt).ap()

        self.x_fm = D("x_fm", [D_MODEL, L], F32, "ExternalInput")
        self.wi = D("wi", [NL, D_MODEL, D_IN], F32, "ExternalInput")
        self.wo = D("wo", [NL, D_INNER, D_MODEL], F32, "ExternalInput")
        self.dcwL = D("dcwL", [NL, NXC * DCONV, 128, 128], F32, "ExternalInput")
        self.cbL = D("cbL", [NL, 128, NXC], F32, "ExternalInput")
        self.dtbL = D("dtbL", [NL, NH, 1], F32, "ExternalInput")
        self.alogL = D("alogL", [NL, NH, 1], F32, "ExternalInput")
        self.dL = D("dL", [NL, NH, 1], F32, "ExternalInput")
        self.gnwR = D("gnwR", [NL, 128, NZC], F32, "ExternalInput")
        self.bnwR = D("bnwR", [NL, 128, NDC], F32, "ExternalInput")
        self.fnwR = D("fnwR", [128, NDC], F32, "ExternalInput")
        self.w_gate = D("w_gate", [FFN_I, D_MODEL], F32, "ExternalInput")
        self.w_up = D("w_up", [FFN_I, D_MODEL], F32, "ExternalInput")
        self.w_down = D("w_down", [D_MODEL, FFN_I], F32, "ExternalInput")
        self.c_e64x = D("c_e64x", [NH, 2048], BF16, "ExternalInput")
        self.c_e64xb = D("c_e64xb", [NH, 2048], BF16, "ExternalInput")
        self.c_trid = D("c_trid", [32, 2048], BF16, "ExternalInput")
        self.c_tri16 = D("c_tri16", [32, 16 * 128], BF16, "ExternalInput")
        self.c_esel = D("c_esel", [NH, D_INNER], BF16, "ExternalInput")
        self.c_idb = D("c_idb", [128, 128], BF16, "ExternalInput")
        self.c_ones2 = D("c_ones2", [2, 128], BF16, "ExternalInput")
        self.c_ones64 = D("c_ones64", [NH, 128], BF16, "ExternalInput")
        self.c_onescol = D("c_onescol", [128, 1], BF16, "ExternalInput")
        self.out_fm = D("out_fm", [D_MODEL, L], F32, "ExternalOutput")
        st = cfg.stage
        if st in ("s1",):
            self.y_out = D("y_out", [D_MODEL, L], BF16, "ExternalOutput")
        if st in ("s1", "s2", "s3", "s4"):
            self.mx_out = D("mx_out", [1, 6], F32, "ExternalOutput")
        if st in ("s2", "s3", "s4", "s5"):
            self.mxcat = D("mxcat", [6, cfg.n_cores], F32, "ExternalInput")
        if st == "s2":
            self.y_in = D("y_in", [D_MODEL, L], BF16, "ExternalInput")
            self.upraw_out = D("upraw_out", [FFN_I, L], BF16, "ExternalOutput")
            self.gateraw_out = D("gateraw_out", [FFN_I, L], BF16, "ExternalOutput")
        if st == "s3":
            self.gateraw_in = D("gateraw_in", [FFN_I, L], BF16, "ExternalInput")
            self.t4_out = D("t4_out", [FFN_I, L], BF16, "ExternalOutput")
        if st == "s4":
            self.t4_in = D("t4_in", [FFN_I, L], BF16, "ExternalInput")
            self.upraw_in = D("upraw_in", [FFN_I, L], BF16, "ExternalInput")
            self.t5_out = D("t5_out", [FFN_I, L], BF16, "ExternalOutput")
        if st == "s5":
            self.t5_in = D("t5_in", [FFN_I, L], BF16, "ExternalInput")
            self.x1_in = D("x1_in", [D_MODEL, L], F32, "ExternalInput")
        self.gmax_d = D("gmax_d", [1, 6], F32)
        self.mxr_d = D("mxr_d", [6, cfg.n_cores], F32)
        self.wi_bf = D("wi_bf", [NL, D_MODEL, D_IN], BF16)
        self.wo_bf = D("wo_bf", [NL, D_INNER, D_MODEL], BF16)
        self.dcw_bf = D("dcw_bf", [NL, NXC * DCONV, 128, 128], BF16)
        self.xs_b = [D(f"xs_b{i}", [D_INNER, 128], BF16) for i in range(2)]
        self.lt_b = [D(f"lt_b{i}", [2, 128, 16], BF16) for i in range(2)]
        self.red_b = D("red_b", [128, 1], F32)
        self.cc_in = D("cc_in", [1, 4], F32)
        self.cc_out = D("cc_out", [1, 4], F32)
        if cfg.do_ffn:
            self.wg_dq = D("wg_dq", [FFN_I, D_MODEL], BF16)
            self.wu_dq = D("wu_dq", [FFN_I, D_MODEL], BF16)
            self.wd_dq = D("wd_dq", [D_MODEL, FFN_I], BF16)
            self.up_d = D("up_d", [FFN_I, L], BF16)
            self.gate_d = D("gate_d", [FFN_I, L], BF16)

    def declare_sbuf(self):
        nc, cfg = self.nc, self.cfg
        L, T = cfg.L, cfg.TSEG
        self._off = 16640

        def A(name, shape, dtype, at=None):
            if at is None:
                nbytes = shape[1] * mybir.dt.size(dtype)
                t = nc.alloc_sbuf_tensor_at(name, shape, dtype, offset=self._off)
                self._off += (nbytes + 31) // 32 * 32
                return t
            return nc.alloc_sbuf_tensor_at(name, shape, dtype, offset=at)

        self.h = A("h", [128, NDC * L], F32)
        self.sE64X = [A(f"sE64X{i}", [98, 2048], BF16) for i in range(2)]
        self.sESEL = A("sESEL", [NH, D_INNER], BF16)
        self.sIDB = A("sIDB", [128, 128], BF16)
        self.sONES2 = A("sONES2", [2, 128], BF16)
        self.sONES64 = A("sONES64", [NH, 128], BF16)
        self.sONESC = A("sONESC", [128, 1], BF16)
        self.sCB = A("sCB", [128, NXC], F32)
        self.sGNWC = A("sGNWC", [128, NZC], F32)
        self.sBNWC = A("sBNWC", [128, NDC], F32)
        self.sFNWC = A("sFNWC", [128, NDC], F32)
        self.sDTB = A("sDTB", [NH, 1], F32)
        self.sALOG = A("sALOG", [NH, 1], F32)
        self.sACOL = A("sACOL", [NH, 1], F32)
        self.sDCOL = A("sDCOL", [NH, 1], F32)
        self.sDBF = A("sDBF", [NH, 1], BF16)
        self.sDEXP = A("sDEXP", [128, NZC], F32)
        self.sCARRY = A("sCARRY", [128, NXC * (DCONV - 1)], BF16)
        self.sZERO64 = A("sZERO64", [NH, 128], F32)
        self._sEPS1 = A("sEPS1", [128, 2], F32)
        self.sEPS = self._sEPS1.ap()[:, 0:1]
        self.sONE1 = self._sEPS1.ap()[:, 1:2]
        self.sEVAC = A("sEVAC", [128, 516], F32)
        self.sXB = A("sXB", [128, 512], BF16)
        self.sXB2 = A("sXB2", [128, 512], BF16)
        self.sROW = A("sROW", [1, 512], F32)
        self.sROW2 = A("sROW2", [1, 512], F32)
        self.sROW3 = A("sROW3", [1, 512], F32)
        self.sRH = A("sRH", [1, 512], BF16)
        self.sRL = A("sRL", [1, 512], BF16)
        self.sRED = A("sRED", [1, 128], F32)
        self.sSCR = A("sSCR", [1, 16], F32)
        self.sWIB = [A(f"sWIB{i}", [128, 2048], BF16) for i in range(2)]
        self.sWOB = self.sWIB
        self.sXTILE = A("sXTILE", [128, 512], F32)
        self.sXBCB = A("sXBCB", [128, 515], BF16)
        self.sCACC = A("sCACC", [128, 512], F32)
        self.sCH1 = self.sCACC.ap()[:, 0:256].bitcast(BF16)
        self.sCH2 = self.sCACC.ap()[:, 256:512].bitcast(BF16)
        self.sMXC = A("sMXC", [128, 8], F32)
        self.sSC = A("sSC", [128, 8], F32)
        seg_base = self._off
        self.sS = A("sS", [128, D_INNER], F32)
        self.sSBF = A("sSBF", [128, D_INNER], BF16)
        self.sHN = A("sHN", [128, NDC * T], BF16)
        self.sZ = A("sZ", [128, NZC * T], BF16)
        self.sCO = A("sCO", [128, NXC * T], BF16)
        self.sY = A("sY", [128, NZC * T], BF16)
        self.sDT = A("sDT", [NH, T], F32)
        self.sAA = A("sAA", [NH, T], F32)
        self.sL = [A(f"sL{i}", [NH, 128], F32) for i in range(2)]
        self.sHL = [A(f"sHL{i}", [128, 128], BF16) for i in range(2)]
        self.sMLH = A("sMLH", [98, 16 * 128], BF16)
        self.sLEND = [A(f"sLEND{i}", [NH, 1], F32) for i in range(2)]
        self.sGCOL = [A(f"sGCOL{i}", [NH, 1], F32) for i in range(2)]
        self.sEV2 = A("sEV2", [NH, 128], F32)
        self.sDV = A("sDV", [128, 128], BF16)
        self.sDVT = A("sDVT", [128, 128], BF16)
        self.sEXPL = [A(f"sEXPL{i}", [NH, 128], BF16) for i in range(2)]
        self.sGAM = [A(f"sGAM{i}", [NH, 128], BF16) for i in range(2)]
        self.sG = [A(f"sG{i}", [128, 128], BF16) for i in range(2)]
        self.sBT = [A(f"sBT{i}", [128, 128], BF16) for i in range(2)]
        self.sLTB = [A(f"sLTB{i}", [128, 128], BF16) for i in range(2)]
        xt_base = self._off
        self.sXT = [A(f"sXT{i}", [128, D_INNER], BF16) for i in range(2)]
        self.sWXT = A("sWXT", [128, D_INNER], BF16)
        self.sMG2 = [A(f"sMG{i}", [128, 16 * 128], BF16) for i in range(2)]
        self.sDCW = A("sDCW", [128, NXC * DCONV * 128], BF16, at=xt_base)
        seg_top = self._off
        if cfg.do_ffn:
            off = seg_base
            def F(name, shape, dtype):
                nonlocal off
                t = nc.alloc_sbuf_tensor_at(name, shape, dtype, offset=off)
                off += (shape[1] * mybir.dt.size(dtype) + 31) // 32 * 32
                return t
            self.sFW = [F(f"sFW{i}", [128, 4096], BF16) for i in range(2)]
            self.sBLK = F("sBLK", [128, 96], F32)
            self.sRS = F("sRS", [128, 96], F32)
            self.sDQ1 = F("sDQ1", [128, 3072], F32)
            self.sO = F("sO", [128, 24 * 512], BF16)
            self.sT1 = F("sT1", [128, 512], BF16)
            self.sT2 = F("sT2", [128, 512], BF16)
            self.sT3 = F("sT3", [128, 512], BF16)
            assert off <= seg_top, (off, seg_top)

        P = nc.alloc_psum_tensor
        self.pPS = P("pPS", [128, 4096], F32)

    def pe_transpose_bf(self, src_ap, dst_sbuf_ap, rkeys, wkeys):
        sc = self.sc
        n_p, n_f = src_ap.shape[0], src_ap.shape[1]
        sc.ops("PE", [lambda e: e.transpose(out=self.pPS.ap()[:, 3968:4096].bitcast(BF16)[0:n_f, 0:n_p], in_=src_ap,
                                            identity=self.sIDB[0:n_p, 0:n_p])],
               reads=list(rkeys) + ["cIDB"], writes=["P7"])
        sc.op("DVE", lambda e: e.tensor_copy(out=dst_sbuf_ap, in_=self.pPS.ap()[:, 3968:4096].bitcast(BF16)[0:n_f, 0:n_p]),
              reads=["P7"], writes=list(wkeys))

    def row_hilo(self, row_ap, T, rkey):
        sc = self.sc
        sc.op("DVE", lambda e: e.tensor_copy(out=self.sRH[0:1, 0:T], in_=row_ap),
              reads=[rkey], writes=["rhl"])
        sc.op("DVE", lambda e: e.tensor_tensor(out=self.sRL[0:1, 0:T], in0=row_ap,
                                               in1=self.sRH[0:1, 0:T], op=ALU.subtract),
              reads=[rkey, "rhl"], writes=["rhl"])

    def bcast_row(self, lhsT_ap, lkeys, T, psum_ap, pkey):
        sc = self.sc
        sc.ops("PE", [lambda e: e.matmul(out=psum_ap, lhsT=lhsT_ap,
                                         rhs=self.sRH[0:1, 0:T], start=True, stop=False),
                      lambda e: e.matmul(out=psum_ap, lhsT=lhsT_ap,
                                         rhs=self.sRL[0:1, 0:T], start=False, stop=True)],
               reads=list(lkeys) + ["rhl"], writes=[pkey])

    def load_consts(self):
        sc = self.sc
        pairs = [(self.sESEL, self.c_esel, "cESEL"),
                 (self.sIDB, self.c_idb, "cIDB"), (self.sONES2, self.c_ones2, "cONES2"),
                 (self.sONES64, self.c_ones64, "cONES64"),
                 (self.sONESC, self.c_onescol, "cONESC")]
        for dst, src, key in pairs:
            sc.dma("SP", lambda e, d=dst, s=src: e.dma_start(out=d[:], in_=s), writes=[key])
        for i, csrc in ((0, self.c_e64x), (1, self.c_e64xb)):
            sc.dma("SP", lambda e, i=i, csrc=csrc: e.dma_start(
                out=self.sE64X[i].ap()[0:64, :], in_=csrc[:]), writes=["cE64X"])
            sc.dma("SP", lambda e, i=i: e.dma_start(
                out=self.sE64X[i].ap()[66:98, :], in_=self.c_trid[:]), writes=["cE64X"])
        sc.dma("SP", lambda e: e.dma_start(out=self.sMLH.ap()[66:98, :],
                                           in_=self.c_tri16[:]), writes=["cMLH"])
        sc.op("DVE", lambda e: e.memset(self.sMLH.ap()[64:66, :], 1.0),
              writes=["cMLH"])
        sc.dma("POOL", lambda e: e.dma_start(out=self.sFNWC[:], in_=self.fnwR[:]),
               writes=["fnwb"])
        sc.op("DVE", lambda e: e.memset(self.sZERO64[:], 0.0), writes=["zero64"])
        sc.op("DVE", lambda e: e.memset(self.sEPS[:], LN_EPS), writes=["ceps"])
        sc.op("DVE", lambda e: e.memset(self.sONE1[:], 1.0), writes=["cone"])
        sc.op("DVE", lambda e: e.memset(self.sCARRY[:], 0.0), writes=["carry"])
        sc.op("DVE", lambda e: e.memset(self.sMXC[:], 0.0), writes=["mxc"])
        sc.op("DVE", lambda e: e.memset(self.sSCR[:], 0.0), writes=["scr"])
        sc.dma("ACT", lambda e: e.dma_start(out=self.cc_in, in_=self.sSCR[0:1, 0:4]),
               reads=["scr"], writes=["ccin"])

    def preconvert_weights(self):
        sc = self.sc
        for l in range(self.cfg.nlayer):
            sc.dma("POOL", lambda e, l=l: e.dma_start(out=self.wi_bf[l], in_=self.wi[l]),
                   writes=[f"wi_bf:{l}"])
            sc.dma("POOL", lambda e, l=l: e.dma_start(out=self.wo_bf[l], in_=self.wo[l]),
                   writes=[f"wo_bf:{l}"])
            sc.dma("POOL", lambda e, l=l: e.dma_start(out=self.dcw_bf[l], in_=self.dcwL[l]),
                   writes=[f"dcw_bf:{l}"])

    def load_layer_params(self, l):
        sc = self.sc
        for dst, src, key in [(self.sCB, self.cbL[l], "cb"),
                              (self.sDTB, self.dtbL[l], "dtb"), (self.sALOG, self.alogL[l], "alog"),
                              (self.sDCOL, self.dL[l], "dcol")]:
            sc.dma("SP", lambda e, d=dst, s=src: e.dma_start(out=d[:], in_=s), writes=[key])
        sc.dma("POOL", lambda e, l=l: e.dma_start(out=self.sGNWC[:], in_=self.gnwR[l]),
               writes=["gnwb"])
        sc.dma("POOL", lambda e, l=l: e.dma_start(out=self.sBNWC[:], in_=self.bnwR[l]),
               writes=["bnwb"])
        sc.op("ACT", lambda e: e.activation(out=self.sACOL[:], in_=self.sALOG[:], func=AF.Exp),
              reads=["alog"], writes=["acol"])
        sc.op("DVE", lambda e: e.tensor_scalar(out=self.sACOL[:], in0=self.sACOL[:],
                                               scalar1=-1.0, scalar2=None, op0=ALU.mult),
              reads=["acol"], writes=["acol"])
        sc.op("DVE", lambda e: e.tensor_copy(out=self.sDBF[:], in_=self.sDCOL[:]),
              reads=["dcol"], writes=["dbf"])
        sc.ops("PE", [(lambda e, k=k: e.matmul(out=self.pPS[:, 3072 + k:3072 + k + 1],
                                               lhsT=self.sESEL[:, 128 * k:128 * (k + 1)],
                                               rhs=self.sDBF[:], start=True, stop=True))
                      for k in range(NZC)],
               reads=["dbf", "cESEL"], writes=["P6"])
        sc.op("DVE", lambda e: e.tensor_copy(out=self.sDEXP[:], in_=self.pPS[:, 3072 + 0:3072 + NZC]),
              reads=["P6"], writes=["dexp"])
        sc.op("DVE", lambda e: e.memset(self.sS[:], 0.0), writes=["S"])
        sc.op("DVE", lambda e: e.memset(self.sCARRY[:], 0.0), writes=["carry"])

    def rms_rows(self, T, get_tile, rkeys, nchunks, mean_div, prep=None):
        sc = self.sc
        for k in range(nchunks):
            if prep is not None:
                prep(k)
            ap = get_tile(k)
            sc.op("ACT", lambda e, ap=ap: e.activation(out=self.sXB[:, 0:T], in_=ap,
                                                       func=AF.Square),
                  reads=rkeys(k), writes=["sXB"])
            sc.ops("PE", [lambda e, k=k: e.matmul(out=self.pPS[0:1, 3072 + 0:3072 + T],
                                                  lhsT=self.sONESC[:], rhs=self.sXB[:, 0:T],
                                                  start=(k == 0), stop=(k == nchunks - 1))],
                   reads=["sXB", "cONESC"], writes=["P6"])
        sc.op("ACT", lambda e: e.activation(out=self.sROW2[0:1, 0:T], in_=self.pPS[0:1, 3072 + 0:3072 + T],
                                            func=AF.Ln, scale=1.0 / mean_div,
                                            bias=self.sEPS[0:1]),
              reads=["P6", "ceps"], writes=["rowB0"])
        sc.op("ACT", lambda e: e.activation(out=self.sROW[0:1, 0:T], in_=self.sROW2[0:1, 0:T],
                                            func=AF.Exp, scale=-0.5),
              reads=["rowB0"], writes=["rowA0"])

    def ln_rows(self, T, get_tile, rkeys, prep=None):
        sc = self.sc
        for k in range(NDC):
            if prep is not None:
                prep(k)
            ap = get_tile(k)
            sc.op("DVE", lambda e, ap=ap: e.tensor_copy(out=self.sXB2[:, 0:T], in_=ap),
                  reads=rkeys(k), writes=["sXB2"])
            sc.op("ACT", lambda e, ap=ap: e.activation(out=self.sXB[:, 0:T], in_=ap,
                                                       func=AF.Square),
                  reads=rkeys(k), writes=["sXB"])
            sc.ops("PE", [lambda e, k=k: e.matmul(out=self.pPS[0:1, 3072 + 0:3072 + T], lhsT=self.sONESC[:],
                                                  rhs=self.sXB2[:, 0:T], start=(k == 0),
                                                  stop=(k == NDC - 1))],
                   reads=["sXB2", "cONESC"], writes=["P6"])
            sc.ops("PE", [lambda e, k=k: e.matmul(out=self.pPS[0:1, 3584:3584 + T],
                                                  lhsT=self.sONESC[:],
                                                  rhs=self.sXB[:, 0:T], start=(k == 0),
                                                  stop=(k == NDC - 1))],
                   reads=["sXB", "cONESC"], writes=["P7"])
        sc.op("DVE", lambda e: e.tensor_scalar(out=self.sROW3[0:1, 0:T],
                                               in0=self.pPS[0:1, 3072 + 0:3072 + T],
                                               scalar1=1.0 / D_MODEL, scalar2=None, op0=ALU.mult),
              reads=["P6"], writes=["rowA1"])
        sc.op("DVE", lambda e: e.tensor_scalar(out=self.sROW2[0:1, 0:T],
                                               in0=self.pPS[0:1, 3584:3584 + T],
                                               scalar1=1.0 / D_MODEL, scalar2=None, op0=ALU.mult),
              reads=["P7"], writes=["rowB0"])
        sc.op("DVE", lambda e: e.tensor_tensor(out=self.sROW[0:1, 0:T],
                                               in0=self.sROW3[0:1, 0:T],
                                               in1=self.sROW3[0:1, 0:T], op=ALU.mult),
              reads=["rowA1"], writes=["rowA0"])
        sc.op("DVE", lambda e: e.tensor_tensor(out=self.sROW2[0:1, 0:T],
                                               in0=self.sROW2[0:1, 0:T],
                                               in1=self.sROW[0:1, 0:T], op=ALU.subtract),
              reads=["rowB0", "rowA0"], writes=["rowB0"])
        sc.op("ACT", lambda e: e.activation(out=self.sROW2[0:1, 0:T], in_=self.sROW2[0:1, 0:T],
                                            func=AF.Ln, bias=self.sEPS[0:1]),
              reads=["rowB0", "ceps"], writes=["rowB0"])
        sc.op("ACT", lambda e: e.activation(out=self.sROW[0:1, 0:T], in_=self.sROW2[0:1, 0:T],
                                            func=AF.Exp, scale=-0.5),
              reads=["rowB0"], writes=["rowA0"])
        sc.op("DVE", lambda e: e.tensor_tensor(out=self.sROW2[0:1, 0:T],
                                               in0=self.sROW3[0:1, 0:T],
                                               in1=self.sROW[0:1, 0:T], op=ALU.mult),
              reads=["rowA1", "rowA0"], writes=["rowB0"])

    def ln_bcasts(self, T):
        self.row_hilo(self.sROW[0:1, 0:T], T, "rowA0")
        self.bcast_row(self.sONES2[0:1, :], ["cONES2"], T, self.pPS[:, 1024 + 0:1024 + T], "P2")
        self.row_hilo(self.sROW2[0:1, 0:T], T, "rowB0")
        self.bcast_row(self.sONES2[0:1, :], ["cONES2"], T, self.pPS[:, 1024 + 512:1024 + 512 + T], "P3")

    def phase_init_ln(self):
        cfg, sc = self.cfg, self.sc
        L = cfg.L
        for j in range(L // 512):
            t0 = 512 * j

            def prep(k, t0=t0):
                sc.dma("SP", lambda e, k=k: e.dma_start(
                    out=self.sXTILE[:],
                    in_=self.x_fm[128 * k:128 * (k + 1), t0:t0 + 512]),
                    writes=["xt"])
            self.ln_rows(512, lambda k: self.sXTILE[:], lambda k: ["xt"], prep=prep)
            self.ln_bcasts(512)
            for k in range(NDC):
                prep(k)
                hsl = self.h[:, k * L + t0: k * L + t0 + 512]
                sc.op("DVE", lambda e: e.tensor_tensor(
                    out=self.sEVAC[:, 0:512], in0=self.sXTILE[:],
                    in1=self.pPS[:, 1024 + 0:1024 + 512], op=ALU.mult),
                    reads=["xt", "P2"], writes=["evac"])
                sc.op("DVE", lambda e, hsl=hsl: e.tensor_tensor(
                    out=hsl, in0=self.sEVAC[:, 0:512], in1=self.pPS[:, 1024 + 512:1024 + 1024],
                    op=ALU.subtract),
                    reads=["evac", "P3"], writes=[f"h:{k}:{j}"])

    def phase_A(self, l, seg):
        cfg, sc = self.cfg, self.sc
        T, L = cfg.TSEG, cfg.L
        t0 = seg * T
        hsl = lambda k: self.h[:, k * L + t0: k * L + t0 + T]
        self.rms_rows(T, hsl, lambda k: [f"h:{k}:{seg}"], NDC, D_MODEL)
        self.row_hilo(self.sROW[0:1, 0:T], T, "rowA0")
        self.bcast_row(self.sONES2[0:1, :], ["cONES2"], T,
                       self.pPS[:, 1024 + 0:1024 + T], "P2")
        for k in range(NDC):
            hap = hsl(k)
            sc.op("DVE", lambda e, k=k, hap=hap: e.scalar_tensor_tensor(
                out=self.sHN[:, k * T:(k + 1) * T], in0=hap,
                scalar=self.sBNWC[:, k:k + 1], in1=self.pPS[:, 1024 + 0:1024 + T],
                op0=ALU.mult, op1=ALU.mult),
                reads=[f"h:{k}:{seg}", "P2", "bnwb"], writes=[f"hn:{k}"])
        hn_keys = [f"hn:{k}" for k in range(NDC)]
        sc.dma("SP", lambda e, l=l: e.dma_start(
            out=self.sDCW.ap().rearrange("p (m n) -> p m n", m=NXC * DCONV),
            in_=self.dcw_bf[l].rearrange("m p n -> p m n")),
            reads=[f"dcw_bf:{l}", "xst:0", "xst:1", "dxt", "wxt", "mg:0", "mg:1"],
            writes=["dcw"])
        for pos, np_ in enumerate([17, 16] + list(range(8, 16)) + list(range(8))):
            buf = pos % 2
            wkey = f"wib:{buf}"
            w = 256 if np_ < 17 else 64
            nof = 256 * np_
            sc.dma("SP", lambda e, buf=buf, w=w, nof=nof, l=l: e.dma_start(
                out=self.sWIB[buf].ap()[:, 0:NDC * w].rearrange(
                    "p (k n) -> p k n", k=NDC),
                in_=self.wi_bf[l][:, nof:nof + w].rearrange("(k p) n -> p k n", p=128)),
                reads=[f"wi_bf:{l}"], writes=[wkey])
            for sub in range(2 if np_ < 17 else 1):
                n = 2 * np_ + sub
                ncols = 128 if n < 34 else 64
                pb = n % 2
                pk = f"P{pb}"
                psl = self.pPS[:, 512 * pb:512 * pb + T]
                sc.ops("PE", [(lambda e, k=k, buf=buf, w=w, sub=sub, ncols=ncols, psl=psl:
                               e.matmul(out=psl[0:ncols, :] if ncols < 128 else psl,
                                        lhsT=self.sWIB[buf][:, w * k + 128 * sub:
                                                            w * k + 128 * sub + ncols],
                                        rhs=self.sHN[:, k * T:(k + 1) * T],
                                        start=(k == 0), stop=(k == NDC - 1)))
                              for k in range(NDC)],
                       reads=[wkey] + hn_keys, writes=[pk])
                if n < NZC:
                    sc.op("ACT", lambda e, n=n, psl=psl: e.activation(
                        out=self.sZ[:, n * T:(n + 1) * T], in_=psl, func=AF.Silu),
                        reads=[pk], writes=[f"z:{n}"])
                elif n < 34:
                    self.conv_chunk(n - NZC, psl, pk)
                else:
                    self.dt_chunk(psl, pk)
        sc.op("DVE", lambda e: e.tensor_scalar(out=self.sAA[:, 0:T], in0=self.sDT[:, 0:T],
                                               scalar1=self.sACOL[:], scalar2=None,
                                               op0=ALU.mult),
              reads=["dt", "acol"], writes=["aa"])

    def conv_chunk(self, j, psl, pk):
        sc, T = self.sc, self.cfg.TSEG
        xb = self.sXBCB
        sc.op("DVE", lambda e, j=j: e.tensor_copy(out=xb[:, 0:3],
                                                  in_=self.sCARRY[:, 3 * j:3 * j + 3]),
              reads=["carry"], writes=["xbcb"])
        sc.op("DVE", lambda e: e.tensor_copy(out=xb[:, 3:3 + T], in_=psl),
              reads=[pk], writes=["xbcb"])
        sc.op("DVE", lambda e, j=j: e.tensor_copy(out=self.sCARRY[:, 3 * j:3 * j + 3],
                                                  in_=xb[:, T:T + 3]),
              reads=["xbcb"], writes=["carry"])
        cb_bank = 2 + (j % 2)
        cps = self.pPS[:, 512 * cb_bank:512 * cb_bank + T]

        def mk(e, j=j, cps=cps):
            r = None
            for kk in range(DCONV):
                r = e.matmul(out=cps, lhsT=self.sDCW[:, 128 * (4 * j + kk):
                                                     128 * (4 * j + kk) + 128],
                             rhs=xb[:, kk:kk + T],
                             start=(kk == 0), stop=(kk == DCONV - 1))
            return r
        sc.ops("PE", [mk], reads=["dcw", "xbcb"], writes=[f"P{cb_bank}"])
        sc.op("ACT", lambda e, j=j, cps=cps: e.activation(
            out=self.sCO[:, j * T:(j + 1) * T], in_=cps, func=AF.Silu,
            bias=self.sCB[:, j:j + 1]),
            reads=[f"P{cb_bank}", "cb"],
            writes=[f"co:{j}"] + [f"cox:{cc}" for cc in range(4)])

    def dt_chunk(self, psl, pk):
        sc, T = self.sc, self.cfg.TSEG
        t1 = self.sEVAC[0:64, 0:T]
        t4 = self.sXTILE[0:64, 0:T]
        ps = psl[0:64, :]
        sc.op("ACT", lambda e: e.activation(out=t1, in_=ps, func=AF.Abs, bias=self.sDTB[:]),
              reads=[pk, "dtb"], writes=["evac"])
        sc.op("ACT", lambda e: e.activation(out=t1, in_=t1, func=AF.Exp, scale=-1.0),
              reads=["evac"], writes=["evac"])
        sc.op("ACT", lambda e: e.activation(out=t1, in_=t1, func=AF.Ln,
                                            bias=self.sONE1[0:64]),
              reads=["evac", "cone"], writes=["evac"])
        sc.op("ACT", lambda e: e.activation(out=t4, in_=ps, func=AF.Relu, bias=self.sDTB[:]),
              reads=[pk, "dtb", "xt"], writes=["xt"])
        sc.op("DVE", lambda e: e.tensor_tensor(out=self.sDT[:, 0:T], in0=t1, in1=t4,
                                               op=ALU.add),
              reads=["evac", "xt"], writes=["dt"])

    def phase_B_prep_front(self, l, seg, c):
        cfg, sc = self.cfg, self.sc
        T, C = cfg.TSEG, cfg.C
        cof = c * C
        cbuf = c % 2
        sL, sHL = self.sL[cbuf], self.sHL[cbuf]
        kL, kHL = f"Lc:{cbuf}", f"hl:{cbuf}"
        kEND, kGC = f"Lend:{cbuf}", f"gcol:{cbuf}"
        sc.op("DVE", lambda e, cof=cof, sL=sL: e.tensor_tensor_scan(
            out=sL[:], data0=self.sAA[:, cof:cof + C], data1=self.sZERO64[:],
            initial=0.0, op0=ALU.add, op1=ALU.add),
            reads=["aa", "zero64"], writes=[kL])
        sc.op("DVE", lambda e, sL=sL, sHL=sHL: e.tensor_copy(
            out=sHL[0:64, :], in_=sL[:]), reads=[kL], writes=[kHL])
        sc.op("DVE", lambda e, sL=sL, sHL=sHL: e.tensor_tensor(
            out=sHL[64:128, :], in0=sL[:], in1=sHL[0:64, :], op=ALU.subtract),
            reads=[kL, kHL], writes=[kHL])
        for h2 in range(2):
            b0 = 8 * cbuf + 4 * h2
            wk = [f"mlh:{b0 + sl}" for sl in range(4)]
            out_hi = self.sMLH.ap()[0:32, 128 * b0:128 * b0 + 512].rearrange(
                "p (s n) -> p s n", s=4)
            out_lo = self.sMLH.ap()[32:64, 128 * b0:128 * b0 + 512].rearrange(
                "p (s n) -> p s n", s=4)
            sc.op("DVE", lambda e, out_hi=out_hi, h2=h2, sHL=sHL: e.tensor_copy(
                out=out_hi,
                in_=sHL[32 * h2:32 * h2 + 32, :].unsqueeze(1).to_broadcast((32, 4, 128))),
                reads=[kHL], writes=wk)
            sc.op("DVE", lambda e, out_lo=out_lo, h2=h2, sHL=sHL: e.tensor_copy(
                out=out_lo,
                in_=sHL[64 + 32 * h2:64 + 32 * h2 + 32, :].unsqueeze(1).to_broadcast((32, 4, 128))),
                reads=[kHL], writes=wk)
        sc.op("DVE", lambda e, sL=sL, cbuf=cbuf: e.tensor_copy(
            out=self.sLEND[cbuf][:], in_=sL[:, C - 1:C]), reads=[kL], writes=[kEND])
        sc.op("ACT", lambda e, cbuf=cbuf: e.activation(
            out=self.sGCOL[cbuf][:], in_=self.sLEND[cbuf][:], func=AF.Exp),
            reads=[kEND], writes=[kGC])
        sc.op("DVE", lambda e, cbuf=cbuf: e.tensor_scalar(
            out=self.sGAM[cbuf][:], in0=self.sONES64[:],
            scalar1=self.sGCOL[cbuf][:], scalar2=None, op0=ALU.mult),
            reads=["cONES64", kGC], writes=[f"gam:{cbuf}"])
        sc.op("ACT", lambda e, sL=sL, cbuf=cbuf: e.activation(
            out=self.sEXPL[cbuf][:], in_=sL[:], func=AF.Exp),
            reads=[kL], writes=[f"expl:{cbuf}"])
        sc.op("ACT", lambda e, sL=sL, cbuf=cbuf: e.activation(
            out=self.sEV2[:], in_=sL[:], func=AF.Exp,
            scale=-1.0, bias=self.sLEND[cbuf][:]),
            reads=[kL, kEND], writes=["ev2"])
        sc.op("DVE", lambda e, cof=cof: e.tensor_copy(
            out=self.sDV.ap()[0:64, :], in_=self.sDT[:, cof:cof + C]),
            reads=["dt"], writes=["dv"])
        sc.op("DVE", lambda e, cof=cof: e.tensor_tensor(
            out=self.sDV.ap()[64:128, :], in0=self.sEV2[:],
            in1=self.sDT[:, cof:cof + C], op=ALU.mult),
            reads=["ev2", "dt"], writes=["dv"])
        co3 = self.sCO.ap().rearrange("p (j t) -> p j t", j=NXC)
        xs_src = co3[:, 0:16, cof:cof + C]
        xbounce = self.xs_b[cbuf]
        sc.dma("ACT", lambda e, xs_src=xs_src, xbounce=xbounce: e.dma_start(
            out=xbounce.rearrange("(j p) t -> p j t", p=128), in_=xs_src),
            reads=[f"cox:{c}"], writes=[f"xsb:{cbuf}"])
        sc.dma("ACT", lambda e, cbuf=cbuf: e.dma_start_transpose(
            out=self.sXT[cbuf][:], in_=self.xs_b[cbuf]),
            reads=[f"xsb:{cbuf}"], writes=[f"xst:{cbuf}"])

    def phase_B_prep_back(self, l, seg, c):
        cfg, sc = self.cfg, self.sc
        T, C = cfg.TSEG, cfg.C
        cof = c * C
        cbuf = c % 2
        sHL = self.sHL[cbuf]
        kHL = f"hl:{cbuf}"
        co3 = self.sCO.ap().rearrange("p (j t) -> p j t", j=NXC)
        self.pe_transpose_bf(self.sDV[:], self.sDVT[:], ["dv"], ["dvt"])
        self.pe_transpose_bf(sHL[:], self.sLTB[cbuf][:], [kHL], [f"ltb:{cbuf}"])
        sc.op("DVE", lambda e, cbuf=cbuf: e.tensor_tensor(
            out=self.sWXT.ap().rearrange("p (h q) -> p h q", h=NH),
            in0=self.sXT[cbuf].ap().rearrange("p (h q) -> p h q", h=NH),
            in1=bc_in(self.sDVT.ap()[:, 64:128], HD), op=ALU.mult),
            reads=[f"xst:{cbuf}", "dvt"], writes=["wxt"])
        sc.op("DVE", lambda e, cbuf=cbuf: e.tensor_tensor(
            out=self.sXT[cbuf].ap().rearrange("p (h q) -> p h q", h=NH),
            in0=self.sXT[cbuf].ap().rearrange("p (h q) -> p h q", h=NH),
            in1=bc_in(self.sDVT.ap()[:, 0:64], HD), op=ALU.mult),
            reads=[f"xst:{cbuf}", "dvt", "wxt"], writes=["dxt", f"xst:{cbuf}"])
        bsl = co3[:, 16, cof:cof + C]
        csl = co3[:, 17, cof:cof + C]
        sc.ops("PE", [lambda e, bsl=bsl, csl=csl: e.matmul(
            out=self.pPS[:, 3584 + 0:3584 + 128], lhsT=bsl, rhs=csl, start=True, stop=True)],
            reads=[f"cox:{c}"], writes=["P7"])
        sc.op("DVE", lambda e, cbuf=cbuf: e.tensor_copy(
            out=self.sG[cbuf][:], in_=self.pPS[:, 3584 + 0:3584 + 128]),
            reads=["P7"], writes=[f"G:{cbuf}"])
        self.pe_transpose_bf(bsl, self.sBT[cbuf][:], [f"cox:{c}"], [f"BT:{cbuf}"])
        sc.op("DVE", lambda e, cof=cof, co3=co3: e.tensor_tensor(
            out=co3[:, 0:16, cof:cof + C], in0=co3[:, 0:16, cof:cof + C],
            in1=bc_in(self.sDEXP[:, 0:16], C), op=ALU.mult),
            reads=[f"cox:{c}", "dexp", f"xsb:{cbuf}"], writes=[f"cox:{c}"])

    def phase_B_compute(self, l, seg, c):
        cfg, sc = self.cfg, self.sc
        sc.op("DVE", lambda e: e.tensor_copy(out=self.sSBF[:], in_=self.sS[:]),
              reads=["S"], writes=["SBF"])
        self.state_update(c)
        if c + 1 < cfg.NCH:
            self.phase_B_prep_front(l, seg, c + 1)
        for hq in range(2):
            self.mask_quarter(c, hq)
            self.y_quarter(c, hq)
        if c + 1 < cfg.NCH:
            self.phase_B_prep_back(l, seg, c + 1)
        for hq in range(2, 4):
            self.mask_quarter(c, hq)
            self.y_quarter(c, hq)

    def mask_quarter(self, c, hq):
        sc = self.sc
        h2, s2 = hq // 2, hq % 2
        par = hq % 2
        cbuf = c % 2
        hgb = (4 * c + hq) % 2
        mgk = f"mg:{par}"
        sMG = self.sMG2[par]
        bb = self.lt_b[hgb]
        sLTB, kLTB = self.sLTB[cbuf], f"ltb:{cbuf}"
        sc.dma("ACT", lambda e, hq=hq, bb=bb, sLTB=sLTB: e.dma_start(
            out=bb[0], in_=sLTB[:, 16 * hq:16 * hq + 16]),
            reads=[kLTB], writes=[f"ltbd:{hgb}"])
        sc.dma("ACT", lambda e, hq=hq, bb=bb, sLTB=sLTB: e.dma_start(
            out=bb[1], in_=sLTB[:, 64 + 16 * hq:64 + 16 * hq + 16]),
            reads=[kLTB], writes=[f"ltbd:{hgb}"])
        sc.dma("ACT", lambda e, bb=bb, s2=s2: e.dma_start(
            out=self.sE64X[s2].ap()[64:66, :], in_=bb.rearrange("r t h -> r (t h)")),
            reads=[f"ltbd:{hgb}"], writes=[f"flat:{s2}"])
        for sl in range(4):
            po = self.pPS[:, 1024 + 512 * (sl % 2):1024 + 512 * (sl % 2) + 512]
            i = 8 * cbuf + 4 * h2 + sl
            sc.ops("PE", [lambda e, po=po, i=i, sl=sl, s2=s2: e.matmul(
                out=po, lhsT=self.sMLH.ap()[0:98, 128 * i:128 * (i + 1)],
                rhs=self.sE64X[s2][0:98, 512 * sl:512 * (sl + 1)],
                start=True, stop=True)],
                reads=[f"mlh:{i}", "cMLH", "cE64X", f"flat:{s2}"],
                writes=[f"P{2 + sl % 2}"])
            sc.op("ACT", lambda e, sl=sl, po=po, sMG=sMG: e.activation(
                out=sMG[:, 512 * sl:512 * (sl + 1)], in_=po, func=AF.Exp),
                reads=[f"P{2 + sl % 2}"], writes=[mgk])
            mg3s = sMG.ap()[:, 512 * sl:512 * (sl + 1)].rearrange(
                "p (t h) -> p t h", h=16)
            sc.op("DVE", lambda e, mg3s=mg3s, sl=sl, cbuf=cbuf: e.tensor_tensor(
                out=mg3s, in0=mg3s,
                in1=bc_in(self.sG[cbuf].ap()[:, 32 * sl:32 * (sl + 1)], 16),
                op=ALU.mult),
                reads=[mgk, f"G:{cbuf}"], writes=[mgk])

    def y_quarter(self, c, hq):
        sc, cfg = self.sc, self.cfg
        T, C = cfg.TSEG, cfg.C
        cof = c * C
        cbuf = c % 2
        par = hq % 2
        mgk = f"mg:{par}"
        mg3 = self.sMG2[par].ap().rearrange("p (t h) -> p t h", h=16)
        co3 = self.sCO.ap().rearrange("p (j t) -> p j t", j=NXC)
        z3 = self.sZ.ap().rearrange("p (j t) -> p j t", j=NZC)
        y3 = self.sY.ap().rearrange("p (j t) -> p j t", j=NZC)
        csl = co3[:, 17, cof:cof + C]
        ks = list(range(4 * hq, 4 * hq + 4))
        pI = self.pPS[:, 2048:2560]
        pN = self.pPS[:, 2560:3072]
        pX = self.pPS[:, 3072:3584]
        sc.ops("PE", [(lambda e, k=k, i=i: e.matmul(
            out=pX[:, 128 * i:128 * (i + 1)],
            lhsT=self.sESEL[:, 128 * k:128 * (k + 1)], rhs=self.sEXPL[cbuf][:],
            start=True, stop=True)) for i, k in enumerate(ks)],
            reads=["cESEL", f"expl:{cbuf}"], writes=["P6"])
        sc.ops("PE", [(lambda e, k=k, i=i: e.matmul(
            out=pN[:, 128 * i:128 * (i + 1)],
            lhsT=self.sSBF[:, 128 * k:128 * (k + 1)], rhs=csl,
            start=True, stop=True)) for i, k in enumerate(ks)],
            reads=["SBF", f"cox:{c}"], writes=["P5"])

        def intra(e, ks=ks, hq=hq, cbuf=cbuf):
            r = None
            for i, k in enumerate(ks):
                for jj in range(4):
                    h_loc = (4 * k + jj) - 16 * hq
                    r = e.matmul(out=pI[32 * jj:32 * jj + 32, 128 * i:128 * (i + 1)],
                                 lhsT=self.sXT[cbuf][:, 128 * k + 32 * jj:
                                                     128 * k + 32 * jj + 32],
                                 rhs=mg3[:, :, h_loc],
                                 start=True, stop=True, tile_position=(0, 32 * jj))
            return r
        sc.ops("PE", [intra], reads=["dxt", mgk, f"xst:{cbuf}"], writes=["P4"])
        k0 = ks[0]
        xs_sl = co3[:, k0:k0 + 4, cof:cof + C]
        z_sl = z3[:, k0:k0 + 4, cof:cof + C]
        y_sl = y3[:, k0:k0 + 4, cof:cof + C]
        ch13 = self.sCH1.rearrange("p (j t) -> p j t", j=4)
        ch23 = self.sCH2.rearrange("p (j t) -> p j t", j=4)
        xs_keys = [f"cox:{c}"]
        z_keys = [f"z:{k}" for k in ks]
        y_keys = [f"y:{k}" for k in ks]
        sc.op("ACT", lambda e, pX=pX: e.activation(out=self.sCH1, in_=pX, func=AF.Copy),
              reads=["P6", "cacc"], writes=["ch1", "cacc"])
        sc.op("DVE", lambda e, pN=pN: e.tensor_tensor(out=self.sCH2, in0=pN,
                                                      in1=self.sCH1, op=ALU.mult),
              reads=["P5", "ch1", "cacc"], writes=["ch2", "cacc"])
        sc.op("DVE", lambda e, pI=pI: e.tensor_tensor(out=self.sCH1, in0=pI,
                                                      in1=self.sCH2, op=ALU.add),
              reads=["P4", "ch2", "cacc"], writes=["ch1", "cacc"])
        sc.op("POOL", lambda e, xs_sl=xs_sl, ch13=ch13, ch23=ch23: e.tensor_tensor(
            out=ch23, in0=ch13, in1=xs_sl, op=ALU.add),
            reads=["ch1", "cacc"] + xs_keys, writes=["ch2", "cacc"])
        sc.op("POOL", lambda e, z_sl=z_sl, y_sl=y_sl, ch23=ch23: e.tensor_tensor(
            out=y_sl, in0=ch23, in1=z_sl, op=ALU.mult),
            reads=["ch2", "cacc"] + z_keys, writes=y_keys)

    def state_update(self, c):
        sc = self.sc
        cbuf = c % 2
        for sl in range(4):
            pA = self.pPS[:, 0:512]
            pB = self.pPS[:, 512:1024]
            esl = self.sESEL[:, 512 * sl:512 * (sl + 1)]
            ssl = self.sS[:, 512 * sl:512 * (sl + 1)]
            wsl = self.sWXT[:, 512 * sl:512 * (sl + 1)]
            sc.ops("PE", [lambda e, esl=esl, pA=pA, cbuf=cbuf: e.matmul(
                out=pA, lhsT=self.sGAM[cbuf][:], rhs=esl, start=True, stop=True)],
                reads=[f"gam:{cbuf}", "cESEL"], writes=["P0"])
            sc.ops("PE", [lambda e, wsl=wsl, pB=pB, cbuf=cbuf: e.matmul(
                out=pB, lhsT=self.sBT[cbuf][:], rhs=wsl, start=True, stop=True)],
                reads=[f"BT:{cbuf}", "wxt"], writes=["P1"])
            sc.op("DVE", lambda e, ssl=ssl, pA=pA: e.tensor_tensor(
                out=ssl, in0=pA, in1=ssl, op=ALU.mult),
                reads=["P0", "S"], writes=["S"])
            sc.op("DVE", lambda e, ssl=ssl, pB=pB: e.tensor_tensor(
                out=ssl, in0=ssl, in1=pB, op=ALU.add),
                reads=["P1", "S"], writes=["S"])

    def phase_C(self, l, seg):
        cfg, sc = self.cfg, self.sc
        T, L = cfg.TSEG, cfg.L
        t0 = seg * T
        ysl = lambda k: self.sY[:, k * T:(k + 1) * T]
        self.rms_rows(T, ysl, lambda k: [f"y:{k}"], NZC, D_INNER)
        self.row_hilo(self.sROW[0:1, 0:T], T, "rowA0")
        self.bcast_row(self.sONES2[0:1, :], ["cONES2"], T,
                       self.pPS[:, 3072 + 0:3072 + T], "P6")
        for k in range(NZC):
            sc.op("DVE", lambda e, k=k: e.scalar_tensor_tensor(
                out=self.sY[:, k * T:(k + 1) * T], in0=self.sY[:, k * T:(k + 1) * T],
                scalar=self.sGNWC[:, k:k + 1], in1=self.pPS[:, 3072 + 0:3072 + T],
                op0=ALU.mult, op1=ALU.mult),
                reads=[f"y:{k}", "P6", "gnwb"], writes=[f"y:{k}"])
        y_keys = [f"y:{k}" for k in range(NZC)]
        allp = [f"P{b}" for b in range(8)]
        for f in range(NZC):
            buf = f % 2
            sc.dma("SP", lambda e, f=f, buf=buf, l=l: e.dma_start(
                out=self.sWOB[buf][:, 0:1024], in_=self.wo_bf[l][128 * f:128 * (f + 1), :]),
                reads=[f"wo_bf:{l}"], writes=[f"wib:{buf}"])

            def mk(e, f=f, buf=buf):
                r = None
                for dk in range(NDC):
                    po = self.pPS[:, 512 * dk:512 * dk + 512]
                    r = e.matmul(out=po, lhsT=self.sWOB[buf][:, 128 * dk:128 * (dk + 1)],
                                 rhs=self.sY[:, f * T:f * T + 512],
                                 start=(f == 0), stop=(f == NZC - 1))
                return r
            sc.ops("PE", [mk], reads=[f"wib:{buf}"] + y_keys, writes=allp)
        for dk in range(NDC):
            po = self.pPS[:, 512 * dk:512 * dk + 512]
            hsl = self.h[:, dk * L + t0: dk * L + t0 + 512]
            sc.op("DVE", lambda e, po=po, hsl=hsl: e.tensor_tensor(
                out=hsl, in0=hsl, in1=po, op=ALU.add),
                reads=[f"P{dk}", f"h:{dk}:{seg}"], writes=[f"h:{dk}:{seg}"])

    def phase_x1(self):
        cfg, sc = self.cfg, self.sc
        L = cfg.L
        for j in range(L // 512):
            t0 = 512 * j
            hsl = lambda k, t0=t0: self.h[:, k * L + t0: k * L + t0 + 512]
            self.rms_rows(512, hsl, lambda k, j=j: [f"h:{k}:{j}"], NDC, D_MODEL)
            self.row_hilo(self.sROW[0:1, 0:512], 512, "rowA0")
            self.bcast_row(self.sONES2[0:1, :], ["cONES2"], 512,
                           self.pPS[:, 1024 + 0:1024 + 512], "P2")
            for k in range(NDC):
                sc.dma("SP", lambda e, k=k, t0=t0: e.dma_start(
                    out=self.sXTILE[:],
                    in_=self.x_fm[128 * k:128 * (k + 1), t0:t0 + 512]),
                    writes=["xt"])
                hap = hsl(k)
                sc.op("DVE", lambda e, k=k, hap=hap: e.scalar_tensor_tensor(
                    out=self.sEVAC[:, 0:512], in0=hap,
                    scalar=self.sFNWC[:, k:k + 1], in1=self.pPS[:, 1024 + 0:1024 + 512],
                    op0=ALU.mult, op1=ALU.mult),
                    reads=[f"h:{k}:{j}", "P2", "fnwb"], writes=["evac"])
                sc.op("DVE", lambda e, hap=hap: e.tensor_tensor(
                    out=hap, in0=self.sEVAC[:, 0:512],
                    in1=self.sXTILE[:], op=ALU.add),
                    reads=["evac", "xt"], writes=[f"h:{k}:{j}"])

    def dequant_matrix(self, wsrc, wdst, n_i, n_d, wkey):
        sc = self.sc
        nb = n_d // 32
        for i in range(n_i // 128):
            sc.dma("SP", lambda e, i=i, wsrc=wsrc: e.dma_start(
                out=self.sDQ1[:, 0:n_d], in_=wsrc[128 * i:128 * (i + 1), :]),
                writes=["dq1"])
            w3 = self.sDQ1.ap()[:, 0:n_d].rearrange("p (b k) -> p b k", k=32)
            sc.op("DVE", lambda e, w3=w3: e.reduce_max(
                out=self.sBLK[:, 0:nb].unsqueeze(2), in_=w3, axis=AXX,
                apply_absolute_value=True),
                reads=["dq1"], writes=["blk"])
            sc.op("DVE", lambda e: e.tensor_scalar(
                out=self.sBLK[:, 0:nb], in0=self.sBLK[:, 0:nb], scalar1=1.0 / 7.0,
                scalar2=1e-12, op0=ALU.mult, op1=ALU.max),
                reads=["blk"], writes=["blk"])
            sc.op("DVE", lambda e: e.reciprocal(out=self.sRS[:, 0:nb], in_=self.sBLK[:, 0:nb]),
                  reads=["blk"], writes=["rs"])
            sc.op("DVE", lambda e, w3=w3: e.tensor_tensor(
                out=w3, in0=w3, in1=bc_in(self.sRS[:, 0:nb], 32), op=ALU.mult),
                reads=["dq1", "rs"], writes=["dq1"])
            sc.op("DVE", lambda e: e.tensor_scalar(
                out=self.sDQ1[:, 0:n_d], in0=self.sDQ1[:, 0:n_d], scalar1=MAGIC,
                scalar2=MAGIC, op0=ALU.add, op1=ALU.subtract),
                reads=["dq1"], writes=["dq1"])
            sc.op("DVE", lambda e, w3=w3: e.tensor_tensor(
                out=w3, in0=w3, in1=bc_in(self.sBLK[:, 0:nb], 32), op=ALU.mult),
                reads=["dq1", "blk"], writes=["dq1"])
            sc.op("DVE", lambda e: e.tensor_copy(out=self.sFW[0][:, 0:n_d],
                                                 in_=self.sDQ1[:, 0:n_d]),
                  reads=["dq1"], writes=["fw:0"])
            sc.dma("SP", lambda e, i=i, wdst=wdst: e.dma_start(
                out=wdst[128 * i:128 * (i + 1), :], in_=self.sFW[0][:, 0:n_d]),
                reads=["fw:0"], writes=[wkey])

    def absmax_to_scalar(self, col_ap, rkey, idx):
        sc = self.sc
        sc.dma("ACT", lambda e, col_ap=col_ap: e.dma_start(out=self.red_b, in_=col_ap),
               reads=[rkey], writes=["redb"])
        sc.dma("ACT", lambda e: e.dma_start(
            out=self.sRED[:], in_=self.red_b.rearrange("p o -> (p o)").unsqueeze(0)),
            reads=["redb"], writes=["redrow"])
        sc.op("DVE", lambda e, idx=idx: e.reduce_max(
            out=self.sSCR[0:1, idx:idx + 1], in_=self.sRED[:], axis=AXX),
            reads=["redrow"], writes=["scr"])
        sc.dma("ACT", lambda e, idx=idx: e.dma_start(
            out=self.cc_in[0:1, idx:idx + 1], in_=self.sSCR[0:1, idx:idx + 1]),
            reads=["scr"], writes=["ccin"])

    def allreduce_max(self):
        sc = self.sc
        sc.cc(lambda e: e.collective_compute(
            "AllReduce", ALU.max, replica_groups=[list(range(self.cfg.n_cores))],
            ins=[self.cc_in.opt()], outs=[self.cc_out.opt()]),
            reads=["ccin"], writes=["ccout"])

    def scale_cols(self, idx, sidx, qmax):
        sc = self.sc
        sc.dma("ACT", lambda e, idx=idx, sidx=sidx: e.dma_start(
            out=self.sSC[:, sidx:sidx + 1],
            in_=self.cc_out[0:1, idx:idx + 1].to_broadcast((128, 1))),
            reads=["ccout"], writes=["ssc"])
        sc.op("DVE", lambda e, sidx=sidx, qmax=qmax: e.tensor_scalar(
            out=self.sSC[:, sidx:sidx + 1], in0=self.sSC[:, sidx:sidx + 1],
            scalar1=1.0 / qmax, scalar2=1e-12, op0=ALU.mult, op1=ALU.max),
            reads=["ssc"], writes=["ssc"])
        sc.op("DVE", lambda e, sidx=sidx: e.reciprocal(
            out=self.sSC[:, sidx + 1:sidx + 2], in_=self.sSC[:, sidx:sidx + 1]),
            reads=["ssc"], writes=["ssc"])

    def qdq_tile(self, out_ap, in_ap, sidx, rkeys, wkeys):
        sc = self.sc
        W = in_ap.shape[1]
        tmp = self.sEVAC[:, 0:W]
        sc.op("DVE", lambda e: e.tensor_scalar(
            out=tmp, in0=in_ap, scalar1=self.sSC[:, sidx + 1:sidx + 2],
            scalar2=MAGIC, op0=ALU.mult, op1=ALU.add),
            reads=list(rkeys) + ["ssc"], writes=["evac"])
        sc.op("DVE", lambda e: e.tensor_scalar(
            out=out_ap, in0=tmp, scalar1=MAGIC, scalar2=self.sSC[:, sidx:sidx + 1],
            op0=ALU.subtract, op1=ALU.mult),
            reads=["evac", "ssc"], writes=list(wkeys))

    def phase_ffn(self):
        cfg, sc = self.cfg, self.sc
        L = cfg.L
        assert L == 2048
        self.dequant_matrix(self.w_gate, self.wg_dq, FFN_I, D_MODEL, "wgdq")
        self.dequant_matrix(self.w_up, self.wu_dq, FFN_I, D_MODEL, "wudq")
        self.dequant_matrix(self.w_down, self.wd_dq, D_MODEL, FFN_I, "wddq")

        def yq_t(k):
            return (self.sY[:, (k % 4) * 2048:(k % 4 + 1) * 2048] if k < 4
                    else self.sZ[:, (k - 4) * 2048:(k - 3) * 2048])

        def yq_keys(k):
            return [f"yq:{k}", (f"y:{4 * (k % 4)}" if k < 4 else f"z:{4 * (k - 4)}")]

        for j in range(L // 512):
            t0 = 512 * j
            hsl = lambda k, t0=t0: self.h[:, k * L + t0: k * L + t0 + 512]
            self.ln_rows(512, hsl, lambda k, j=j: [f"h:{k}:{j}"])
            self.ln_bcasts(512)
            for k in range(NDC):
                ysl = yq_t(k)[:, t0:t0 + 512]
                hap = hsl(k)
                sc.op("DVE", lambda e, hap=hap: e.tensor_tensor(
                    out=self.sEVAC[:, 0:512], in0=hap, in1=self.pPS[:, 1024 + 0:1024 + 512],
                    op=ALU.mult),
                    reads=[f"h:{k}:{j}", "P2"], writes=["evac"])
                sc.op("DVE", lambda e, ysl=ysl: e.tensor_tensor(
                    out=ysl, in0=self.sEVAC[:, 0:512], in1=self.pPS[:, 1024 + 512:1024 + 1024],
                    op=ALU.subtract),
                    reads=["evac", "P3"], writes=yq_keys(k) + ["yq_all"])
                sc.op("DVE", lambda e, ysl=ysl: e.reduce_max(
                    out=self.sEVAC[:, 512:513], in_=ysl, axis=AXX,
                    apply_absolute_value=True),
                    reads=yq_keys(k), writes=["evac"])
                sc.op("DVE", lambda e: e.tensor_tensor(
                    out=self.sMXC[:, 0:1], in0=self.sMXC[:, 0:1],
                    in1=self.sEVAC[:, 512:513], op=ALU.max),
                    reads=["evac", "mxc"], writes=["mxc"])
        self.absmax_to_scalar(self.sMXC[:, 0:1], "mxc", 0)
        self.allreduce_max()
        self.scale_cols(0, 0, QMAX)
        for k in range(NDC):
            for j in range(L // 512):
                sl = yq_t(k)[:, 512 * j:512 * (j + 1)]
                self.qdq_tile(sl, sl, 0, yq_keys(k), yq_keys(k) + ["yq_all"])
        sc.op("DVE", lambda e: e.memset(self.sMXC[:, 1:3], 0.0),
              reads=["mxc"], writes=["mxc"])
        all_yq = ["yq_all"]
        for which, wdq, dst, mcol, wdqk, dkey in (
                (0, self.wu_dq, self.up_d, 1, "wudq", "uprawd"),
                (1, self.wg_dq, self.gate_d, 2, "wgdq", "gaterawd")):
            for ig in range(FFN_I // 512):
                buf = ig % 2
                for k in range(NDC):
                    sc.dma("SP", lambda e, ig=ig, buf=buf, k=k, wdq=wdq: e.dma_start_transpose(
                        out=self.sFW[buf][:, 512 * k:512 * (k + 1)],
                        in_=wdq[512 * ig:512 * (ig + 1), 128 * k:128 * (k + 1)]),
                        reads=[wdqk], writes=[f"fw:{buf}"])
                for isub in range(4):
                    iof = 512 * ig + 128 * isub
                    for j in range(L // 512):
                        pb = (4 * ig + isub + j) % 2
                        psl = self.pPS[:, 512 * pb:512 * (pb + 1)]
                        t0 = 512 * j

                        def mk(e, buf=buf, isub=isub, t0=t0, psl=psl):
                            r = None
                            for k in range(NDC):
                                r = e.matmul(out=psl,
                                             lhsT=self.sFW[buf][:, 512 * k + 128 * isub:
                                                                512 * k + 128 * isub + 128],
                                             rhs=yq_t(k)[:, t0:t0 + 512],
                                             start=(k == 0), stop=(k == NDC - 1))
                            return r
                        sc.ops("PE", [mk], reads=[f"fw:{buf}"] + all_yq,
                               writes=[f"P{pb}"])
                        sc.op("DVE", lambda e, psl=psl: e.tensor_copy(
                            out=self.sT1[:], in_=psl),
                            reads=[f"P{pb}"], writes=["t1"])
                        sc.op("DVE", lambda e: e.reduce_max(
                            out=self.sEVAC[:, 512:513], in_=self.sT1[:], axis=AXX,
                            apply_absolute_value=True),
                            reads=["t1"], writes=["evac"])
                        sc.op("DVE", lambda e, mcol=mcol: e.tensor_tensor(
                            out=self.sMXC[:, mcol:mcol + 1], in0=self.sMXC[:, mcol:mcol + 1],
                            in1=self.sEVAC[:, 512:513], op=ALU.max),
                            reads=["evac", "mxc"], writes=["mxc"])
                        sc.dma("SP", lambda e, iof=iof, t0=t0, dst=dst: e.dma_start(
                            out=dst[iof:iof + 128, t0:t0 + 512], in_=self.sT1[:]),
                            reads=["t1"], writes=[dkey])
        self.absmax_to_scalar(self.sMXC[:, 1:2], "mxc", 1)
        self.absmax_to_scalar(self.sMXC[:, 2:3], "mxc", 2)
        self.allreduce_max()
        self.scale_cols(1, 2, QMAX)
        self.scale_cols(2, 4, QMAX)
        sc.op("DVE", lambda e: e.memset(self.sMXC[:, 3:4], 0.0),
              reads=["mxc"], writes=["mxc"])
        for ic in range(FFN_I // 128):
            for j in range(L // 512):
                t0 = 512 * j
                sc.dma("SP", lambda e, ic=ic, t0=t0: e.dma_start(
                    out=self.sT1[:], in_=self.gate_d[128 * ic:128 * (ic + 1), t0:t0 + 512]),
                    reads=["gaterawd"], writes=["t1"])
                self.qdq_tile(self.sT1[:], self.sT1[:], 4, ["t1"], ["t1"])
                sc.op("ACT", lambda e: e.activation(out=self.sT2[:], in_=self.sT1[:],
                                                    func=AF.Sigmoid),
                      reads=["t1"], writes=["t2"])
                sc.op("DVE", lambda e: e.tensor_scalar(
                    out=self.sEVAC[:, 0:512], in0=self.sT2[:], scalar1=U16S, scalar2=MAGIC,
                    op0=ALU.mult, op1=ALU.add),
                    reads=["t2"], writes=["evac"])
                sc.op("DVE", lambda e: e.tensor_scalar(
                    out=self.sT2[:], in0=self.sEVAC[:, 0:512], scalar1=MAGIC,
                    scalar2=1.0 / U16S, op0=ALU.subtract, op1=ALU.mult),
                    reads=["evac"], writes=["t2"])
                sc.op("DVE", lambda e: e.tensor_tensor(out=self.sT3[:], in0=self.sT1[:],
                                                       in1=self.sT2[:], op=ALU.mult),
                      reads=["t1", "t2"], writes=["t3"])
                sc.op("DVE", lambda e: e.reduce_max(
                    out=self.sEVAC[:, 512:513], in_=self.sT3[:], axis=AXX,
                    apply_absolute_value=True),
                    reads=["t3"], writes=["evac"])
                sc.op("DVE", lambda e: e.tensor_tensor(
                    out=self.sMXC[:, 3:4], in0=self.sMXC[:, 3:4],
                    in1=self.sEVAC[:, 512:513], op=ALU.max),
                    reads=["evac", "mxc"], writes=["mxc"])
                sc.dma("SP", lambda e, ic=ic, t0=t0: e.dma_start(
                    out=self.gate_d[128 * ic:128 * (ic + 1), t0:t0 + 512], in_=self.sT3[:]),
                    reads=["t3"], writes=["t4d"])
                sc.dma("SP", lambda e, ic=ic, t0=t0: e.dma_start(
                    out=self.sT1[:], in_=self.up_d[128 * ic:128 * (ic + 1), t0:t0 + 512]),
                    reads=["uprawd"], writes=["t1"])
                self.qdq_tile(self.sT1[:], self.sT1[:], 2, ["t1"], ["t1"])
                sc.dma("SP", lambda e, ic=ic, t0=t0: e.dma_start(
                    out=self.up_d[128 * ic:128 * (ic + 1), t0:t0 + 512], in_=self.sT1[:]),
                    reads=["t1"], writes=["upqd"])
        self.absmax_to_scalar(self.sMXC[:, 3:4], "mxc", 3)
        self.allreduce_max()
        self.scale_cols(3, 6, QMAX)
        sc.op("DVE", lambda e: e.memset(self.sMXC[:, 4:5], 0.0),
              reads=["mxc"], writes=["mxc"])
        for ic in range(FFN_I // 128):
            for j in range(L // 512):
                t0 = 512 * j
                sc.dma("SP", lambda e, ic=ic, t0=t0: e.dma_start(
                    out=self.sT1[:], in_=self.gate_d[128 * ic:128 * (ic + 1), t0:t0 + 512]),
                    reads=["t4d"], writes=["t1"])
                self.qdq_tile(self.sT1[:], self.sT1[:], 6, ["t1"], ["t1"])
                sc.dma("SP", lambda e, ic=ic, t0=t0: e.dma_start(
                    out=self.sT2[:], in_=self.up_d[128 * ic:128 * (ic + 1), t0:t0 + 512]),
                    reads=["upqd"], writes=["t2"])
                sc.op("DVE", lambda e: e.tensor_tensor(out=self.sT3[:], in0=self.sT1[:],
                                                       in1=self.sT2[:], op=ALU.mult),
                      reads=["t1", "t2"], writes=["t3"])
                sc.op("DVE", lambda e: e.reduce_max(
                    out=self.sEVAC[:, 512:513], in_=self.sT3[:], axis=AXX,
                    apply_absolute_value=True),
                    reads=["t3"], writes=["evac"])
                sc.op("DVE", lambda e: e.tensor_tensor(
                    out=self.sMXC[:, 4:5], in0=self.sMXC[:, 4:5],
                    in1=self.sEVAC[:, 512:513], op=ALU.max),
                    reads=["evac", "mxc"], writes=["mxc"])
                sc.dma("SP", lambda e, ic=ic, t0=t0: e.dma_start(
                    out=self.gate_d[128 * ic:128 * (ic + 1), t0:t0 + 512], in_=self.sT3[:]),
                    reads=["t3"], writes=["t5d"])
        self.absmax_to_scalar(self.sMXC[:, 4:5], "mxc", 0)
        self.allreduce_max()
        self.scale_cols(0, 0, QMAX)
        allp = [f"P{b}" for b in range(8)]
        for j in range(L // 512):
            t0 = 512 * j
            for ic in range(FFN_I // 128):
                sc.dma("SP", lambda e, ic=ic, t0=t0: e.dma_start(
                    out=self.sT1[:],
                    in_=self.gate_d[128 * ic:128 * (ic + 1), t0:t0 + 512]),
                    reads=["t5d"], writes=["t1"])
                self.qdq_tile(self.sO[:, 512 * ic:512 * (ic + 1)], self.sT1[:],
                              0, ["t1"], [f"o:{ic}"])
            for ic in range(FFN_I // 128):
                buf = ic % 2
                for dh in range(2):
                    sc.dma("SP", lambda e, ic=ic, buf=buf, dh=dh: e.dma_start_transpose(
                        out=self.sFW[buf][:, 512 * dh:512 * (dh + 1)],
                        in_=self.wd_dq[512 * dh:512 * (dh + 1), 128 * ic:128 * (ic + 1)]),
                        reads=["wddq"], writes=[f"fw:{buf}"])

                def mk(e, ic=ic, buf=buf):
                    r = None
                    for dk in range(NDC):
                        po = self.pPS[:, 512 * dk:512 * dk + 512]
                        r = e.matmul(out=po, lhsT=self.sFW[buf][:, 128 * dk:128 * (dk + 1)],
                                     rhs=self.sO[:, 512 * ic:512 * (ic + 1)],
                                     start=(ic == 0), stop=(ic == FFN_I // 128 - 1))
                    return r
                sc.ops("PE", [mk], reads=[f"fw:{buf}", f"o:{ic}"], writes=allp)
            for dk in range(NDC):
                po = self.pPS[:, 512 * dk:512 * dk + 512]
                hsl = self.h[:, dk * L + t0: dk * L + t0 + 512]
                sc.op("DVE", lambda e, po=po, hsl=hsl: e.tensor_tensor(
                    out=self.sEVAC[:, 0:512], in0=hsl, in1=po, op=ALU.add),
                    reads=[f"P{dk}", f"h:{dk}:{j}"], writes=["evac"])
                sc.dma("SP", lambda e, dk=dk, t0=t0: e.dma_start(
                    out=self.out_fm[128 * dk:128 * (dk + 1), t0:t0 + 512],
                    in_=self.sEVAC[:, 0:512]),
                    reads=["evac"], writes=["OUT"])

    def output_no_ffn(self):
        sc, L = self.sc, self.cfg.L
        for k in range(NDC):
            for j in range(L // 512):
                sc.dma("SP", lambda e, k=k, j=j: e.dma_start(
                    out=self.out_fm[128 * k:128 * (k + 1), 512 * j:512 * (j + 1)],
                    in_=self.h[:, k * L + 512 * j: k * L + 512 * (j + 1)]),
                    reads=[f"h:{k}:{j}"], writes=["OUT"])

    def absmax_to_out(self, col_ap, rkey, idx):
        sc = self.sc
        sc.dma("ACT", lambda e, col_ap=col_ap: e.dma_start(out=self.red_b, in_=col_ap),
               reads=[rkey], writes=["redb"])
        sc.dma("ACT", lambda e: e.dma_start(
            out=self.sRED[:], in_=self.red_b.rearrange("p o -> (p o)").unsqueeze(0)),
            reads=["redb"], writes=["redrow"])
        sc.op("DVE", lambda e, idx=idx: e.reduce_max(
            out=self.sSCR[0:1, idx:idx + 1], in_=self.sRED[:], axis=AXX),
            reads=["redrow"], writes=["scr"])
        sc.dma("ACT", lambda e, idx=idx: e.dma_start(
            out=self.mx_out[0:1, idx:idx + 1], in_=self.sSCR[0:1, idx:idx + 1]),
            reads=["scr"], writes=["OUTMX"])

    def reduce_mxcat(self):
        sc, NCC = self.sc, self.cfg.n_cores
        sc.dma("ACT", lambda e: e.dma_start(out=self.mxr_d, in_=self.mxcat),
               writes=["mxrd"])
        sc.dma("ACT", lambda e: e.dma_start(
            out=self.sRED[0:1, 0:6 * NCC],
            in_=self.mxr_d.rearrange("a b -> (a b)").unsqueeze(0)),
            reads=["mxrd"], writes=["redrow"])
        sc.op("DVE", lambda e, NCC=NCC: e.reduce_max(
            out=self.sSCR[0:1, 8:14].unsqueeze(2),
            in_=self.sRED.ap()[0:1, 0:6 * NCC].rearrange("o (a b) -> o a b", a=6),
            axis=AXX),
            reads=["redrow"], writes=["scr"])
        sc.dma("ACT", lambda e: e.dma_start(out=self.gmax_d[0:1, 0:6],
                                            in_=self.sSCR[0:1, 8:14]),
               reads=["scr"], writes=["gmaxd"])

    def scale_cols_g(self, idx, sidx, qmax):
        sc = self.sc
        sc.dma("ACT", lambda e, idx=idx, sidx=sidx: e.dma_start(
            out=self.sSC[:, sidx:sidx + 1],
            in_=self.gmax_d[0:1, idx:idx + 1].to_broadcast((128, 1))),
            reads=["gmaxd"], writes=["ssc"])
        sc.op("DVE", lambda e, sidx=sidx, qmax=qmax: e.tensor_scalar(
            out=self.sSC[:, sidx:sidx + 1], in0=self.sSC[:, sidx:sidx + 1],
            scalar1=1.0 / qmax, scalar2=1e-12, op0=ALU.mult, op1=ALU.max),
            reads=["ssc"], writes=["ssc"])
        sc.op("DVE", lambda e, sidx=sidx: e.reciprocal(
            out=self.sSC[:, sidx + 1:sidx + 2], in_=self.sSC[:, sidx:sidx + 1]),
            reads=["ssc"], writes=["ssc"])

    def stage1_tail(self):
        cfg, sc = self.cfg, self.sc
        L = cfg.L
        for j in range(L // 512):
            t0 = 512 * j
            hsl = lambda k, t0=t0: self.h[:, k * L + t0: k * L + t0 + 512]
            self.ln_rows(512, hsl, lambda k, j=j: [f"h:{k}:{j}"])
            self.ln_bcasts(512)
            for k in range(NDC):
                hap = hsl(k)
                sc.op("DVE", lambda e, hap=hap: e.tensor_tensor(
                    out=self.sEVAC[:, 0:512], in0=hap, in1=self.pPS[:, 1024 + 0:1024 + 512],
                    op=ALU.mult),
                    reads=[f"h:{k}:{j}", "P2"], writes=["evac"])
                sc.op("DVE", lambda e: e.tensor_tensor(
                    out=self.sXB[:, 0:512], in0=self.sEVAC[:, 0:512],
                    in1=self.pPS[:, 1024 + 512:1024 + 1024], op=ALU.subtract),
                    reads=["evac", "P3"], writes=["sXB"])
                sc.op("DVE", lambda e: e.reduce_max(
                    out=self.sEVAC[:, 512:513], in_=self.sXB[:, 0:512], axis=AXX,
                    apply_absolute_value=True),
                    reads=["sXB"], writes=["evac"])
                sc.op("DVE", lambda e: e.tensor_tensor(
                    out=self.sMXC[:, 0:1], in0=self.sMXC[:, 0:1],
                    in1=self.sEVAC[:, 512:513], op=ALU.max),
                    reads=["evac", "mxc"], writes=["mxc"])
                sc.dma("SP", lambda e, k=k, t0=t0: e.dma_start(
                    out=self.y_out[128 * k:128 * (k + 1), t0:t0 + 512],
                    in_=self.sXB[:, 0:512]),
                    reads=["sXB"], writes=["OUT"])
        self.absmax_to_out(self.sMXC[:, 0:1], "mxc", 0)
        self.output_no_ffn()

    def stage2(self):
        cfg, sc = self.cfg, self.sc
        L = cfg.L
        self.dequant_matrix(self.w_gate, self.wg_dq, FFN_I, D_MODEL, "wgdq")
        self.dequant_matrix(self.w_up, self.wu_dq, FFN_I, D_MODEL, "wudq")
        self.reduce_mxcat()
        self.scale_cols_g(0, 0, QMAX)

        def yq_t(k):
            return (self.sY[:, (k % 4) * 2048:(k % 4 + 1) * 2048] if k < 4
                    else self.sZ[:, (k - 4) * 2048:(k - 3) * 2048])
        for k in range(NDC):
            for j in range(L // 512):
                t0 = 512 * j
                sl = yq_t(k)[:, t0:t0 + 512]
                sc.dma("SP", lambda e, k=k, t0=t0, sl=sl: e.dma_start(
                    out=sl, in_=self.y_in[128 * k:128 * (k + 1), t0:t0 + 512]),
                    writes=["yq_all"])
                self.qdq_tile(sl, sl, 0, ["yq_all"], ["yq_all"])
        for which, wdq, dst, mcol, wdqk, dkey in (
                (0, self.wu_dq, self.upraw_out, 1, "wudq", "uprawd"),
                (1, self.wg_dq, self.gateraw_out, 2, "wgdq", "gaterawd")):
            for ig in range(FFN_I // 512):
                buf = ig % 2
                for k in range(NDC):
                    sc.dma("SP", lambda e, ig=ig, buf=buf, k=k, wdq=wdq: e.dma_start_transpose(
                        out=self.sFW[buf][:, 512 * k:512 * (k + 1)],
                        in_=wdq[512 * ig:512 * (ig + 1), 128 * k:128 * (k + 1)]),
                        reads=[wdqk], writes=[f"fw:{buf}"])
                for isub in range(4):
                    iof = 512 * ig + 128 * isub
                    for j in range(L // 512):
                        pb = (4 * ig + isub + j) % 2
                        psl = self.pPS[:, 512 * pb:512 * (pb + 1)]
                        t0 = 512 * j

                        def mk(e, buf=buf, isub=isub, t0=t0, psl=psl):
                            r = None
                            for k in range(NDC):
                                r = e.matmul(out=psl,
                                             lhsT=self.sFW[buf][:, 512 * k + 128 * isub:
                                                                512 * k + 128 * isub + 128],
                                             rhs=yq_t(k)[:, t0:t0 + 512],
                                             start=(k == 0), stop=(k == NDC - 1))
                            return r
                        sc.ops("PE", [mk], reads=[f"fw:{buf}", "yq_all"],
                               writes=[f"P{pb}"])
                        sc.op("DVE", lambda e, psl=psl: e.tensor_copy(
                            out=self.sT1[:], in_=psl),
                            reads=[f"P{pb}"], writes=["t1"])
                        sc.op("DVE", lambda e: e.reduce_max(
                            out=self.sEVAC[:, 512:513], in_=self.sT1[:], axis=AXX,
                            apply_absolute_value=True),
                            reads=["t1"], writes=["evac"])
                        sc.op("DVE", lambda e, mcol=mcol: e.tensor_tensor(
                            out=self.sMXC[:, mcol:mcol + 1], in0=self.sMXC[:, mcol:mcol + 1],
                            in1=self.sEVAC[:, 512:513], op=ALU.max),
                            reads=["evac", "mxc"], writes=["mxc"])
                        sc.dma("SP", lambda e, iof=iof, t0=t0, dst=dst: e.dma_start(
                            out=dst[iof:iof + 128, t0:t0 + 512], in_=self.sT1[:]),
                            reads=["t1"], writes=["OUT"])
        self.absmax_to_out(self.sMXC[:, 1:2], "mxc", 1)
        self.absmax_to_out(self.sMXC[:, 2:3], "mxc", 2)

    def stage3(self):
        cfg, sc = self.cfg, self.sc
        L = cfg.L
        self.reduce_mxcat()
        self.scale_cols_g(2, 4, QMAX)
        for ic in range(FFN_I // 128):
            for j in range(L // 512):
                t0 = 512 * j
                sc.dma("SP", lambda e, ic=ic, t0=t0: e.dma_start(
                    out=self.sT1[:], in_=self.gateraw_in[128 * ic:128 * (ic + 1), t0:t0 + 512]),
                    writes=["t1"])
                self.qdq_tile(self.sT1[:], self.sT1[:], 4, ["t1"], ["t1"])
                sc.op("ACT", lambda e: e.activation(out=self.sT2[:], in_=self.sT1[:],
                                                    func=AF.Sigmoid),
                      reads=["t1"], writes=["t2"])
                sc.op("DVE", lambda e: e.tensor_scalar(
                    out=self.sEVAC[:, 0:512], in0=self.sT2[:], scalar1=U16S, scalar2=MAGIC,
                    op0=ALU.mult, op1=ALU.add),
                    reads=["t2"], writes=["evac"])
                sc.op("DVE", lambda e: e.tensor_scalar(
                    out=self.sT2[:], in0=self.sEVAC[:, 0:512], scalar1=MAGIC,
                    scalar2=1.0 / U16S, op0=ALU.subtract, op1=ALU.mult),
                    reads=["evac"], writes=["t2"])
                sc.op("DVE", lambda e: e.tensor_tensor(out=self.sT3[:], in0=self.sT1[:],
                                                       in1=self.sT2[:], op=ALU.mult),
                      reads=["t1", "t2"], writes=["t3"])
                sc.op("DVE", lambda e: e.reduce_max(
                    out=self.sEVAC[:, 512:513], in_=self.sT3[:], axis=AXX,
                    apply_absolute_value=True),
                    reads=["t3"], writes=["evac"])
                sc.op("DVE", lambda e: e.tensor_tensor(
                    out=self.sMXC[:, 3:4], in0=self.sMXC[:, 3:4],
                    in1=self.sEVAC[:, 512:513], op=ALU.max),
                    reads=["evac", "mxc"], writes=["mxc"])
                sc.dma("SP", lambda e, ic=ic, t0=t0: e.dma_start(
                    out=self.t4_out[128 * ic:128 * (ic + 1), t0:t0 + 512], in_=self.sT3[:]),
                    reads=["t3"], writes=["OUT"])
        self.absmax_to_out(self.sMXC[:, 3:4], "mxc", 3)

    def stage4(self):
        cfg, sc = self.cfg, self.sc
        L = cfg.L
        self.reduce_mxcat()
        self.scale_cols_g(3, 6, QMAX)
        self.scale_cols_g(1, 2, QMAX)
        for ic in range(FFN_I // 128):
            for j in range(L // 512):
                t0 = 512 * j
                sc.dma("SP", lambda e, ic=ic, t0=t0: e.dma_start(
                    out=self.sT1[:], in_=self.t4_in[128 * ic:128 * (ic + 1), t0:t0 + 512]),
                    writes=["t1"])
                self.qdq_tile(self.sT1[:], self.sT1[:], 6, ["t1"], ["t1"])
                sc.dma("SP", lambda e, ic=ic, t0=t0: e.dma_start(
                    out=self.sT2[:], in_=self.upraw_in[128 * ic:128 * (ic + 1), t0:t0 + 512]),
                    writes=["t2"])
                self.qdq_tile(self.sT2[:], self.sT2[:], 2, ["t2"], ["t2"])
                sc.op("DVE", lambda e: e.tensor_tensor(out=self.sT3[:], in0=self.sT1[:],
                                                       in1=self.sT2[:], op=ALU.mult),
                      reads=["t1", "t2"], writes=["t3"])
                sc.op("DVE", lambda e: e.reduce_max(
                    out=self.sEVAC[:, 512:513], in_=self.sT3[:], axis=AXX,
                    apply_absolute_value=True),
                    reads=["t3"], writes=["evac"])
                sc.op("DVE", lambda e: e.tensor_tensor(
                    out=self.sMXC[:, 4:5], in0=self.sMXC[:, 4:5],
                    in1=self.sEVAC[:, 512:513], op=ALU.max),
                    reads=["evac", "mxc"], writes=["mxc"])
                sc.dma("SP", lambda e, ic=ic, t0=t0: e.dma_start(
                    out=self.t5_out[128 * ic:128 * (ic + 1), t0:t0 + 512], in_=self.sT3[:]),
                    reads=["t3"], writes=["OUT"])
        self.absmax_to_out(self.sMXC[:, 4:5], "mxc", 4)

    def stage5(self):
        cfg, sc = self.cfg, self.sc
        L = cfg.L
        self.dequant_matrix(self.w_down, self.wd_dq, D_MODEL, FFN_I, "wddq")
        self.reduce_mxcat()
        self.scale_cols_g(4, 0, QMAX)
        allp = [f"P{b}" for b in range(8)]
        for j in range(L // 512):
            t0 = 512 * j
            for ic in range(FFN_I // 128):
                sc.dma("SP", lambda e, ic=ic, t0=t0: e.dma_start(
                    out=self.sT1[:], in_=self.t5_in[128 * ic:128 * (ic + 1), t0:t0 + 512]),
                    writes=["t1"])
                self.qdq_tile(self.sO[:, 512 * ic:512 * (ic + 1)], self.sT1[:],
                              0, ["t1"], [f"o:{ic}"])
            for ic in range(FFN_I // 128):
                buf = ic % 2
                for dh in range(2):
                    sc.dma("SP", lambda e, ic=ic, buf=buf, dh=dh: e.dma_start_transpose(
                        out=self.sFW[buf][:, 512 * dh:512 * (dh + 1)],
                        in_=self.wd_dq[512 * dh:512 * (dh + 1), 128 * ic:128 * (ic + 1)]),
                        reads=["wddq"], writes=[f"fw:{buf}"])

                def mk(e, ic=ic, buf=buf):
                    r = None
                    for dk in range(NDC):
                        po = self.pPS[:, 512 * dk:512 * dk + 512]
                        r = e.matmul(out=po, lhsT=self.sFW[buf][:, 128 * dk:128 * (dk + 1)],
                                     rhs=self.sO[:, 512 * ic:512 * (ic + 1)],
                                     start=(ic == 0), stop=(ic == FFN_I // 128 - 1))
                    return r
                sc.ops("PE", [mk], reads=[f"fw:{buf}", f"o:{ic}"], writes=allp)
            for dk in range(NDC):
                po = self.pPS[:, 512 * dk:512 * dk + 512]
                sc.dma("SP", lambda e, dk=dk, t0=t0: e.dma_start(
                    out=self.sXTILE[:], in_=self.x1_in[128 * dk:128 * (dk + 1), t0:t0 + 512]),
                    writes=["xt"])
                sc.op("DVE", lambda e, po=po: e.tensor_tensor(
                    out=self.sEVAC[:, 0:512], in0=self.sXTILE[:], in1=po, op=ALU.add),
                    reads=[f"P{dk}", "xt"], writes=["evac"])
                sc.dma("SP", lambda e, dk=dk, t0=t0: e.dma_start(
                    out=self.out_fm[128 * dk:128 * (dk + 1), t0:t0 + 512],
                    in_=self.sEVAC[:, 0:512]),
                    reads=["evac"], writes=["OUT"])

    def build(self):
        cfg, sc = self.cfg, self.sc
        st = cfg.stage
        self.load_consts()
        if st in ("all", "s1"):
            self.preconvert_weights()
            self.phase_init_ln()
            for l in range(cfg.nlayer):
                self.load_layer_params(l)
                for seg in range(cfg.NSEG):
                    self.phase_A(l, seg)
                    self.phase_B_prep_front(l, seg, 0)
                    self.phase_B_prep_back(l, seg, 0)
                    for c in range(cfg.NCH):
                        self.phase_B_compute(l, seg, c)
                    self.phase_C(l, seg)
            self.phase_x1()
        if st == "all":
            if cfg.do_ffn:
                sc.barrier()
                self.phase_ffn()
            else:
                self.output_no_ffn()
        elif st == "s1":
            sc.barrier()
            if cfg.do_ffn:
                self.stage1_tail()
            else:
                self.output_no_ffn()
        elif st == "s2":
            self.stage2()
        elif st == "s3":
            self.stage3()
        elif st == "s4":
            self.stage4()
        elif st == "s5":
            self.stage5()
        sc.finish(["OUT", "OUTMX"] if (cfg.do_ffn and st != "s5" and st != "all")
                  else ["OUT"])
        with self.nc.Block() as block:
            sc.emit(block)
        return self.nc


def host_constants():
    e64s = np.zeros((NH, 512), np.float32)
    e64sb = np.zeros((NH, 512), np.float32)
    for r in range(NH):
        hr = r % 32
        for t in range(32):
            if hr < 16:
                e64s[r, t * 16 + hr] = -1.0
            else:
                e64sb[r, t * 16 + (hr - 16)] = -1.0
    e64x = np.tile(e64s, (1, 4))
    e64xb = np.tile(e64sb, (1, 4))
    trid = np.zeros((32, 512), np.float32)
    for r in range(32):
        trid[r, r * 16:(r + 1) * 16] = -BIG
    trid = np.tile(trid, (1, 4))
    tri16 = np.zeros((32, 16 * 128), np.float32)
    p = np.arange(128)
    for i in range(16):
        sl = i % 4
        for r in range(32):
            tri16[r, 128 * i:128 * (i + 1)] = (p >= 32 * sl + r + 1)
    esel = np.zeros((NH, D_INNER), np.float32)
    for k in range(NZC):
        for q in range(128):
            esel[4 * k + q // 32, 128 * k + q] = 1.0
    idb = np.eye(128, dtype=np.float32)
    bf = lambda a: a.astype(ml_dtypes.bfloat16)
    return dict(c_e64x=bf(e64x), c_e64xb=bf(e64xb), c_trid=bf(trid), c_tri16=bf(tri16),
                c_esel=bf(esel), c_idb=bf(idb),
                c_ones2=bf(np.ones((2, 128), np.float32)),
                c_ones64=bf(np.ones((NH, 128), np.float32)),
                c_onescol=bf(np.ones((128, 1), np.float32)))


def host_layout(inputs, cfg: Cfg):
    NL = cfg.nlayer
    cw = np.asarray(inputs["L0_attn_conv_w"], np.float32)[:NL]
    cb = np.asarray(inputs["L0_attn_conv_b"], np.float32)[:NL]
    dcwL = np.zeros((NL, NXC * DCONV, 128, 128), np.float32)
    idx = np.arange(128)
    for l in range(NL):
        for j in range(NXC):
            for k in range(DCONV):
                dcwL[l, 4 * j + k, idx, idx] = cw[l, 128 * j + idx, k]
    cbL = np.ascontiguousarray(cb.reshape(NL, NXC, 128).transpose(0, 2, 1))
    colmaj = lambda v, n: np.ascontiguousarray(v.reshape(v.shape[:-1] + (n, 128)).swapaxes(-1, -2))
    gnwR = colmaj(np.asarray(inputs["L0_attn_gate_norm_w"], np.float32)[:NL], NZC)
    bnwR = colmaj(np.asarray(inputs["L0_attn_block_norm_w"], np.float32)[:NL], NDC)
    fnwR = colmaj(np.asarray(inputs["L0_attn_final_norm_w"], np.float32), NDC)
    col = lambda v: np.ascontiguousarray(np.asarray(v, np.float32)[:NL][:, :, None])
    m = dict(
        wi=np.ascontiguousarray(np.asarray(inputs["L0_attn_W_in"], np.float32)[:NL]),
        wo=np.ascontiguousarray(np.asarray(inputs["L0_attn_W_out"], np.float32)[:NL]),
        dcwL=dcwL, cbL=cbL,
        dtbL=col(inputs["L0_attn_dt_bias"]),
        alogL=col(inputs["L0_attn_A_log"]),
        dL=col(inputs["L0_attn_D"]),
        gnwR=gnwR, bnwR=bnwR, fnwR=fnwR,
        w_gate=np.ascontiguousarray(np.asarray(inputs["L0_ffn_w_gate"], np.float32)),
        w_up=np.ascontiguousarray(np.asarray(inputs["L0_ffn_w_up"], np.float32)),
        w_down=np.ascontiguousarray(np.asarray(inputs["L0_ffn_w_down"], np.float32)),
    )
    m.update(host_constants())
    return m


def kernel(**inputs):
    n_cores = 8
    x = np.asarray(inputs["L0_attn_x"], np.float32)
    L = x.shape[1]
    shared = host_layout(inputs, Cfg(L=L, n_cores=n_cores))
    x_fms = [np.ascontiguousarray(x[i].T) for i in range(n_cores)]
    mx = np.zeros((6, n_cores), np.float32)

    def run(stage, extra_names, per_core_extra):
        cfg = Cfg(L=L, nlayer=4, do_ffn=True, n_cores=n_cores, stage=stage)
        b = Builder(cfg)
        nc = b.build()
        in_names = {t.name for t in
                    (a for a in nc.m.functions[0].allocations
                     if getattr(a, "kind", None) == "ExternalInput")}
        maps = []
        for i in range(n_cores):
            m = {k: v for k, v in shared.items()}
            m["x_fm"] = x_fms[i]
            for name, arrs in per_core_extra.items():
                m[name] = arrs[i]
            for name in extra_names:
                m[name] = extra_names[name]
            maps.append(m)
        res = run_bass_kernel_spmd(nc, maps, core_ids=list(range(n_cores)))
        return res

    r1 = run("s1", {}, {})
    x1s = [np.asarray(r1.results[i]["out_fm"]) for i in range(n_cores)]
    ys = [np.asarray(r1.results[i]["y_out"]) for i in range(n_cores)]
    for i in range(n_cores):
        mx[0, i] = np.asarray(r1.results[i]["mx_out"]).ravel()[0]

    r2 = run("s2", {"mxcat": mx}, {"y_in": ys})
    ups = [np.asarray(r2.results[i]["upraw_out"]) for i in range(n_cores)]
    gates = [np.asarray(r2.results[i]["gateraw_out"]) for i in range(n_cores)]
    for i in range(n_cores):
        v = np.asarray(r2.results[i]["mx_out"]).ravel()
        mx[1, i] = v[1]
        mx[2, i] = v[2]

    r3 = run("s3", {"mxcat": mx}, {"gateraw_in": gates})
    t4s = [np.asarray(r3.results[i]["t4_out"]) for i in range(n_cores)]
    for i in range(n_cores):
        mx[3, i] = np.asarray(r3.results[i]["mx_out"]).ravel()[3]

    r4 = run("s4", {"mxcat": mx}, {"t4_in": t4s, "upraw_in": ups})
    t5s = [np.asarray(r4.results[i]["t5_out"]) for i in range(n_cores)]
    for i in range(n_cores):
        mx[4, i] = np.asarray(r4.results[i]["mx_out"]).ravel()[4]

    r5 = run("s5", {"mxcat": mx}, {"t5_in": t5s, "x1_in": x1s})
    out = np.stack([np.asarray(r5.results[i]["out_fm"]).T for i in range(n_cores)],
                   axis=0)
    return out.astype(np.float32)


# version 13
# speedup vs baseline: 1.1200x; 1.0888x over previous
import numpy as np
import ml_dtypes
import concourse.bass as bass
import concourse.mybir as mybir
from concourse.bass_utils import run_bass_kernel_spmd

F32 = mybir.dt.float32
BF16 = mybir.dt.bfloat16
AF = mybir.ActivationFunctionType
ALU = mybir.AluOpType
AXX = mybir.AxisListType.X

MAGIC = float(1.5 * 2 ** 23)
BIG = 57344.0
LN_EPS = 1e-5

D_MODEL = 1024
D_INNER = 2048
NH = 64
HD = 32
DS = 128
DCONV = 4
CONV_DIM = D_INNER + 2 * DS
D_IN = 2 * D_INNER + 2 * DS + NH
FFN_I = 3072
NZC = D_INNER // 128
NXC = CONV_DIM // 128
NDC = D_MODEL // 128
QMAX = 32767.0
U16S = 65536.0

ENGS = ("PE", "DVE", "ACT", "POOL", "SP")


class Cfg:
    def __init__(self, L=2048, nlayer=4, do_ffn=True, n_cores=8, stage="all"):
        self.stage = stage
        self.L = L
        self.C = 128
        self.TSEG = min(512, L)
        assert L % self.TSEG == 0 and self.TSEG % self.C == 0
        self.NSEG = L // self.TSEG
        self.NCH = self.TSEG // self.C
        self.nlayer = nlayer
        self.do_ffn = do_ffn
        self.n_cores = n_cores


class Sched:

    NLANES = 6

    def __init__(self, nc):
        self.nc = nc
        self.sem = {e: nc.alloc_semaphore(f"sem_{e}") for e in ENGS}
        self.dsem = {}
        self.cnt = {k: 0 for k in ENGS}
        for qn in ("SP", "ACT", "POOL"):
            for ln in range(self.NLANES):
                key = f"d{qn}:{ln}"
                self.dsem[key] = nc.alloc_semaphore(f"dma_{qn}_{ln}")
                self.cnt[key] = 0
        self.ccsem = nc.alloc_semaphore("sem_cc")
        self.cnt["CC"] = 0
        self.dma_issue = {qn: 0 for qn in ("SP", "ACT", "POOL")}
        self.q = {e: [] for e in ENGS}
        self.waited = {e: {} for e in ENGS}
        self.last_w = {}
        self.readers = {}
        self.nops = 0

    def _sobj(self, semkey):
        if semkey in self.sem:
            return self.sem[semkey]
        if semkey == "CC":
            return self.ccsem
        return self.dsem[semkey]

    def _wait(self, eng, semkey, val):
        if val <= self.waited[eng].get(semkey, 0):
            return
        assert val <= self.cnt[semkey], f"wait on future {semkey}:{val}"
        self.waited[eng][semkey] = val
        sobj = self._sobj(semkey)
        self.q[eng].append(lambda e, s=sobj, v=val: e.wait_ge(s, v))

    def _edges(self, eng, reads, writes):
        for k in reads:
            lw = self.last_w.get(k)
            if lw is not None and not (lw[0] == "PE" == eng):
                self._wait(eng, lw[0], lw[1])
        for k in writes:
            lw = self.last_w.get(k)
            if lw is not None and not (lw[0] == "PE" == eng):
                self._wait(eng, lw[0], lw[1])
            for skey, v in self.readers.get(k, ()):
                if not (skey == "PE" == eng):
                    self._wait(eng, skey, v)

    def _record(self, semkey, val, reads, writes):
        for k in writes:
            self.last_w[k] = (semkey, val)
            self.readers[k] = []
        for k in reads:
            self.readers.setdefault(k, []).append((semkey, val))

    def op(self, eng, fn, reads=(), writes=()):
        self._edges(eng, reads, writes)
        self.cnt[eng] += 1
        sobj = self.sem[eng]
        self.q[eng].append(lambda e, f=fn, s=sobj: f(e).then_inc(s, 1))
        self._record(eng, self.cnt[eng], reads, writes)
        self.nops += 1

    def ops(self, eng, fns, reads=(), writes=()):
        self._edges(eng, reads, writes)
        self.cnt[eng] += 1
        sobj = self.sem[eng]
        fl = list(fns)

        def run(e, fl=fl, s=sobj):
            for f in fl[:-1]:
                f(e)
            fl[-1](e).then_inc(s, 1)

        self.q[eng].append(run)
        self._record(eng, self.cnt[eng], reads, writes)
        self.nops += len(fl)

    def dma(self, queue, fn, reads=(), writes=()):
        lane = self.dma_issue[queue] % self.NLANES
        self.dma_issue[queue] += 1
        semkey = f"d{queue}:{lane}"
        self._wait(queue, semkey, self.cnt[semkey])
        self._edges(queue, reads, writes)
        self.cnt[semkey] += 16
        sobj = self.dsem[semkey]
        self.q[queue].append(lambda e, f=fn, s=sobj: f(e).then_inc(s, 16))
        self._record(semkey, self.cnt[semkey], reads, writes)
        self.nops += 1

    def cc(self, fn, reads=(), writes=()):
        self._wait("POOL", "CC", self.cnt["CC"])
        self._edges("POOL", reads, writes)
        self.cnt["CC"] += 16
        sobj = self.ccsem
        self.q["POOL"].append(lambda e, f=fn, s=sobj: f(e).then_inc(s, 16))
        self._record("CC", self.cnt["CC"], reads, writes)
        self.nops += 1

    def barrier(self):
        semkeys = list(self.sem) + list(self.dsem) + ["CC"]
        for consumer in ENGS:
            for sk in semkeys:
                if self.cnt.get(sk, 0) > 0:
                    self._wait(consumer, sk, self.cnt[sk])

    def finish(self, keys):
        self._edges("SP", keys, ())

    def emit(self, block):
        qs = self.q

        @block.tensor
        def _(e):
            for f in qs["PE"]:
                f(e)

        @block.vector
        def _(e):
            for f in qs["DVE"]:
                f(e)

        @block.scalar
        def _(e):
            for f in qs["ACT"]:
                f(e)

        @block.gpsimd
        def _(e):
            for f in qs["POOL"]:
                f(e)

        @block.sync
        def _(e):
            for f in qs["SP"]:
                f(e)


def bc_in(ap, n):
    return ap.unsqueeze(2).to_broadcast((ap.shape[0], ap.shape[1], n))


class Builder:
    def __init__(self, cfg: Cfg):
        self.cfg = cfg
        self.nc = bass.Bass("TRN2", target_bir_lowering=False, debug=False)
        self.declare_io()
        self.declare_sbuf()
        self.sc = Sched(self.nc)

    def declare_io(self):
        nc, cfg = self.nc, self.cfg
        L, NL = cfg.L, cfg.nlayer

        def D(name, shape, dt, kind=None):
            if kind:
                return nc.dram_tensor(name, shape, dt, kind=kind).ap()
            return nc.dram_tensor(name, shape, dt).ap()

        self.x_fm = D("x_fm", [D_MODEL, L], F32, "ExternalInput")
        self.wi = D("wi", [NL, D_MODEL, D_IN], F32, "ExternalInput")
        self.wo = D("wo", [NL, D_INNER, D_MODEL], F32, "ExternalInput")
        self.dcwL = D("dcwL", [NL, NXC * DCONV, 128, 128], F32, "ExternalInput")
        self.cbL = D("cbL", [NL, 128, NXC], F32, "ExternalInput")
        self.dtbL = D("dtbL", [NL, NH, 1], F32, "ExternalInput")
        self.alogL = D("alogL", [NL, NH, 1], F32, "ExternalInput")
        self.dL = D("dL", [NL, NH, 1], F32, "ExternalInput")
        self.gnwR = D("gnwR", [NL, 128, NZC], F32, "ExternalInput")
        self.bnwR = D("bnwR", [NL, 128, NDC], F32, "ExternalInput")
        self.fnwR = D("fnwR", [128, NDC], F32, "ExternalInput")
        self.w_gate = D("w_gate", [FFN_I, D_MODEL], F32, "ExternalInput")
        self.w_up = D("w_up", [FFN_I, D_MODEL], F32, "ExternalInput")
        self.w_down = D("w_down", [D_MODEL, FFN_I], F32, "ExternalInput")
        self.c_e64x = D("c_e64x", [NH, 2048], BF16, "ExternalInput")
        self.c_e64xb = D("c_e64xb", [NH, 2048], BF16, "ExternalInput")
        self.c_trid = D("c_trid", [32, 2048], BF16, "ExternalInput")
        self.c_tri16 = D("c_tri16", [32, 16 * 128], BF16, "ExternalInput")
        self.c_esel = D("c_esel", [NH, D_INNER], BF16, "ExternalInput")
        self.c_idb = D("c_idb", [128, 128], BF16, "ExternalInput")
        self.c_ones2 = D("c_ones2", [2, 128], BF16, "ExternalInput")
        self.c_ones64 = D("c_ones64", [NH, 128], BF16, "ExternalInput")
        self.c_onescol = D("c_onescol", [128, 1], BF16, "ExternalInput")
        self.out_fm = D("out_fm", [D_MODEL, L], F32, "ExternalOutput")
        st = cfg.stage
        if st in ("s1",):
            self.y_out = D("y_out", [D_MODEL, L], BF16, "ExternalOutput")
        if st in ("s1", "s2", "s3", "s4"):
            self.mx_out = D("mx_out", [1, 6], F32, "ExternalOutput")
        if st in ("s2", "s3", "s4", "s5"):
            self.mxcat = D("mxcat", [6, cfg.n_cores], F32, "ExternalInput")
        if st == "s2":
            self.y_in = D("y_in", [D_MODEL, L], BF16, "ExternalInput")
            self.upraw_out = D("upraw_out", [FFN_I, L], BF16, "ExternalOutput")
            self.gateraw_out = D("gateraw_out", [FFN_I, L], BF16, "ExternalOutput")
        if st == "s3":
            self.gateraw_in = D("gateraw_in", [FFN_I, L], BF16, "ExternalInput")
            self.t4_out = D("t4_out", [FFN_I, L], BF16, "ExternalOutput")
        if st == "s4":
            self.t4_in = D("t4_in", [FFN_I, L], BF16, "ExternalInput")
            self.upraw_in = D("upraw_in", [FFN_I, L], BF16, "ExternalInput")
            self.t5_out = D("t5_out", [FFN_I, L], BF16, "ExternalOutput")
        if st == "s5":
            self.t5_in = D("t5_in", [FFN_I, L], BF16, "ExternalInput")
            self.x1_in = D("x1_in", [D_MODEL, L], F32, "ExternalInput")
        self.gmax_d = D("gmax_d", [1, 6], F32)
        self.mxr_d = D("mxr_d", [6, cfg.n_cores], F32)
        self.wi_bf = D("wi_bf", [NL, D_MODEL, D_IN], BF16)
        self.wo_bf = D("wo_bf", [NL, D_INNER, D_MODEL], BF16)
        self.dcw_bf = D("dcw_bf", [NL, NXC * DCONV, 128, 128], BF16)
        self.xs_b = [D(f"xs_b{i}", [D_INNER, 128], BF16) for i in range(2)]
        self.lt_b = [D(f"lt_b{i}", [2, 128, 16], BF16) for i in range(2)]
        self.red_b = D("red_b", [128, 1], F32)
        self.cc_in = D("cc_in", [1, 4], F32)
        self.cc_out = D("cc_out", [1, 4], F32)
        if cfg.do_ffn:
            self.wg_dq = D("wg_dq", [FFN_I, D_MODEL], BF16)
            self.wu_dq = D("wu_dq", [FFN_I, D_MODEL], BF16)
            self.wd_dq = D("wd_dq", [D_MODEL, FFN_I], BF16)
            self.up_d = D("up_d", [FFN_I, L], BF16)
            self.gate_d = D("gate_d", [FFN_I, L], BF16)

    def declare_sbuf(self):
        nc, cfg = self.nc, self.cfg
        L, T = cfg.L, cfg.TSEG
        self._off = 16640

        def A(name, shape, dtype, at=None):
            if at is None:
                nbytes = shape[1] * mybir.dt.size(dtype)
                t = nc.alloc_sbuf_tensor_at(name, shape, dtype, offset=self._off)
                self._off += (nbytes + 31) // 32 * 32
                return t
            return nc.alloc_sbuf_tensor_at(name, shape, dtype, offset=at)

        self.h = A("h", [128, NDC * L], F32)
        self.sE64X = [A(f"sE64X{i}", [98, 2048], BF16) for i in range(2)]
        self.sESEL = A("sESEL", [NH, D_INNER], BF16)
        self.sIDB = A("sIDB", [128, 128], BF16)
        self.sONES2 = A("sONES2", [2, 128], BF16)
        self.sONES64 = A("sONES64", [NH, 128], BF16)
        self.sONESC = A("sONESC", [128, 1], BF16)
        self.sCB = A("sCB", [128, NXC], F32)
        self.sGNWC = A("sGNWC", [128, NZC], F32)
        self.sBNWC = A("sBNWC", [128, NDC], F32)
        self.sFNWC = A("sFNWC", [128, NDC], F32)
        self.sDTB = A("sDTB", [NH, 1], F32)
        self.sALOG = A("sALOG", [NH, 1], F32)
        self.sACOL = A("sACOL", [NH, 1], F32)
        self.sDCOL = A("sDCOL", [NH, 1], F32)
        self.sDBF = A("sDBF", [NH, 1], BF16)
        self.sDEXP = A("sDEXP", [128, NZC], F32)
        self.sCARRY = A("sCARRY", [128, NXC * (DCONV - 1)], BF16)
        self.sZERO64 = A("sZERO64", [NH, 128], F32)
        self._sEPS1 = A("sEPS1", [128, 2], F32)
        self.sEPS = self._sEPS1.ap()[:, 0:1]
        self.sONE1 = self._sEPS1.ap()[:, 1:2]
        self.sEVAC = A("sEVAC", [128, 516], F32)
        self.sXB = A("sXB", [128, 512], BF16)
        self.sXB2 = A("sXB2", [128, 512], BF16)
        self.sROW = A("sROW", [1, 512], F32)
        self.sROW2 = A("sROW2", [1, 512], F32)
        self.sROW3 = A("sROW3", [1, 512], F32)
        self.sRH = A("sRH", [1, 512], BF16)
        self.sRL = A("sRL", [1, 512], BF16)
        self.sRED = A("sRED", [1, 128], F32)
        self.sSCR = A("sSCR", [1, 16], F32)
        self.sWIB = [A(f"sWIB{i}", [128, 2048], BF16) for i in range(2)]
        self.sWOB = self.sWIB
        self.sXTILE = A("sXTILE", [128, 512], F32)
        self.sXBCB = A("sXBCB", [128, 515], BF16)
        self.sCACC = A("sCACC", [128, 512], F32)
        self.sCH1 = self.sCACC.ap()[:, 0:256].bitcast(BF16)
        self.sCH2 = self.sCACC.ap()[:, 256:512].bitcast(BF16)
        self.sMXC = A("sMXC", [128, 8], F32)
        self.sSC = A("sSC", [128, 8], F32)
        seg_base = self._off
        self.sS = A("sS", [128, D_INNER], F32)
        self.sSBF = A("sSBF", [128, D_INNER], BF16)
        self.sHN = A("sHN", [128, NDC * T], BF16)
        self.sZ = A("sZ", [128, NZC * T], BF16)
        self.sCO = A("sCO", [128, NXC * T], BF16)
        self.sY = A("sY", [128, NZC * T], BF16)
        self.sDT = A("sDT", [NH, T], F32)
        self.sAA = A("sAA", [NH, T], F32)
        self.sL = [A(f"sL{i}", [NH, 128], F32) for i in range(2)]
        self.sHL = [A(f"sHL{i}", [128, 128], BF16) for i in range(2)]
        self.sMLH = A("sMLH", [98, 16 * 128], BF16)
        self.sLEND = [A(f"sLEND{i}", [NH, 1], F32) for i in range(2)]
        self.sGCOL = [A(f"sGCOL{i}", [NH, 1], F32) for i in range(2)]
        self.sEV2 = A("sEV2", [NH, 128], F32)
        self.sDV = A("sDV", [128, 128], BF16)
        self.sDVT = A("sDVT", [128, 128], BF16)
        self.sEXPL = [A(f"sEXPL{i}", [NH, 128], BF16) for i in range(2)]
        self.sGAM = [A(f"sGAM{i}", [NH, 128], BF16) for i in range(2)]
        self.sG = [A(f"sG{i}", [128, 128], BF16) for i in range(2)]
        self.sBT = [A(f"sBT{i}", [128, 128], BF16) for i in range(2)]
        self.sLTB = [A(f"sLTB{i}", [128, 128], BF16) for i in range(2)]
        xt_base = self._off
        self.sXT = [A(f"sXT{i}", [128, D_INNER], BF16) for i in range(2)]
        self.sWXT = A("sWXT", [128, D_INNER], BF16)
        self.sMG2 = [A(f"sMG{i}", [128, 16 * 128], BF16) for i in range(2)]
        self.sDCW = A("sDCW", [128, NXC * DCONV * 128], BF16, at=xt_base)
        seg_top = self._off
        if cfg.do_ffn:
            off = seg_base
            def F(name, shape, dtype):
                nonlocal off
                t = nc.alloc_sbuf_tensor_at(name, shape, dtype, offset=off)
                off += (shape[1] * mybir.dt.size(dtype) + 31) // 32 * 32
                return t
            self.sFW = [F(f"sFW{i}", [128, 4096], BF16) for i in range(2)]
            self.sBLK = F("sBLK", [128, 96], F32)
            self.sRS = F("sRS", [128, 96], F32)
            self.sDQ1 = F("sDQ1", [128, 3072], F32)
            self.sO = F("sO", [128, 24 * 512], BF16)
            self.sT1 = F("sT1", [128, 512], BF16)
            self.sT2 = F("sT2", [128, 512], BF16)
            self.sT3 = F("sT3", [128, 512], BF16)
            assert off <= seg_top, (off, seg_top)

        P = nc.alloc_psum_tensor
        self.pPS = P("pPS", [128, 4096], F32)

    def pe_transpose_bf(self, src_ap, dst_sbuf_ap, rkeys, wkeys):
        sc = self.sc
        n_p, n_f = src_ap.shape[0], src_ap.shape[1]
        sc.ops("PE", [lambda e: e.transpose(out=self.pPS.ap()[:, 3968:4096].bitcast(BF16)[0:n_f, 0:n_p], in_=src_ap,
                                            identity=self.sIDB[0:n_p, 0:n_p])],
               reads=list(rkeys) + ["cIDB"], writes=["P7"])
        sc.op("DVE", lambda e: e.tensor_copy(out=dst_sbuf_ap, in_=self.pPS.ap()[:, 3968:4096].bitcast(BF16)[0:n_f, 0:n_p]),
              reads=["P7"], writes=list(wkeys))

    def row_hilo(self, row_ap, T, rkey):
        sc = self.sc
        sc.op("DVE", lambda e: e.tensor_copy(out=self.sRH[0:1, 0:T], in_=row_ap),
              reads=[rkey], writes=["rhl"])
        sc.op("DVE", lambda e: e.tensor_tensor(out=self.sRL[0:1, 0:T], in0=row_ap,
                                               in1=self.sRH[0:1, 0:T], op=ALU.subtract),
              reads=[rkey, "rhl"], writes=["rhl"])

    def bcast_row(self, lhsT_ap, lkeys, T, psum_ap, pkey):
        sc = self.sc
        sc.ops("PE", [lambda e: e.matmul(out=psum_ap, lhsT=lhsT_ap,
                                         rhs=self.sRH[0:1, 0:T], start=True, stop=False),
                      lambda e: e.matmul(out=psum_ap, lhsT=lhsT_ap,
                                         rhs=self.sRL[0:1, 0:T], start=False, stop=True)],
               reads=list(lkeys) + ["rhl"], writes=[pkey])

    def load_consts(self):
        sc = self.sc
        pairs = [(self.sESEL, self.c_esel, "cESEL"),
                 (self.sIDB, self.c_idb, "cIDB"), (self.sONES2, self.c_ones2, "cONES2"),
                 (self.sONES64, self.c_ones64, "cONES64"),
                 (self.sONESC, self.c_onescol, "cONESC")]
        for dst, src, key in pairs:
            sc.dma("SP", lambda e, d=dst, s=src: e.dma_start(out=d[:], in_=s), writes=[key])
        for i, csrc in ((0, self.c_e64x), (1, self.c_e64xb)):
            sc.dma("SP", lambda e, i=i, csrc=csrc: e.dma_start(
                out=self.sE64X[i].ap()[0:64, :], in_=csrc[:]), writes=["cE64X"])
            sc.dma("SP", lambda e, i=i: e.dma_start(
                out=self.sE64X[i].ap()[66:98, :], in_=self.c_trid[:]), writes=["cE64X"])
        sc.dma("SP", lambda e: e.dma_start(out=self.sMLH.ap()[66:98, :],
                                           in_=self.c_tri16[:]), writes=["cMLH"])
        sc.op("DVE", lambda e: e.memset(self.sMLH.ap()[64:66, :], 1.0),
              writes=["cMLH"])
        sc.dma("POOL", lambda e: e.dma_start(out=self.sFNWC[:], in_=self.fnwR[:]),
               writes=["fnwb"])
        sc.op("DVE", lambda e: e.memset(self.sZERO64[:], 0.0), writes=["zero64"])
        sc.op("DVE", lambda e: e.memset(self.sEPS[:], LN_EPS), writes=["ceps"])
        sc.op("DVE", lambda e: e.memset(self.sONE1[:], 1.0), writes=["cone"])
        sc.op("DVE", lambda e: e.memset(self.sCARRY[:], 0.0), writes=["carry"])
        sc.op("DVE", lambda e: e.memset(self.sMXC[:], 0.0), writes=["mxc"])
        sc.op("DVE", lambda e: e.memset(self.sSCR[:], 0.0), writes=["scr"])
        sc.dma("ACT", lambda e: e.dma_start(out=self.cc_in, in_=self.sSCR[0:1, 0:4]),
               reads=["scr"], writes=["ccin"])

    def preconvert_weights(self):
        sc = self.sc
        for l in range(self.cfg.nlayer):
            sc.dma("POOL", lambda e, l=l: e.dma_start(out=self.wi_bf[l], in_=self.wi[l]),
                   writes=[f"wi_bf:{l}"])
            sc.dma("POOL", lambda e, l=l: e.dma_start(out=self.wo_bf[l], in_=self.wo[l]),
                   writes=[f"wo_bf:{l}"])
            sc.dma("POOL", lambda e, l=l: e.dma_start(out=self.dcw_bf[l], in_=self.dcwL[l]),
                   writes=[f"dcw_bf:{l}"])

    def load_layer_params(self, l):
        sc = self.sc
        for dst, src, key in [(self.sCB, self.cbL[l], "cb"),
                              (self.sDTB, self.dtbL[l], "dtb"), (self.sALOG, self.alogL[l], "alog"),
                              (self.sDCOL, self.dL[l], "dcol")]:
            sc.dma("SP", lambda e, d=dst, s=src: e.dma_start(out=d[:], in_=s), writes=[key])
        sc.dma("POOL", lambda e, l=l: e.dma_start(out=self.sGNWC[:], in_=self.gnwR[l]),
               writes=["gnwb"])
        sc.dma("POOL", lambda e, l=l: e.dma_start(out=self.sBNWC[:], in_=self.bnwR[l]),
               writes=["bnwb"])
        sc.op("ACT", lambda e: e.activation(out=self.sACOL[:], in_=self.sALOG[:], func=AF.Exp),
              reads=["alog"], writes=["acol"])
        sc.op("DVE", lambda e: e.tensor_scalar(out=self.sACOL[:], in0=self.sACOL[:],
                                               scalar1=-1.0, scalar2=None, op0=ALU.mult),
              reads=["acol"], writes=["acol"])
        sc.op("DVE", lambda e: e.tensor_copy(out=self.sDBF[:], in_=self.sDCOL[:]),
              reads=["dcol"], writes=["dbf"])
        sc.ops("PE", [(lambda e, k=k: e.matmul(out=self.pPS[:, 3072 + k:3072 + k + 1],
                                               lhsT=self.sESEL[:, 128 * k:128 * (k + 1)],
                                               rhs=self.sDBF[:], start=True, stop=True))
                      for k in range(NZC)],
               reads=["dbf", "cESEL"], writes=["P6"])
        sc.op("DVE", lambda e: e.tensor_copy(out=self.sDEXP[:], in_=self.pPS[:, 3072 + 0:3072 + NZC]),
              reads=["P6"], writes=["dexp"])
        sc.op("DVE", lambda e: e.memset(self.sS[:], 0.0), writes=["S"])
        sc.op("DVE", lambda e: e.memset(self.sCARRY[:], 0.0), writes=["carry"])

    def rms_rows(self, T, get_tile, rkeys, nchunks, mean_div, prep=None):
        sc = self.sc
        for k in range(nchunks):
            if prep is not None:
                prep(k)
            ap = get_tile(k)
            sc.op("ACT", lambda e, ap=ap: e.activation(out=self.sXB[:, 0:T], in_=ap,
                                                       func=AF.Square),
                  reads=rkeys(k), writes=["sXB"])
            sc.ops("PE", [lambda e, k=k: e.matmul(out=self.pPS[0:1, 3072 + 0:3072 + T],
                                                  lhsT=self.sONESC[:], rhs=self.sXB[:, 0:T],
                                                  start=(k == 0), stop=(k == nchunks - 1))],
                   reads=["sXB", "cONESC"], writes=["P6"])
        sc.op("ACT", lambda e: e.activation(out=self.sROW2[0:1, 0:T], in_=self.pPS[0:1, 3072 + 0:3072 + T],
                                            func=AF.Ln, scale=1.0 / mean_div,
                                            bias=self.sEPS[0:1]),
              reads=["P6", "ceps"], writes=["rowB0"])
        sc.op("ACT", lambda e: e.activation(out=self.sROW[0:1, 0:T], in_=self.sROW2[0:1, 0:T],
                                            func=AF.Exp, scale=-0.5),
              reads=["rowB0"], writes=["rowA0"])

    def ln_rows(self, T, get_tile, rkeys, prep=None):
        sc = self.sc
        for k in range(NDC):
            if prep is not None:
                prep(k)
            ap = get_tile(k)
            sc.op("DVE", lambda e, ap=ap: e.tensor_copy(out=self.sXB2[:, 0:T], in_=ap),
                  reads=rkeys(k), writes=["sXB2"])
            sc.op("ACT", lambda e, ap=ap: e.activation(out=self.sXB[:, 0:T], in_=ap,
                                                       func=AF.Square),
                  reads=rkeys(k), writes=["sXB"])
            sc.ops("PE", [lambda e, k=k: e.matmul(out=self.pPS[0:1, 3072 + 0:3072 + T], lhsT=self.sONESC[:],
                                                  rhs=self.sXB2[:, 0:T], start=(k == 0),
                                                  stop=(k == NDC - 1))],
                   reads=["sXB2", "cONESC"], writes=["P6"])
            sc.ops("PE", [lambda e, k=k: e.matmul(out=self.pPS[0:1, 3584:3584 + T],
                                                  lhsT=self.sONESC[:],
                                                  rhs=self.sXB[:, 0:T], start=(k == 0),
                                                  stop=(k == NDC - 1))],
                   reads=["sXB", "cONESC"], writes=["P7"])
        sc.op("DVE", lambda e: e.tensor_scalar(out=self.sROW3[0:1, 0:T],
                                               in0=self.pPS[0:1, 3072 + 0:3072 + T],
                                               scalar1=1.0 / D_MODEL, scalar2=None, op0=ALU.mult),
              reads=["P6"], writes=["rowA1"])
        sc.op("DVE", lambda e: e.tensor_scalar(out=self.sROW2[0:1, 0:T],
                                               in0=self.pPS[0:1, 3584:3584 + T],
                                               scalar1=1.0 / D_MODEL, scalar2=None, op0=ALU.mult),
              reads=["P7"], writes=["rowB0"])
        sc.op("DVE", lambda e: e.tensor_tensor(out=self.sROW[0:1, 0:T],
                                               in0=self.sROW3[0:1, 0:T],
                                               in1=self.sROW3[0:1, 0:T], op=ALU.mult),
              reads=["rowA1"], writes=["rowA0"])
        sc.op("DVE", lambda e: e.tensor_tensor(out=self.sROW2[0:1, 0:T],
                                               in0=self.sROW2[0:1, 0:T],
                                               in1=self.sROW[0:1, 0:T], op=ALU.subtract),
              reads=["rowB0", "rowA0"], writes=["rowB0"])
        sc.op("ACT", lambda e: e.activation(out=self.sROW2[0:1, 0:T], in_=self.sROW2[0:1, 0:T],
                                            func=AF.Ln, bias=self.sEPS[0:1]),
              reads=["rowB0", "ceps"], writes=["rowB0"])
        sc.op("ACT", lambda e: e.activation(out=self.sROW[0:1, 0:T], in_=self.sROW2[0:1, 0:T],
                                            func=AF.Exp, scale=-0.5),
              reads=["rowB0"], writes=["rowA0"])
        sc.op("DVE", lambda e: e.tensor_tensor(out=self.sROW2[0:1, 0:T],
                                               in0=self.sROW3[0:1, 0:T],
                                               in1=self.sROW[0:1, 0:T], op=ALU.mult),
              reads=["rowA1", "rowA0"], writes=["rowB0"])

    def ln_bcasts(self, T):
        self.row_hilo(self.sROW[0:1, 0:T], T, "rowA0")
        self.bcast_row(self.sONES2[0:1, :], ["cONES2"], T, self.pPS[:, 1024 + 0:1024 + T], "P2")
        self.row_hilo(self.sROW2[0:1, 0:T], T, "rowB0")
        self.bcast_row(self.sONES2[0:1, :], ["cONES2"], T, self.pPS[:, 1024 + 512:1024 + 512 + T], "P3")

    def phase_init_ln(self):
        cfg, sc = self.cfg, self.sc
        L = cfg.L
        for j in range(L // 512):
            t0 = 512 * j

            def prep(k, t0=t0):
                sc.dma("SP", lambda e, k=k: e.dma_start(
                    out=self.sXTILE[:],
                    in_=self.x_fm[128 * k:128 * (k + 1), t0:t0 + 512]),
                    writes=["xt"])
            self.ln_rows(512, lambda k: self.sXTILE[:], lambda k: ["xt"], prep=prep)
            self.ln_bcasts(512)
            for k in range(NDC):
                prep(k)
                hsl = self.h[:, k * L + t0: k * L + t0 + 512]
                sc.op("DVE", lambda e: e.tensor_tensor(
                    out=self.sEVAC[:, 0:512], in0=self.sXTILE[:],
                    in1=self.pPS[:, 1024 + 0:1024 + 512], op=ALU.mult),
                    reads=["xt", "P2"], writes=["evac"])
                sc.op("DVE", lambda e, hsl=hsl: e.tensor_tensor(
                    out=hsl, in0=self.sEVAC[:, 0:512], in1=self.pPS[:, 1024 + 512:1024 + 1024],
                    op=ALU.subtract),
                    reads=["evac", "P3"], writes=[f"h:{k}:{j}"])

    def phase_A(self, l, seg):
        cfg, sc = self.cfg, self.sc
        T, L = cfg.TSEG, cfg.L
        t0 = seg * T
        hsl = lambda k: self.h[:, k * L + t0: k * L + t0 + T]
        self.rms_rows(T, hsl, lambda k: [f"h:{k}:{seg}"], NDC, D_MODEL)
        self.row_hilo(self.sROW[0:1, 0:T], T, "rowA0")
        self.bcast_row(self.sONES2[0:1, :], ["cONES2"], T,
                       self.pPS[:, 1024 + 0:1024 + T], "P2")
        for k in range(NDC):
            hap = hsl(k)
            sc.op("DVE", lambda e, k=k, hap=hap: e.scalar_tensor_tensor(
                out=self.sHN[:, k * T:(k + 1) * T], in0=hap,
                scalar=self.sBNWC[:, k:k + 1], in1=self.pPS[:, 1024 + 0:1024 + T],
                op0=ALU.mult, op1=ALU.mult),
                reads=[f"h:{k}:{seg}", "P2", "bnwb"], writes=[f"hn:{k}"])
        hn_keys = [f"hn:{k}" for k in range(NDC)]
        sc.dma("SP", lambda e, l=l: e.dma_start(
            out=self.sDCW.ap().rearrange("p (m n) -> p m n", m=NXC * DCONV),
            in_=self.dcw_bf[l].rearrange("m p n -> p m n")),
            reads=[f"dcw_bf:{l}", "xst:0", "xst:1", "dxt", "wxt", "mg:0", "mg:1"],
            writes=["dcw"])
        for pos, np_ in enumerate([17, 16] + list(range(8, 16)) + list(range(8))):
            buf = pos % 2
            wkey = f"wib:{buf}"
            w = 256 if np_ < 17 else 64
            nof = 256 * np_
            sc.dma("SP", lambda e, buf=buf, w=w, nof=nof, l=l: e.dma_start(
                out=self.sWIB[buf].ap()[:, 0:NDC * w].rearrange(
                    "p (k n) -> p k n", k=NDC),
                in_=self.wi_bf[l][:, nof:nof + w].rearrange("(k p) n -> p k n", p=128)),
                reads=[f"wi_bf:{l}"], writes=[wkey])
            for sub in range(2 if np_ < 17 else 1):
                n = 2 * np_ + sub
                ncols = 128 if n < 34 else 64
                pb = n % 2
                pk = f"P{pb}"
                psl = self.pPS[:, 512 * pb:512 * pb + T]
                sc.ops("PE", [(lambda e, k=k, buf=buf, w=w, sub=sub, ncols=ncols, psl=psl:
                               e.matmul(out=psl[0:ncols, :] if ncols < 128 else psl,
                                        lhsT=self.sWIB[buf][:, w * k + 128 * sub:
                                                            w * k + 128 * sub + ncols],
                                        rhs=self.sHN[:, k * T:(k + 1) * T],
                                        start=(k == 0), stop=(k == NDC - 1)))
                              for k in range(NDC)],
                       reads=[wkey] + hn_keys, writes=[pk])
                if n < NZC:
                    sc.op("ACT", lambda e, n=n, psl=psl: e.activation(
                        out=self.sZ[:, n * T:(n + 1) * T], in_=psl, func=AF.Silu),
                        reads=[pk], writes=[f"z:{n}"])
                elif n < 34:
                    self.conv_chunk(n - NZC, psl, pk)
                else:
                    self.dt_chunk(psl, pk)
        sc.op("DVE", lambda e: e.tensor_scalar(out=self.sAA[:, 0:T], in0=self.sDT[:, 0:T],
                                               scalar1=self.sACOL[:], scalar2=None,
                                               op0=ALU.mult),
              reads=["dt", "acol"], writes=["aa"])

    def conv_chunk(self, j, psl, pk):
        sc, T = self.sc, self.cfg.TSEG
        xb = self.sXBCB
        sc.op("DVE", lambda e, j=j: e.tensor_copy(out=xb[:, 0:3],
                                                  in_=self.sCARRY[:, 3 * j:3 * j + 3]),
              reads=["carry"], writes=["xbcb"])
        sc.op("DVE", lambda e: e.tensor_copy(out=xb[:, 3:3 + T], in_=psl),
              reads=[pk], writes=["xbcb"])
        sc.op("DVE", lambda e, j=j: e.tensor_copy(out=self.sCARRY[:, 3 * j:3 * j + 3],
                                                  in_=xb[:, T:T + 3]),
              reads=["xbcb"], writes=["carry"])
        cb_bank = 2 + (j % 2)
        cps = self.pPS[:, 512 * cb_bank:512 * cb_bank + T]

        def mk(e, j=j, cps=cps):
            r = None
            for kk in range(DCONV):
                r = e.matmul(out=cps, lhsT=self.sDCW[:, 128 * (4 * j + kk):
                                                     128 * (4 * j + kk) + 128],
                             rhs=xb[:, kk:kk + T],
                             start=(kk == 0), stop=(kk == DCONV - 1))
            return r
        sc.ops("PE", [mk], reads=["dcw", "xbcb"], writes=[f"P{cb_bank}"])
        sc.op("ACT", lambda e, j=j, cps=cps: e.activation(
            out=self.sCO[:, j * T:(j + 1) * T], in_=cps, func=AF.Silu,
            bias=self.sCB[:, j:j + 1]),
            reads=[f"P{cb_bank}", "cb"],
            writes=[f"co:{j}"] + [f"cox:{cc}" for cc in range(4)])

    def dt_chunk(self, psl, pk):
        sc, T = self.sc, self.cfg.TSEG
        t1 = self.sEVAC[0:64, 0:T]
        t4 = self.sXTILE[0:64, 0:T]
        ps = psl[0:64, :]
        sc.op("ACT", lambda e: e.activation(out=t1, in_=ps, func=AF.Abs, bias=self.sDTB[:]),
              reads=[pk, "dtb"], writes=["evac"])
        sc.op("ACT", lambda e: e.activation(out=t1, in_=t1, func=AF.Exp, scale=-1.0),
              reads=["evac"], writes=["evac"])
        sc.op("ACT", lambda e: e.activation(out=t1, in_=t1, func=AF.Ln,
                                            bias=self.sONE1[0:64]),
              reads=["evac", "cone"], writes=["evac"])
        sc.op("ACT", lambda e: e.activation(out=t4, in_=ps, func=AF.Relu, bias=self.sDTB[:]),
              reads=[pk, "dtb", "xt"], writes=["xt"])
        sc.op("DVE", lambda e: e.tensor_tensor(out=self.sDT[:, 0:T], in0=t1, in1=t4,
                                               op=ALU.add),
              reads=["evac", "xt"], writes=["dt"])

    def phase_B_prep_front(self, l, seg, c):
        cfg, sc = self.cfg, self.sc
        T, C = cfg.TSEG, cfg.C
        cof = c * C
        cbuf = c % 2
        sL, sHL = self.sL[cbuf], self.sHL[cbuf]
        kL, kHL = f"Lc:{cbuf}", f"hl:{cbuf}"
        kEND, kGC = f"Lend:{cbuf}", f"gcol:{cbuf}"
        sc.op("DVE", lambda e, cof=cof, sL=sL: e.tensor_tensor_scan(
            out=sL[:], data0=self.sAA[:, cof:cof + C], data1=self.sZERO64[:],
            initial=0.0, op0=ALU.add, op1=ALU.add),
            reads=["aa", "zero64"], writes=[kL])
        sc.op("DVE", lambda e, sL=sL, sHL=sHL: e.tensor_copy(
            out=sHL[0:64, :], in_=sL[:]), reads=[kL], writes=[kHL])
        sc.op("DVE", lambda e, sL=sL, sHL=sHL: e.tensor_tensor(
            out=sHL[64:128, :], in0=sL[:], in1=sHL[0:64, :], op=ALU.subtract),
            reads=[kL, kHL], writes=[kHL])
        for h2 in range(2):
            b0 = 8 * cbuf + 4 * h2
            wk = [f"mlh:{b0 + sl}" for sl in range(4)]
            out_hi = self.sMLH.ap()[0:32, 128 * b0:128 * b0 + 512].rearrange(
                "p (s n) -> p s n", s=4)
            out_lo = self.sMLH.ap()[32:64, 128 * b0:128 * b0 + 512].rearrange(
                "p (s n) -> p s n", s=4)
            sc.op("DVE", lambda e, out_hi=out_hi, h2=h2, sHL=sHL: e.tensor_copy(
                out=out_hi,
                in_=sHL[32 * h2:32 * h2 + 32, :].unsqueeze(1).to_broadcast((32, 4, 128))),
                reads=[kHL], writes=wk)
            sc.op("DVE", lambda e, out_lo=out_lo, h2=h2, sHL=sHL: e.tensor_copy(
                out=out_lo,
                in_=sHL[64 + 32 * h2:64 + 32 * h2 + 32, :].unsqueeze(1).to_broadcast((32, 4, 128))),
                reads=[kHL], writes=wk)
        sc.op("DVE", lambda e, sL=sL, cbuf=cbuf: e.tensor_copy(
            out=self.sLEND[cbuf][:], in_=sL[:, C - 1:C]), reads=[kL], writes=[kEND])
        sc.op("ACT", lambda e, cbuf=cbuf: e.activation(
            out=self.sGCOL[cbuf][:], in_=self.sLEND[cbuf][:], func=AF.Exp),
            reads=[kEND], writes=[kGC])
        sc.op("DVE", lambda e, cbuf=cbuf: e.tensor_scalar(
            out=self.sGAM[cbuf][:], in0=self.sONES64[:],
            scalar1=self.sGCOL[cbuf][:], scalar2=None, op0=ALU.mult),
            reads=["cONES64", kGC], writes=[f"gam:{cbuf}"])
        sc.op("ACT", lambda e, sL=sL, cbuf=cbuf: e.activation(
            out=self.sEXPL[cbuf][:], in_=sL[:], func=AF.Exp),
            reads=[kL], writes=[f"expl:{cbuf}"])
        sc.op("ACT", lambda e, sL=sL, cbuf=cbuf: e.activation(
            out=self.sEV2[:], in_=sL[:], func=AF.Exp,
            scale=-1.0, bias=self.sLEND[cbuf][:]),
            reads=[kL, kEND], writes=["ev2"])
        sc.op("DVE", lambda e, cof=cof: e.tensor_copy(
            out=self.sDV.ap()[0:64, :], in_=self.sDT[:, cof:cof + C]),
            reads=["dt"], writes=["dv"])
        sc.op("DVE", lambda e, cof=cof: e.tensor_tensor(
            out=self.sDV.ap()[64:128, :], in0=self.sEV2[:],
            in1=self.sDT[:, cof:cof + C], op=ALU.mult),
            reads=["ev2", "dt"], writes=["dv"])
        co3 = self.sCO.ap().rearrange("p (j t) -> p j t", j=NXC)
        xs_src = co3[:, 0:16, cof:cof + C]
        xbounce = self.xs_b[cbuf]
        sc.dma("SP", lambda e, xs_src=xs_src, xbounce=xbounce: e.dma_start(
            out=xbounce.rearrange("(j p) t -> p j t", p=128), in_=xs_src),
            reads=[f"cox:{c}"], writes=[f"xsb:{cbuf}"])
        sc.dma("SP", lambda e, cbuf=cbuf: e.dma_start_transpose(
            out=self.sXT[cbuf][:], in_=self.xs_b[cbuf]),
            reads=[f"xsb:{cbuf}"], writes=[f"xst:{cbuf}"])

    def phase_B_prep_back(self, l, seg, c):
        cfg, sc = self.cfg, self.sc
        T, C = cfg.TSEG, cfg.C
        cof = c * C
        cbuf = c % 2
        sHL = self.sHL[cbuf]
        kHL = f"hl:{cbuf}"
        co3 = self.sCO.ap().rearrange("p (j t) -> p j t", j=NXC)
        self.pe_transpose_bf(self.sDV[:], self.sDVT[:], ["dv"], ["dvt"])
        self.pe_transpose_bf(sHL[:], self.sLTB[cbuf][:], [kHL], [f"ltb:{cbuf}"])
        sc.op("DVE", lambda e, cbuf=cbuf: e.tensor_tensor(
            out=self.sWXT.ap().rearrange("p (h q) -> p h q", h=NH),
            in0=self.sXT[cbuf].ap().rearrange("p (h q) -> p h q", h=NH),
            in1=bc_in(self.sDVT.ap()[:, 64:128], HD), op=ALU.mult),
            reads=[f"xst:{cbuf}", "dvt"], writes=["wxt"])
        sc.op("DVE", lambda e, cbuf=cbuf: e.tensor_tensor(
            out=self.sXT[cbuf].ap().rearrange("p (h q) -> p h q", h=NH),
            in0=self.sXT[cbuf].ap().rearrange("p (h q) -> p h q", h=NH),
            in1=bc_in(self.sDVT.ap()[:, 0:64], HD), op=ALU.mult),
            reads=[f"xst:{cbuf}", "dvt", "wxt"], writes=["dxt", f"xst:{cbuf}"])
        bsl = co3[:, 16, cof:cof + C]
        csl = co3[:, 17, cof:cof + C]
        sc.ops("PE", [lambda e, bsl=bsl, csl=csl: e.matmul(
            out=self.pPS[:, 3584 + 0:3584 + 128], lhsT=bsl, rhs=csl, start=True, stop=True)],
            reads=[f"cox:{c}"], writes=["P7"])
        sc.op("DVE", lambda e, cbuf=cbuf: e.tensor_copy(
            out=self.sG[cbuf][:], in_=self.pPS[:, 3584 + 0:3584 + 128]),
            reads=["P7"], writes=[f"G:{cbuf}"])
        self.pe_transpose_bf(bsl, self.sBT[cbuf][:], [f"cox:{c}"], [f"BT:{cbuf}"])
        sc.op("DVE", lambda e, cof=cof, co3=co3: e.tensor_tensor(
            out=co3[:, 0:16, cof:cof + C], in0=co3[:, 0:16, cof:cof + C],
            in1=bc_in(self.sDEXP[:, 0:16], C), op=ALU.mult),
            reads=[f"cox:{c}", "dexp", f"xsb:{cbuf}"], writes=[f"cox:{c}"])

    def phase_B_compute(self, l, seg, c):
        cfg, sc = self.cfg, self.sc
        sc.op("DVE", lambda e: e.tensor_copy(out=self.sSBF[:], in_=self.sS[:]),
              reads=["S"], writes=["SBF"])
        self.state_update(c)
        if c + 1 < cfg.NCH:
            self.phase_B_prep_front(l, seg, c + 1)
        for hq in range(2):
            self.mask_quarter(c, hq)
            self.y_quarter(c, hq)
        if c + 1 < cfg.NCH:
            self.phase_B_prep_back(l, seg, c + 1)
        for hq in range(2, 4):
            self.mask_quarter(c, hq)
            self.y_quarter(c, hq)

    def mask_quarter(self, c, hq):
        sc = self.sc
        h2, s2 = hq // 2, hq % 2
        par = hq % 2
        cbuf = c % 2
        hgb = (4 * c + hq) % 2
        mgk = f"mg:{par}"
        sMG = self.sMG2[par]
        bb = self.lt_b[hgb]
        sLTB, kLTB = self.sLTB[cbuf], f"ltb:{cbuf}"
        sc.dma("SP", lambda e, hq=hq, bb=bb, sLTB=sLTB: e.dma_start(
            out=bb[0], in_=sLTB[:, 16 * hq:16 * hq + 16]),
            reads=[kLTB], writes=[f"ltbd:{hgb}"])
        sc.dma("SP", lambda e, hq=hq, bb=bb, sLTB=sLTB: e.dma_start(
            out=bb[1], in_=sLTB[:, 64 + 16 * hq:64 + 16 * hq + 16]),
            reads=[kLTB], writes=[f"ltbd:{hgb}"])
        sc.dma("SP", lambda e, bb=bb, s2=s2: e.dma_start(
            out=self.sE64X[s2].ap()[64:66, :], in_=bb.rearrange("r t h -> r (t h)")),
            reads=[f"ltbd:{hgb}"], writes=[f"flat:{s2}"])
        for sl in range(4):
            bank = (2, 3, 0, 1)[sl]
            po = self.pPS[:, 512 * bank:512 * bank + 512]
            i = 8 * cbuf + 4 * h2 + sl
            sc.ops("PE", [lambda e, po=po, i=i, sl=sl, s2=s2: e.matmul(
                out=po, lhsT=self.sMLH.ap()[0:98, 128 * i:128 * (i + 1)],
                rhs=self.sE64X[s2][0:98, 512 * sl:512 * (sl + 1)],
                start=True, stop=True)],
                reads=[f"mlh:{i}", "cMLH", "cE64X", f"flat:{s2}"],
                writes=[f"P{bank}"])
            sc.op("ACT", lambda e, sl=sl, po=po, sMG=sMG: e.activation(
                out=sMG[:, 512 * sl:512 * (sl + 1)], in_=po, func=AF.Exp),
                reads=[f"P{bank}"], writes=[mgk])
            mg3s = sMG.ap()[:, 512 * sl:512 * (sl + 1)].rearrange(
                "p (t h) -> p t h", h=16)
            sc.op("DVE", lambda e, mg3s=mg3s, sl=sl, cbuf=cbuf: e.tensor_tensor(
                out=mg3s, in0=mg3s,
                in1=bc_in(self.sG[cbuf].ap()[:, 32 * sl:32 * (sl + 1)], 16),
                op=ALU.mult),
                reads=[mgk, f"G:{cbuf}"], writes=[mgk])

    def y_quarter(self, c, hq):
        sc, cfg = self.sc, self.cfg
        T, C = cfg.TSEG, cfg.C
        cof = c * C
        cbuf = c % 2
        par = hq % 2
        mgk = f"mg:{par}"
        mg3 = self.sMG2[par].ap().rearrange("p (t h) -> p t h", h=16)
        co3 = self.sCO.ap().rearrange("p (j t) -> p j t", j=NXC)
        z3 = self.sZ.ap().rearrange("p (j t) -> p j t", j=NZC)
        y3 = self.sY.ap().rearrange("p (j t) -> p j t", j=NZC)
        csl = co3[:, 17, cof:cof + C]
        ks = list(range(4 * hq, 4 * hq + 4))
        pI = self.pPS[:, 2048:2560]
        pN = self.pPS[:, 2560:3072]
        pX = self.pPS[:, 3072:3584]
        sc.ops("PE", [(lambda e, k=k, i=i: e.matmul(
            out=pX[:, 128 * i:128 * (i + 1)],
            lhsT=self.sESEL[:, 128 * k:128 * (k + 1)], rhs=self.sEXPL[cbuf][:],
            start=True, stop=True)) for i, k in enumerate(ks)],
            reads=["cESEL", f"expl:{cbuf}"], writes=["P6"])
        sc.ops("PE", [(lambda e, k=k, i=i: e.matmul(
            out=pN[:, 128 * i:128 * (i + 1)],
            lhsT=self.sSBF[:, 128 * k:128 * (k + 1)], rhs=csl,
            start=True, stop=True)) for i, k in enumerate(ks)],
            reads=["SBF", f"cox:{c}"], writes=["P5"])

        def intra(e, ks=ks, hq=hq, cbuf=cbuf):
            r = None
            for i, k in enumerate(ks):
                for jj in range(4):
                    h_loc = (4 * k + jj) - 16 * hq
                    r = e.matmul(out=pI[32 * jj:32 * jj + 32, 128 * i:128 * (i + 1)],
                                 lhsT=self.sXT[cbuf][:, 128 * k + 32 * jj:
                                                     128 * k + 32 * jj + 32],
                                 rhs=mg3[:, :, h_loc],
                                 start=True, stop=True, tile_position=(0, 32 * jj))
            return r
        k0 = ks[0]
        xs_sl = co3[:, k0:k0 + 4, cof:cof + C]
        z_sl = z3[:, k0:k0 + 4, cof:cof + C]
        y_sl = y3[:, k0:k0 + 4, cof:cof + C]
        ch13 = self.sCH1.rearrange("p (j t) -> p j t", j=4)
        ch23 = self.sCH2.rearrange("p (j t) -> p j t", j=4)
        xs_keys = [f"cox:{c}"]
        z_keys = [f"z:{k}" for k in ks]
        y_keys = [f"y:{k}" for k in ks]
        sc.op("ACT", lambda e, pX=pX: e.activation(out=self.sCH1, in_=pX, func=AF.Copy),
              reads=["P6", "cacc"], writes=["ch1", "cacc"])
        sc.op("DVE", lambda e, pN=pN: e.tensor_tensor(out=self.sCH2, in0=pN,
                                                      in1=self.sCH1, op=ALU.mult),
              reads=["P5", "ch1", "cacc"], writes=["ch2", "cacc"])
        sc.ops("PE", [intra], reads=["dxt", mgk, f"xst:{cbuf}"], writes=["P4"])
        sc.op("DVE", lambda e, pI=pI: e.tensor_tensor(out=self.sCH1, in0=pI,
                                                      in1=self.sCH2, op=ALU.add),
              reads=["P4", "ch2", "cacc"], writes=["ch1", "cacc"])
        sc.op("POOL", lambda e, xs_sl=xs_sl, ch13=ch13, ch23=ch23: e.tensor_tensor(
            out=ch23, in0=ch13, in1=xs_sl, op=ALU.add),
            reads=["ch1", "cacc"] + xs_keys, writes=["ch2", "cacc"])
        sc.op("POOL", lambda e, z_sl=z_sl, y_sl=y_sl, ch23=ch23: e.tensor_tensor(
            out=y_sl, in0=ch23, in1=z_sl, op=ALU.mult),
            reads=["ch2", "cacc"] + z_keys, writes=y_keys)

    def state_update(self, c):
        sc = self.sc
        cbuf = c % 2
        for sl in range(4):
            pA = self.pPS[:, 0:512]
            pB = self.pPS[:, 512:1024]
            esl = self.sESEL[:, 512 * sl:512 * (sl + 1)]
            ssl = self.sS[:, 512 * sl:512 * (sl + 1)]
            wsl = self.sWXT[:, 512 * sl:512 * (sl + 1)]
            sc.ops("PE", [lambda e, esl=esl, pA=pA, cbuf=cbuf: e.matmul(
                out=pA, lhsT=self.sGAM[cbuf][:], rhs=esl, start=True, stop=True)],
                reads=[f"gam:{cbuf}", "cESEL"], writes=["P0"])
            sc.ops("PE", [lambda e, wsl=wsl, pB=pB, cbuf=cbuf: e.matmul(
                out=pB, lhsT=self.sBT[cbuf][:], rhs=wsl, start=True, stop=True)],
                reads=[f"BT:{cbuf}", "wxt"], writes=["P1"])
            sc.op("DVE", lambda e, ssl=ssl, pA=pA: e.tensor_tensor(
                out=ssl, in0=pA, in1=ssl, op=ALU.mult),
                reads=["P0", "S"], writes=["S"])
            sc.op("DVE", lambda e, ssl=ssl, pB=pB: e.tensor_tensor(
                out=ssl, in0=ssl, in1=pB, op=ALU.add),
                reads=["P1", "S"], writes=["S"])

    def phase_C(self, l, seg):
        cfg, sc = self.cfg, self.sc
        T, L = cfg.TSEG, cfg.L
        t0 = seg * T
        ysl = lambda k: self.sY[:, k * T:(k + 1) * T]
        self.rms_rows(T, ysl, lambda k: [f"y:{k}"], NZC, D_INNER)
        self.row_hilo(self.sROW[0:1, 0:T], T, "rowA0")
        self.bcast_row(self.sONES2[0:1, :], ["cONES2"], T,
                       self.pPS[:, 3072 + 0:3072 + T], "P6")
        for k in range(NZC):
            sc.op("DVE", lambda e, k=k: e.scalar_tensor_tensor(
                out=self.sY[:, k * T:(k + 1) * T], in0=self.sY[:, k * T:(k + 1) * T],
                scalar=self.sGNWC[:, k:k + 1], in1=self.pPS[:, 3072 + 0:3072 + T],
                op0=ALU.mult, op1=ALU.mult),
                reads=[f"y:{k}", "P6", "gnwb"], writes=[f"y:{k}"])
        y_keys = [f"y:{k}" for k in range(NZC)]
        allp = [f"P{b}" for b in range(8)]
        for f in range(NZC):
            buf = f % 2
            sc.dma("SP", lambda e, f=f, buf=buf, l=l: e.dma_start(
                out=self.sWOB[buf][:, 0:1024], in_=self.wo_bf[l][128 * f:128 * (f + 1), :]),
                reads=[f"wo_bf:{l}"], writes=[f"wib:{buf}"])

            def mk(e, f=f, buf=buf):
                r = None
                for dk in range(NDC):
                    po = self.pPS[:, 512 * dk:512 * dk + 512]
                    r = e.matmul(out=po, lhsT=self.sWOB[buf][:, 128 * dk:128 * (dk + 1)],
                                 rhs=self.sY[:, f * T:f * T + 512],
                                 start=(f == 0), stop=(f == NZC - 1))
                return r
            sc.ops("PE", [mk], reads=[f"wib:{buf}"] + y_keys, writes=allp)
        for dk in range(NDC):
            po = self.pPS[:, 512 * dk:512 * dk + 512]
            hsl = self.h[:, dk * L + t0: dk * L + t0 + 512]
            sc.op("DVE", lambda e, po=po, hsl=hsl: e.tensor_tensor(
                out=hsl, in0=hsl, in1=po, op=ALU.add),
                reads=[f"P{dk}", f"h:{dk}:{seg}"], writes=[f"h:{dk}:{seg}"])

    def phase_x1(self):
        cfg, sc = self.cfg, self.sc
        L = cfg.L
        for j in range(L // 512):
            t0 = 512 * j
            hsl = lambda k, t0=t0: self.h[:, k * L + t0: k * L + t0 + 512]
            self.rms_rows(512, hsl, lambda k, j=j: [f"h:{k}:{j}"], NDC, D_MODEL)
            self.row_hilo(self.sROW[0:1, 0:512], 512, "rowA0")
            self.bcast_row(self.sONES2[0:1, :], ["cONES2"], 512,
                           self.pPS[:, 1024 + 0:1024 + 512], "P2")
            for k in range(NDC):
                sc.dma("SP", lambda e, k=k, t0=t0: e.dma_start(
                    out=self.sXTILE[:],
                    in_=self.x_fm[128 * k:128 * (k + 1), t0:t0 + 512]),
                    writes=["xt"])
                hap = hsl(k)
                sc.op("DVE", lambda e, k=k, hap=hap: e.scalar_tensor_tensor(
                    out=self.sEVAC[:, 0:512], in0=hap,
                    scalar=self.sFNWC[:, k:k + 1], in1=self.pPS[:, 1024 + 0:1024 + 512],
                    op0=ALU.mult, op1=ALU.mult),
                    reads=[f"h:{k}:{j}", "P2", "fnwb"], writes=["evac"])
                sc.op("DVE", lambda e, hap=hap: e.tensor_tensor(
                    out=hap, in0=self.sEVAC[:, 0:512],
                    in1=self.sXTILE[:], op=ALU.add),
                    reads=["evac", "xt"], writes=[f"h:{k}:{j}"])

    def dequant_matrix(self, wsrc, wdst, n_i, n_d, wkey):
        sc = self.sc
        nb = n_d // 32
        for i in range(n_i // 128):
            sc.dma("SP", lambda e, i=i, wsrc=wsrc: e.dma_start(
                out=self.sDQ1[:, 0:n_d], in_=wsrc[128 * i:128 * (i + 1), :]),
                writes=["dq1"])
            w3 = self.sDQ1.ap()[:, 0:n_d].rearrange("p (b k) -> p b k", k=32)
            sc.op("DVE", lambda e, w3=w3: e.reduce_max(
                out=self.sBLK[:, 0:nb].unsqueeze(2), in_=w3, axis=AXX,
                apply_absolute_value=True),
                reads=["dq1"], writes=["blk"])
            sc.op("DVE", lambda e: e.tensor_scalar(
                out=self.sBLK[:, 0:nb], in0=self.sBLK[:, 0:nb], scalar1=1.0 / 7.0,
                scalar2=1e-12, op0=ALU.mult, op1=ALU.max),
                reads=["blk"], writes=["blk"])
            sc.op("DVE", lambda e: e.reciprocal(out=self.sRS[:, 0:nb], in_=self.sBLK[:, 0:nb]),
                  reads=["blk"], writes=["rs"])
            sc.op("DVE", lambda e, w3=w3: e.tensor_tensor(
                out=w3, in0=w3, in1=bc_in(self.sRS[:, 0:nb], 32), op=ALU.mult),
                reads=["dq1", "rs"], writes=["dq1"])
            sc.op("DVE", lambda e: e.tensor_scalar(
                out=self.sDQ1[:, 0:n_d], in0=self.sDQ1[:, 0:n_d], scalar1=MAGIC,
                scalar2=MAGIC, op0=ALU.add, op1=ALU.subtract),
                reads=["dq1"], writes=["dq1"])
            sc.op("DVE", lambda e, w3=w3: e.tensor_tensor(
                out=w3, in0=w3, in1=bc_in(self.sBLK[:, 0:nb], 32), op=ALU.mult),
                reads=["dq1", "blk"], writes=["dq1"])
            sc.op("DVE", lambda e: e.tensor_copy(out=self.sFW[0][:, 0:n_d],
                                                 in_=self.sDQ1[:, 0:n_d]),
                  reads=["dq1"], writes=["fw:0"])
            sc.dma("SP", lambda e, i=i, wdst=wdst: e.dma_start(
                out=wdst[128 * i:128 * (i + 1), :], in_=self.sFW[0][:, 0:n_d]),
                reads=["fw:0"], writes=[wkey])

    def absmax_to_scalar(self, col_ap, rkey, idx):
        sc = self.sc
        sc.dma("ACT", lambda e, col_ap=col_ap: e.dma_start(out=self.red_b, in_=col_ap),
               reads=[rkey], writes=["redb"])
        sc.dma("ACT", lambda e: e.dma_start(
            out=self.sRED[:], in_=self.red_b.rearrange("p o -> (p o)").unsqueeze(0)),
            reads=["redb"], writes=["redrow"])
        sc.op("DVE", lambda e, idx=idx: e.reduce_max(
            out=self.sSCR[0:1, idx:idx + 1], in_=self.sRED[:], axis=AXX),
            reads=["redrow"], writes=["scr"])
        sc.dma("ACT", lambda e, idx=idx: e.dma_start(
            out=self.cc_in[0:1, idx:idx + 1], in_=self.sSCR[0:1, idx:idx + 1]),
            reads=["scr"], writes=["ccin"])

    def allreduce_max(self):
        sc = self.sc
        sc.cc(lambda e: e.collective_compute(
            "AllReduce", ALU.max, replica_groups=[list(range(self.cfg.n_cores))],
            ins=[self.cc_in.opt()], outs=[self.cc_out.opt()]),
            reads=["ccin"], writes=["ccout"])

    def scale_cols(self, idx, sidx, qmax):
        sc = self.sc
        sc.dma("ACT", lambda e, idx=idx, sidx=sidx: e.dma_start(
            out=self.sSC[:, sidx:sidx + 1],
            in_=self.cc_out[0:1, idx:idx + 1].to_broadcast((128, 1))),
            reads=["ccout"], writes=["ssc"])
        sc.op("DVE", lambda e, sidx=sidx, qmax=qmax: e.tensor_scalar(
            out=self.sSC[:, sidx:sidx + 1], in0=self.sSC[:, sidx:sidx + 1],
            scalar1=1.0 / qmax, scalar2=1e-12, op0=ALU.mult, op1=ALU.max),
            reads=["ssc"], writes=["ssc"])
        sc.op("DVE", lambda e, sidx=sidx: e.reciprocal(
            out=self.sSC[:, sidx + 1:sidx + 2], in_=self.sSC[:, sidx:sidx + 1]),
            reads=["ssc"], writes=["ssc"])

    def qdq_tile(self, out_ap, in_ap, sidx, rkeys, wkeys):
        sc = self.sc
        W = in_ap.shape[1]
        tmp = self.sEVAC[:, 0:W]
        sc.op("DVE", lambda e: e.tensor_scalar(
            out=tmp, in0=in_ap, scalar1=self.sSC[:, sidx + 1:sidx + 2],
            scalar2=MAGIC, op0=ALU.mult, op1=ALU.add),
            reads=list(rkeys) + ["ssc"], writes=["evac"])
        sc.op("DVE", lambda e: e.tensor_scalar(
            out=out_ap, in0=tmp, scalar1=MAGIC, scalar2=self.sSC[:, sidx:sidx + 1],
            op0=ALU.subtract, op1=ALU.mult),
            reads=["evac", "ssc"], writes=list(wkeys))

    def phase_ffn(self):
        cfg, sc = self.cfg, self.sc
        L = cfg.L
        assert L == 2048
        self.dequant_matrix(self.w_gate, self.wg_dq, FFN_I, D_MODEL, "wgdq")
        self.dequant_matrix(self.w_up, self.wu_dq, FFN_I, D_MODEL, "wudq")
        self.dequant_matrix(self.w_down, self.wd_dq, D_MODEL, FFN_I, "wddq")

        def yq_t(k):
            return (self.sY[:, (k % 4) * 2048:(k % 4 + 1) * 2048] if k < 4
                    else self.sZ[:, (k - 4) * 2048:(k - 3) * 2048])

        def yq_keys(k):
            return [f"yq:{k}", (f"y:{4 * (k % 4)}" if k < 4 else f"z:{4 * (k - 4)}")]

        for j in range(L // 512):
            t0 = 512 * j
            hsl = lambda k, t0=t0: self.h[:, k * L + t0: k * L + t0 + 512]
            self.ln_rows(512, hsl, lambda k, j=j: [f"h:{k}:{j}"])
            self.ln_bcasts(512)
            for k in range(NDC):
                ysl = yq_t(k)[:, t0:t0 + 512]
                hap = hsl(k)
                sc.op("DVE", lambda e, hap=hap: e.tensor_tensor(
                    out=self.sEVAC[:, 0:512], in0=hap, in1=self.pPS[:, 1024 + 0:1024 + 512],
                    op=ALU.mult),
                    reads=[f"h:{k}:{j}", "P2"], writes=["evac"])
                sc.op("DVE", lambda e, ysl=ysl: e.tensor_tensor(
                    out=ysl, in0=self.sEVAC[:, 0:512], in1=self.pPS[:, 1024 + 512:1024 + 1024],
                    op=ALU.subtract),
                    reads=["evac", "P3"], writes=yq_keys(k) + ["yq_all"])
                sc.op("DVE", lambda e, ysl=ysl: e.reduce_max(
                    out=self.sEVAC[:, 512:513], in_=ysl, axis=AXX,
                    apply_absolute_value=True),
                    reads=yq_keys(k), writes=["evac"])
                sc.op("DVE", lambda e: e.tensor_tensor(
                    out=self.sMXC[:, 0:1], in0=self.sMXC[:, 0:1],
                    in1=self.sEVAC[:, 512:513], op=ALU.max),
                    reads=["evac", "mxc"], writes=["mxc"])
        self.absmax_to_scalar(self.sMXC[:, 0:1], "mxc", 0)
        self.allreduce_max()
        self.scale_cols(0, 0, QMAX)
        for k in range(NDC):
            for j in range(L // 512):
                sl = yq_t(k)[:, 512 * j:512 * (j + 1)]
                self.qdq_tile(sl, sl, 0, yq_keys(k), yq_keys(k) + ["yq_all"])
        sc.op("DVE", lambda e: e.memset(self.sMXC[:, 1:3], 0.0),
              reads=["mxc"], writes=["mxc"])
        all_yq = ["yq_all"]
        for which, wdq, dst, mcol, wdqk, dkey in (
                (0, self.wu_dq, self.up_d, 1, "wudq", "uprawd"),
                (1, self.wg_dq, self.gate_d, 2, "wgdq", "gaterawd")):
            for ig in range(FFN_I // 512):
                buf = ig % 2
                for k in range(NDC):
                    sc.dma("SP", lambda e, ig=ig, buf=buf, k=k, wdq=wdq: e.dma_start_transpose(
                        out=self.sFW[buf][:, 512 * k:512 * (k + 1)],
                        in_=wdq[512 * ig:512 * (ig + 1), 128 * k:128 * (k + 1)]),
                        reads=[wdqk], writes=[f"fw:{buf}"])
                for isub in range(4):
                    iof = 512 * ig + 128 * isub
                    for j in range(L // 512):
                        pb = (4 * ig + isub + j) % 2
                        psl = self.pPS[:, 512 * pb:512 * (pb + 1)]
                        t0 = 512 * j

                        def mk(e, buf=buf, isub=isub, t0=t0, psl=psl):
                            r = None
                            for k in range(NDC):
                                r = e.matmul(out=psl,
                                             lhsT=self.sFW[buf][:, 512 * k + 128 * isub:
                                                                512 * k + 128 * isub + 128],
                                             rhs=yq_t(k)[:, t0:t0 + 512],
                                             start=(k == 0), stop=(k == NDC - 1))
                            return r
                        sc.ops("PE", [mk], reads=[f"fw:{buf}"] + all_yq,
                               writes=[f"P{pb}"])
                        sc.op("DVE", lambda e, psl=psl: e.tensor_copy(
                            out=self.sT1[:], in_=psl),
                            reads=[f"P{pb}"], writes=["t1"])
                        sc.op("DVE", lambda e: e.reduce_max(
                            out=self.sEVAC[:, 512:513], in_=self.sT1[:], axis=AXX,
                            apply_absolute_value=True),
                            reads=["t1"], writes=["evac"])
                        sc.op("DVE", lambda e, mcol=mcol: e.tensor_tensor(
                            out=self.sMXC[:, mcol:mcol + 1], in0=self.sMXC[:, mcol:mcol + 1],
                            in1=self.sEVAC[:, 512:513], op=ALU.max),
                            reads=["evac", "mxc"], writes=["mxc"])
                        sc.dma("SP", lambda e, iof=iof, t0=t0, dst=dst: e.dma_start(
                            out=dst[iof:iof + 128, t0:t0 + 512], in_=self.sT1[:]),
                            reads=["t1"], writes=[dkey])
        self.absmax_to_scalar(self.sMXC[:, 1:2], "mxc", 1)
        self.absmax_to_scalar(self.sMXC[:, 2:3], "mxc", 2)
        self.allreduce_max()
        self.scale_cols(1, 2, QMAX)
        self.scale_cols(2, 4, QMAX)
        sc.op("DVE", lambda e: e.memset(self.sMXC[:, 3:4], 0.0),
              reads=["mxc"], writes=["mxc"])
        for ic in range(FFN_I // 128):
            for j in range(L // 512):
                t0 = 512 * j
                sc.dma("SP", lambda e, ic=ic, t0=t0: e.dma_start(
                    out=self.sT1[:], in_=self.gate_d[128 * ic:128 * (ic + 1), t0:t0 + 512]),
                    reads=["gaterawd"], writes=["t1"])
                self.qdq_tile(self.sT1[:], self.sT1[:], 4, ["t1"], ["t1"])
                sc.op("ACT", lambda e: e.activation(out=self.sT2[:], in_=self.sT1[:],
                                                    func=AF.Sigmoid),
                      reads=["t1"], writes=["t2"])
                sc.op("DVE", lambda e: e.tensor_scalar(
                    out=self.sEVAC[:, 0:512], in0=self.sT2[:], scalar1=U16S, scalar2=MAGIC,
                    op0=ALU.mult, op1=ALU.add),
                    reads=["t2"], writes=["evac"])
                sc.op("DVE", lambda e: e.tensor_scalar(
                    out=self.sT2[:], in0=self.sEVAC[:, 0:512], scalar1=MAGIC,
                    scalar2=1.0 / U16S, op0=ALU.subtract, op1=ALU.mult),
                    reads=["evac"], writes=["t2"])
                sc.op("DVE", lambda e: e.tensor_tensor(out=self.sT3[:], in0=self.sT1[:],
                                                       in1=self.sT2[:], op=ALU.mult),
                      reads=["t1", "t2"], writes=["t3"])
                sc.op("DVE", lambda e: e.reduce_max(
                    out=self.sEVAC[:, 512:513], in_=self.sT3[:], axis=AXX,
                    apply_absolute_value=True),
                    reads=["t3"], writes=["evac"])
                sc.op("DVE", lambda e: e.tensor_tensor(
                    out=self.sMXC[:, 3:4], in0=self.sMXC[:, 3:4],
                    in1=self.sEVAC[:, 512:513], op=ALU.max),
                    reads=["evac", "mxc"], writes=["mxc"])
                sc.dma("SP", lambda e, ic=ic, t0=t0: e.dma_start(
                    out=self.gate_d[128 * ic:128 * (ic + 1), t0:t0 + 512], in_=self.sT3[:]),
                    reads=["t3"], writes=["t4d"])
                sc.dma("SP", lambda e, ic=ic, t0=t0: e.dma_start(
                    out=self.sT1[:], in_=self.up_d[128 * ic:128 * (ic + 1), t0:t0 + 512]),
                    reads=["uprawd"], writes=["t1"])
                self.qdq_tile(self.sT1[:], self.sT1[:], 2, ["t1"], ["t1"])
                sc.dma("SP", lambda e, ic=ic, t0=t0: e.dma_start(
                    out=self.up_d[128 * ic:128 * (ic + 1), t0:t0 + 512], in_=self.sT1[:]),
                    reads=["t1"], writes=["upqd"])
        self.absmax_to_scalar(self.sMXC[:, 3:4], "mxc", 3)
        self.allreduce_max()
        self.scale_cols(3, 6, QMAX)
        sc.op("DVE", lambda e: e.memset(self.sMXC[:, 4:5], 0.0),
              reads=["mxc"], writes=["mxc"])
        for ic in range(FFN_I // 128):
            for j in range(L // 512):
                t0 = 512 * j
                sc.dma("SP", lambda e, ic=ic, t0=t0: e.dma_start(
                    out=self.sT1[:], in_=self.gate_d[128 * ic:128 * (ic + 1), t0:t0 + 512]),
                    reads=["t4d"], writes=["t1"])
                self.qdq_tile(self.sT1[:], self.sT1[:], 6, ["t1"], ["t1"])
                sc.dma("SP", lambda e, ic=ic, t0=t0: e.dma_start(
                    out=self.sT2[:], in_=self.up_d[128 * ic:128 * (ic + 1), t0:t0 + 512]),
                    reads=["upqd"], writes=["t2"])
                sc.op("DVE", lambda e: e.tensor_tensor(out=self.sT3[:], in0=self.sT1[:],
                                                       in1=self.sT2[:], op=ALU.mult),
                      reads=["t1", "t2"], writes=["t3"])
                sc.op("DVE", lambda e: e.reduce_max(
                    out=self.sEVAC[:, 512:513], in_=self.sT3[:], axis=AXX,
                    apply_absolute_value=True),
                    reads=["t3"], writes=["evac"])
                sc.op("DVE", lambda e: e.tensor_tensor(
                    out=self.sMXC[:, 4:5], in0=self.sMXC[:, 4:5],
                    in1=self.sEVAC[:, 512:513], op=ALU.max),
                    reads=["evac", "mxc"], writes=["mxc"])
                sc.dma("SP", lambda e, ic=ic, t0=t0: e.dma_start(
                    out=self.gate_d[128 * ic:128 * (ic + 1), t0:t0 + 512], in_=self.sT3[:]),
                    reads=["t3"], writes=["t5d"])
        self.absmax_to_scalar(self.sMXC[:, 4:5], "mxc", 0)
        self.allreduce_max()
        self.scale_cols(0, 0, QMAX)
        allp = [f"P{b}" for b in range(8)]
        for j in range(L // 512):
            t0 = 512 * j
            for ic in range(FFN_I // 128):
                sc.dma("SP", lambda e, ic=ic, t0=t0: e.dma_start(
                    out=self.sT1[:],
                    in_=self.gate_d[128 * ic:128 * (ic + 1), t0:t0 + 512]),
                    reads=["t5d"], writes=["t1"])
                self.qdq_tile(self.sO[:, 512 * ic:512 * (ic + 1)], self.sT1[:],
                              0, ["t1"], [f"o:{ic}"])
            for ic in range(FFN_I // 128):
                buf = ic % 2
                for dh in range(2):
                    sc.dma("SP", lambda e, ic=ic, buf=buf, dh=dh: e.dma_start_transpose(
                        out=self.sFW[buf][:, 512 * dh:512 * (dh + 1)],
                        in_=self.wd_dq[512 * dh:512 * (dh + 1), 128 * ic:128 * (ic + 1)]),
                        reads=["wddq"], writes=[f"fw:{buf}"])

                def mk(e, ic=ic, buf=buf):
                    r = None
                    for dk in range(NDC):
                        po = self.pPS[:, 512 * dk:512 * dk + 512]
                        r = e.matmul(out=po, lhsT=self.sFW[buf][:, 128 * dk:128 * (dk + 1)],
                                     rhs=self.sO[:, 512 * ic:512 * (ic + 1)],
                                     start=(ic == 0), stop=(ic == FFN_I // 128 - 1))
                    return r
                sc.ops("PE", [mk], reads=[f"fw:{buf}", f"o:{ic}"], writes=allp)
            for dk in range(NDC):
                po = self.pPS[:, 512 * dk:512 * dk + 512]
                hsl = self.h[:, dk * L + t0: dk * L + t0 + 512]
                sc.op("DVE", lambda e, po=po, hsl=hsl: e.tensor_tensor(
                    out=self.sEVAC[:, 0:512], in0=hsl, in1=po, op=ALU.add),
                    reads=[f"P{dk}", f"h:{dk}:{j}"], writes=["evac"])
                sc.dma("SP", lambda e, dk=dk, t0=t0: e.dma_start(
                    out=self.out_fm[128 * dk:128 * (dk + 1), t0:t0 + 512],
                    in_=self.sEVAC[:, 0:512]),
                    reads=["evac"], writes=["OUT"])

    def output_no_ffn(self):
        sc, L = self.sc, self.cfg.L
        for k in range(NDC):
            for j in range(L // 512):
                sc.dma("SP", lambda e, k=k, j=j: e.dma_start(
                    out=self.out_fm[128 * k:128 * (k + 1), 512 * j:512 * (j + 1)],
                    in_=self.h[:, k * L + 512 * j: k * L + 512 * (j + 1)]),
                    reads=[f"h:{k}:{j}"], writes=["OUT"])

    def absmax_to_out(self, col_ap, rkey, idx):
        sc = self.sc
        sc.dma("ACT", lambda e, col_ap=col_ap: e.dma_start(out=self.red_b, in_=col_ap),
               reads=[rkey], writes=["redb"])
        sc.dma("ACT", lambda e: e.dma_start(
            out=self.sRED[:], in_=self.red_b.rearrange("p o -> (p o)").unsqueeze(0)),
            reads=["redb"], writes=["redrow"])
        sc.op("DVE", lambda e, idx=idx: e.reduce_max(
            out=self.sSCR[0:1, idx:idx + 1], in_=self.sRED[:], axis=AXX),
            reads=["redrow"], writes=["scr"])
        sc.dma("ACT", lambda e, idx=idx: e.dma_start(
            out=self.mx_out[0:1, idx:idx + 1], in_=self.sSCR[0:1, idx:idx + 1]),
            reads=["scr"], writes=["OUTMX"])

    def reduce_mxcat(self):
        sc, NCC = self.sc, self.cfg.n_cores
        sc.dma("ACT", lambda e: e.dma_start(out=self.mxr_d, in_=self.mxcat),
               writes=["mxrd"])
        sc.dma("ACT", lambda e: e.dma_start(
            out=self.sRED[0:1, 0:6 * NCC],
            in_=self.mxr_d.rearrange("a b -> (a b)").unsqueeze(0)),
            reads=["mxrd"], writes=["redrow"])
        sc.op("DVE", lambda e, NCC=NCC: e.reduce_max(
            out=self.sSCR[0:1, 8:14].unsqueeze(2),
            in_=self.sRED.ap()[0:1, 0:6 * NCC].rearrange("o (a b) -> o a b", a=6),
            axis=AXX),
            reads=["redrow"], writes=["scr"])
        sc.dma("ACT", lambda e: e.dma_start(out=self.gmax_d[0:1, 0:6],
                                            in_=self.sSCR[0:1, 8:14]),
               reads=["scr"], writes=["gmaxd"])

    def scale_cols_g(self, idx, sidx, qmax):
        sc = self.sc
        sc.dma("ACT", lambda e, idx=idx, sidx=sidx: e.dma_start(
            out=self.sSC[:, sidx:sidx + 1],
            in_=self.gmax_d[0:1, idx:idx + 1].to_broadcast((128, 1))),
            reads=["gmaxd"], writes=["ssc"])
        sc.op("DVE", lambda e, sidx=sidx, qmax=qmax: e.tensor_scalar(
            out=self.sSC[:, sidx:sidx + 1], in0=self.sSC[:, sidx:sidx + 1],
            scalar1=1.0 / qmax, scalar2=1e-12, op0=ALU.mult, op1=ALU.max),
            reads=["ssc"], writes=["ssc"])
        sc.op("DVE", lambda e, sidx=sidx: e.reciprocal(
            out=self.sSC[:, sidx + 1:sidx + 2], in_=self.sSC[:, sidx:sidx + 1]),
            reads=["ssc"], writes=["ssc"])

    def stage1_tail(self):
        cfg, sc = self.cfg, self.sc
        L = cfg.L
        for j in range(L // 512):
            t0 = 512 * j
            hsl = lambda k, t0=t0: self.h[:, k * L + t0: k * L + t0 + 512]
            self.ln_rows(512, hsl, lambda k, j=j: [f"h:{k}:{j}"])
            self.ln_bcasts(512)
            for k in range(NDC):
                hap = hsl(k)
                sc.op("DVE", lambda e, hap=hap: e.tensor_tensor(
                    out=self.sEVAC[:, 0:512], in0=hap, in1=self.pPS[:, 1024 + 0:1024 + 512],
                    op=ALU.mult),
                    reads=[f"h:{k}:{j}", "P2"], writes=["evac"])
                sc.op("DVE", lambda e: e.tensor_tensor(
                    out=self.sXB[:, 0:512], in0=self.sEVAC[:, 0:512],
                    in1=self.pPS[:, 1024 + 512:1024 + 1024], op=ALU.subtract),
                    reads=["evac", "P3"], writes=["sXB"])
                sc.op("DVE", lambda e: e.reduce_max(
                    out=self.sEVAC[:, 512:513], in_=self.sXB[:, 0:512], axis=AXX,
                    apply_absolute_value=True),
                    reads=["sXB"], writes=["evac"])
                sc.op("DVE", lambda e: e.tensor_tensor(
                    out=self.sMXC[:, 0:1], in0=self.sMXC[:, 0:1],
                    in1=self.sEVAC[:, 512:513], op=ALU.max),
                    reads=["evac", "mxc"], writes=["mxc"])
                sc.dma("SP", lambda e, k=k, t0=t0: e.dma_start(
                    out=self.y_out[128 * k:128 * (k + 1), t0:t0 + 512],
                    in_=self.sXB[:, 0:512]),
                    reads=["sXB"], writes=["OUT"])
        self.absmax_to_out(self.sMXC[:, 0:1], "mxc", 0)
        self.output_no_ffn()

    def stage2(self):
        cfg, sc = self.cfg, self.sc
        L = cfg.L
        self.dequant_matrix(self.w_gate, self.wg_dq, FFN_I, D_MODEL, "wgdq")
        self.dequant_matrix(self.w_up, self.wu_dq, FFN_I, D_MODEL, "wudq")
        self.reduce_mxcat()
        self.scale_cols_g(0, 0, QMAX)

        def yq_t(k):
            return (self.sY[:, (k % 4) * 2048:(k % 4 + 1) * 2048] if k < 4
                    else self.sZ[:, (k - 4) * 2048:(k - 3) * 2048])
        for k in range(NDC):
            for j in range(L // 512):
                t0 = 512 * j
                sl = yq_t(k)[:, t0:t0 + 512]
                sc.dma("SP", lambda e, k=k, t0=t0, sl=sl: e.dma_start(
                    out=sl, in_=self.y_in[128 * k:128 * (k + 1), t0:t0 + 512]),
                    writes=["yq_all"])
                self.qdq_tile(sl, sl, 0, ["yq_all"], ["yq_all"])
        for which, wdq, dst, mcol, wdqk, dkey in (
                (0, self.wu_dq, self.upraw_out, 1, "wudq", "uprawd"),
                (1, self.wg_dq, self.gateraw_out, 2, "wgdq", "gaterawd")):
            for ig in range(FFN_I // 512):
                buf = ig % 2
                for k in range(NDC):
                    sc.dma("SP", lambda e, ig=ig, buf=buf, k=k, wdq=wdq: e.dma_start_transpose(
                        out=self.sFW[buf][:, 512 * k:512 * (k + 1)],
                        in_=wdq[512 * ig:512 * (ig + 1), 128 * k:128 * (k + 1)]),
                        reads=[wdqk], writes=[f"fw:{buf}"])
                for isub in range(4):
                    iof = 512 * ig + 128 * isub
                    for j in range(L // 512):
                        pb = (4 * ig + isub + j) % 2
                        psl = self.pPS[:, 512 * pb:512 * (pb + 1)]
                        t0 = 512 * j

                        def mk(e, buf=buf, isub=isub, t0=t0, psl=psl):
                            r = None
                            for k in range(NDC):
                                r = e.matmul(out=psl,
                                             lhsT=self.sFW[buf][:, 512 * k + 128 * isub:
                                                                512 * k + 128 * isub + 128],
                                             rhs=yq_t(k)[:, t0:t0 + 512],
                                             start=(k == 0), stop=(k == NDC - 1))
                            return r
                        sc.ops("PE", [mk], reads=[f"fw:{buf}", "yq_all"],
                               writes=[f"P{pb}"])
                        sc.op("DVE", lambda e, psl=psl: e.tensor_copy(
                            out=self.sT1[:], in_=psl),
                            reads=[f"P{pb}"], writes=["t1"])
                        sc.op("DVE", lambda e: e.reduce_max(
                            out=self.sEVAC[:, 512:513], in_=self.sT1[:], axis=AXX,
                            apply_absolute_value=True),
                            reads=["t1"], writes=["evac"])
                        sc.op("DVE", lambda e, mcol=mcol: e.tensor_tensor(
                            out=self.sMXC[:, mcol:mcol + 1], in0=self.sMXC[:, mcol:mcol + 1],
                            in1=self.sEVAC[:, 512:513], op=ALU.max),
                            reads=["evac", "mxc"], writes=["mxc"])
                        sc.dma("SP", lambda e, iof=iof, t0=t0, dst=dst: e.dma_start(
                            out=dst[iof:iof + 128, t0:t0 + 512], in_=self.sT1[:]),
                            reads=["t1"], writes=["OUT"])
        self.absmax_to_out(self.sMXC[:, 1:2], "mxc", 1)
        self.absmax_to_out(self.sMXC[:, 2:3], "mxc", 2)

    def stage3(self):
        cfg, sc = self.cfg, self.sc
        L = cfg.L
        self.reduce_mxcat()
        self.scale_cols_g(2, 4, QMAX)
        for ic in range(FFN_I // 128):
            for j in range(L // 512):
                t0 = 512 * j
                sc.dma("SP", lambda e, ic=ic, t0=t0: e.dma_start(
                    out=self.sT1[:], in_=self.gateraw_in[128 * ic:128 * (ic + 1), t0:t0 + 512]),
                    writes=["t1"])
                self.qdq_tile(self.sT1[:], self.sT1[:], 4, ["t1"], ["t1"])
                sc.op("ACT", lambda e: e.activation(out=self.sT2[:], in_=self.sT1[:],
                                                    func=AF.Sigmoid),
                      reads=["t1"], writes=["t2"])
                sc.op("DVE", lambda e: e.tensor_scalar(
                    out=self.sEVAC[:, 0:512], in0=self.sT2[:], scalar1=U16S, scalar2=MAGIC,
                    op0=ALU.mult, op1=ALU.add),
                    reads=["t2"], writes=["evac"])
                sc.op("DVE", lambda e: e.tensor_scalar(
                    out=self.sT2[:], in0=self.sEVAC[:, 0:512], scalar1=MAGIC,
                    scalar2=1.0 / U16S, op0=ALU.subtract, op1=ALU.mult),
                    reads=["evac"], writes=["t2"])
                sc.op("DVE", lambda e: e.tensor_tensor(out=self.sT3[:], in0=self.sT1[:],
                                                       in1=self.sT2[:], op=ALU.mult),
                      reads=["t1", "t2"], writes=["t3"])
                sc.op("DVE", lambda e: e.reduce_max(
                    out=self.sEVAC[:, 512:513], in_=self.sT3[:], axis=AXX,
                    apply_absolute_value=True),
                    reads=["t3"], writes=["evac"])
                sc.op("DVE", lambda e: e.tensor_tensor(
                    out=self.sMXC[:, 3:4], in0=self.sMXC[:, 3:4],
                    in1=self.sEVAC[:, 512:513], op=ALU.max),
                    reads=["evac", "mxc"], writes=["mxc"])
                sc.dma("SP", lambda e, ic=ic, t0=t0: e.dma_start(
                    out=self.t4_out[128 * ic:128 * (ic + 1), t0:t0 + 512], in_=self.sT3[:]),
                    reads=["t3"], writes=["OUT"])
        self.absmax_to_out(self.sMXC[:, 3:4], "mxc", 3)

    def stage4(self):
        cfg, sc = self.cfg, self.sc
        L = cfg.L
        self.reduce_mxcat()
        self.scale_cols_g(3, 6, QMAX)
        self.scale_cols_g(1, 2, QMAX)
        for ic in range(FFN_I // 128):
            for j in range(L // 512):
                t0 = 512 * j
                sc.dma("SP", lambda e, ic=ic, t0=t0: e.dma_start(
                    out=self.sT1[:], in_=self.t4_in[128 * ic:128 * (ic + 1), t0:t0 + 512]),
                    writes=["t1"])
                self.qdq_tile(self.sT1[:], self.sT1[:], 6, ["t1"], ["t1"])
                sc.dma("SP", lambda e, ic=ic, t0=t0: e.dma_start(
                    out=self.sT2[:], in_=self.upraw_in[128 * ic:128 * (ic + 1), t0:t0 + 512]),
                    writes=["t2"])
                self.qdq_tile(self.sT2[:], self.sT2[:], 2, ["t2"], ["t2"])
                sc.op("DVE", lambda e: e.tensor_tensor(out=self.sT3[:], in0=self.sT1[:],
                                                       in1=self.sT2[:], op=ALU.mult),
                      reads=["t1", "t2"], writes=["t3"])
                sc.op("DVE", lambda e: e.reduce_max(
                    out=self.sEVAC[:, 512:513], in_=self.sT3[:], axis=AXX,
                    apply_absolute_value=True),
                    reads=["t3"], writes=["evac"])
                sc.op("DVE", lambda e: e.tensor_tensor(
                    out=self.sMXC[:, 4:5], in0=self.sMXC[:, 4:5],
                    in1=self.sEVAC[:, 512:513], op=ALU.max),
                    reads=["evac", "mxc"], writes=["mxc"])
                sc.dma("SP", lambda e, ic=ic, t0=t0: e.dma_start(
                    out=self.t5_out[128 * ic:128 * (ic + 1), t0:t0 + 512], in_=self.sT3[:]),
                    reads=["t3"], writes=["OUT"])
        self.absmax_to_out(self.sMXC[:, 4:5], "mxc", 4)

    def stage5(self):
        cfg, sc = self.cfg, self.sc
        L = cfg.L
        self.dequant_matrix(self.w_down, self.wd_dq, D_MODEL, FFN_I, "wddq")
        self.reduce_mxcat()
        self.scale_cols_g(4, 0, QMAX)
        allp = [f"P{b}" for b in range(8)]
        for j in range(L // 512):
            t0 = 512 * j
            for ic in range(FFN_I // 128):
                sc.dma("SP", lambda e, ic=ic, t0=t0: e.dma_start(
                    out=self.sT1[:], in_=self.t5_in[128 * ic:128 * (ic + 1), t0:t0 + 512]),
                    writes=["t1"])
                self.qdq_tile(self.sO[:, 512 * ic:512 * (ic + 1)], self.sT1[:],
                              0, ["t1"], [f"o:{ic}"])
            for ic in range(FFN_I // 128):
                buf = ic % 2
                for dh in range(2):
                    sc.dma("SP", lambda e, ic=ic, buf=buf, dh=dh: e.dma_start_transpose(
                        out=self.sFW[buf][:, 512 * dh:512 * (dh + 1)],
                        in_=self.wd_dq[512 * dh:512 * (dh + 1), 128 * ic:128 * (ic + 1)]),
                        reads=["wddq"], writes=[f"fw:{buf}"])

                def mk(e, ic=ic, buf=buf):
                    r = None
                    for dk in range(NDC):
                        po = self.pPS[:, 512 * dk:512 * dk + 512]
                        r = e.matmul(out=po, lhsT=self.sFW[buf][:, 128 * dk:128 * (dk + 1)],
                                     rhs=self.sO[:, 512 * ic:512 * (ic + 1)],
                                     start=(ic == 0), stop=(ic == FFN_I // 128 - 1))
                    return r
                sc.ops("PE", [mk], reads=[f"fw:{buf}", f"o:{ic}"], writes=allp)
            for dk in range(NDC):
                po = self.pPS[:, 512 * dk:512 * dk + 512]
                sc.dma("SP", lambda e, dk=dk, t0=t0: e.dma_start(
                    out=self.sXTILE[:], in_=self.x1_in[128 * dk:128 * (dk + 1), t0:t0 + 512]),
                    writes=["xt"])
                sc.op("DVE", lambda e, po=po: e.tensor_tensor(
                    out=self.sEVAC[:, 0:512], in0=self.sXTILE[:], in1=po, op=ALU.add),
                    reads=[f"P{dk}", "xt"], writes=["evac"])
                sc.dma("SP", lambda e, dk=dk, t0=t0: e.dma_start(
                    out=self.out_fm[128 * dk:128 * (dk + 1), t0:t0 + 512],
                    in_=self.sEVAC[:, 0:512]),
                    reads=["evac"], writes=["OUT"])

    def build(self):
        cfg, sc = self.cfg, self.sc
        st = cfg.stage
        self.load_consts()
        if st in ("all", "s1"):
            self.preconvert_weights()
            self.phase_init_ln()
            for l in range(cfg.nlayer):
                self.load_layer_params(l)
                for seg in range(cfg.NSEG):
                    self.phase_A(l, seg)
                    self.phase_B_prep_front(l, seg, 0)
                    self.phase_B_prep_back(l, seg, 0)
                    for c in range(cfg.NCH):
                        self.phase_B_compute(l, seg, c)
                    self.phase_C(l, seg)
            self.phase_x1()
        if st == "all":
            if cfg.do_ffn:
                sc.barrier()
                self.phase_ffn()
            else:
                self.output_no_ffn()
        elif st == "s1":
            sc.barrier()
            if cfg.do_ffn:
                self.stage1_tail()
            else:
                self.output_no_ffn()
        elif st == "s2":
            self.stage2()
        elif st == "s3":
            self.stage3()
        elif st == "s4":
            self.stage4()
        elif st == "s5":
            self.stage5()
        sc.finish(["OUT", "OUTMX"] if (cfg.do_ffn and st != "s5" and st != "all")
                  else ["OUT"])
        with self.nc.Block() as block:
            sc.emit(block)
        return self.nc


def host_constants():
    e64s = np.zeros((NH, 512), np.float32)
    e64sb = np.zeros((NH, 512), np.float32)
    for r in range(NH):
        hr = r % 32
        for t in range(32):
            if hr < 16:
                e64s[r, t * 16 + hr] = -1.0
            else:
                e64sb[r, t * 16 + (hr - 16)] = -1.0
    e64x = np.tile(e64s, (1, 4))
    e64xb = np.tile(e64sb, (1, 4))
    trid = np.zeros((32, 512), np.float32)
    for r in range(32):
        trid[r, r * 16:(r + 1) * 16] = -BIG
    trid = np.tile(trid, (1, 4))
    tri16 = np.zeros((32, 16 * 128), np.float32)
    p = np.arange(128)
    for i in range(16):
        sl = i % 4
        for r in range(32):
            tri16[r, 128 * i:128 * (i + 1)] = (p >= 32 * sl + r + 1)
    esel = np.zeros((NH, D_INNER), np.float32)
    for k in range(NZC):
        for q in range(128):
            esel[4 * k + q // 32, 128 * k + q] = 1.0
    idb = np.eye(128, dtype=np.float32)
    bf = lambda a: a.astype(ml_dtypes.bfloat16)
    return dict(c_e64x=bf(e64x), c_e64xb=bf(e64xb), c_trid=bf(trid), c_tri16=bf(tri16),
                c_esel=bf(esel), c_idb=bf(idb),
                c_ones2=bf(np.ones((2, 128), np.float32)),
                c_ones64=bf(np.ones((NH, 128), np.float32)),
                c_onescol=bf(np.ones((128, 1), np.float32)))


def host_layout(inputs, cfg: Cfg):
    NL = cfg.nlayer
    cw = np.asarray(inputs["L0_attn_conv_w"], np.float32)[:NL]
    cb = np.asarray(inputs["L0_attn_conv_b"], np.float32)[:NL]
    dcwL = np.zeros((NL, NXC * DCONV, 128, 128), np.float32)
    idx = np.arange(128)
    for l in range(NL):
        for j in range(NXC):
            for k in range(DCONV):
                dcwL[l, 4 * j + k, idx, idx] = cw[l, 128 * j + idx, k]
    cbL = np.ascontiguousarray(cb.reshape(NL, NXC, 128).transpose(0, 2, 1))
    colmaj = lambda v, n: np.ascontiguousarray(v.reshape(v.shape[:-1] + (n, 128)).swapaxes(-1, -2))
    gnwR = colmaj(np.asarray(inputs["L0_attn_gate_norm_w"], np.float32)[:NL], NZC)
    bnwR = colmaj(np.asarray(inputs["L0_attn_block_norm_w"], np.float32)[:NL], NDC)
    fnwR = colmaj(np.asarray(inputs["L0_attn_final_norm_w"], np.float32), NDC)
    col = lambda v: np.ascontiguousarray(np.asarray(v, np.float32)[:NL][:, :, None])
    m = dict(
        wi=np.ascontiguousarray(np.asarray(inputs["L0_attn_W_in"], np.float32)[:NL]),
        wo=np.ascontiguousarray(np.asarray(inputs["L0_attn_W_out"], np.float32)[:NL]),
        dcwL=dcwL, cbL=cbL,
        dtbL=col(inputs["L0_attn_dt_bias"]),
        alogL=col(inputs["L0_attn_A_log"]),
        dL=col(inputs["L0_attn_D"]),
        gnwR=gnwR, bnwR=bnwR, fnwR=fnwR,
        w_gate=np.ascontiguousarray(np.asarray(inputs["L0_ffn_w_gate"], np.float32)),
        w_up=np.ascontiguousarray(np.asarray(inputs["L0_ffn_w_up"], np.float32)),
        w_down=np.ascontiguousarray(np.asarray(inputs["L0_ffn_w_down"], np.float32)),
    )
    m.update(host_constants())
    return m


def kernel(**inputs):
    n_cores = 8
    x = np.asarray(inputs["L0_attn_x"], np.float32)
    L = x.shape[1]
    shared = host_layout(inputs, Cfg(L=L, n_cores=n_cores))
    x_fms = [np.ascontiguousarray(x[i].T) for i in range(n_cores)]
    mx = np.zeros((6, n_cores), np.float32)

    def run(stage, extra_names, per_core_extra):
        cfg = Cfg(L=L, nlayer=4, do_ffn=True, n_cores=n_cores, stage=stage)
        b = Builder(cfg)
        nc = b.build()
        in_names = {t.name for t in
                    (a for a in nc.m.functions[0].allocations
                     if getattr(a, "kind", None) == "ExternalInput")}
        maps = []
        for i in range(n_cores):
            m = {k: v for k, v in shared.items()}
            m["x_fm"] = x_fms[i]
            for name, arrs in per_core_extra.items():
                m[name] = arrs[i]
            for name in extra_names:
                m[name] = extra_names[name]
            maps.append(m)
        res = run_bass_kernel_spmd(nc, maps, core_ids=list(range(n_cores)))
        return res

    r1 = run("s1", {}, {})
    x1s = [np.asarray(r1.results[i]["out_fm"]) for i in range(n_cores)]
    ys = [np.asarray(r1.results[i]["y_out"]) for i in range(n_cores)]
    for i in range(n_cores):
        mx[0, i] = np.asarray(r1.results[i]["mx_out"]).ravel()[0]

    r2 = run("s2", {"mxcat": mx}, {"y_in": ys})
    ups = [np.asarray(r2.results[i]["upraw_out"]) for i in range(n_cores)]
    gates = [np.asarray(r2.results[i]["gateraw_out"]) for i in range(n_cores)]
    for i in range(n_cores):
        v = np.asarray(r2.results[i]["mx_out"]).ravel()
        mx[1, i] = v[1]
        mx[2, i] = v[2]

    r3 = run("s3", {"mxcat": mx}, {"gateraw_in": gates})
    t4s = [np.asarray(r3.results[i]["t4_out"]) for i in range(n_cores)]
    for i in range(n_cores):
        mx[3, i] = np.asarray(r3.results[i]["mx_out"]).ravel()[3]

    r4 = run("s4", {"mxcat": mx}, {"t4_in": t4s, "upraw_in": ups})
    t5s = [np.asarray(r4.results[i]["t5_out"]) for i in range(n_cores)]
    for i in range(n_cores):
        mx[4, i] = np.asarray(r4.results[i]["mx_out"]).ravel()[4]

    r5 = run("s5", {"mxcat": mx}, {"t5_in": t5s, "x1_in": x1s})
    out = np.stack([np.asarray(r5.results[i]["out_fm"]).T for i in range(n_cores)],
                   axis=0)
    return out.astype(np.float32)


# version 14
# speedup vs baseline: 1.2251x; 1.0939x over previous
import numpy as np
import ml_dtypes
import concourse.bass as bass
import concourse.mybir as mybir
from concourse.bass_utils import run_bass_kernel_spmd

F32 = mybir.dt.float32
BF16 = mybir.dt.bfloat16
AF = mybir.ActivationFunctionType
ALU = mybir.AluOpType
AXX = mybir.AxisListType.X

MAGIC = float(1.5 * 2 ** 23)
BIG = 57344.0
LN_EPS = 1e-5

D_MODEL = 1024
D_INNER = 2048
NH = 64
HD = 32
DS = 128
DCONV = 4
CONV_DIM = D_INNER + 2 * DS
D_IN = 2 * D_INNER + 2 * DS + NH
FFN_I = 3072
NZC = D_INNER // 128
NXC = CONV_DIM // 128
NDC = D_MODEL // 128
QMAX = 32767.0
U16S = 65536.0

ENGS = ("PE", "DVE", "ACT", "POOL", "SP")


class Cfg:
    def __init__(self, L=2048, nlayer=4, do_ffn=True, n_cores=8, stage="all"):
        self.stage = stage
        self.L = L
        self.C = 128
        self.TSEG = min(512, L)
        assert L % self.TSEG == 0 and self.TSEG % self.C == 0
        self.NSEG = L // self.TSEG
        self.NCH = self.TSEG // self.C
        self.nlayer = nlayer
        self.do_ffn = do_ffn
        self.n_cores = n_cores


class Sched:

    NLANES = 6

    def __init__(self, nc):
        self.nc = nc
        self.sem = {e: nc.alloc_semaphore(f"sem_{e}") for e in ENGS}
        self.dsem = {}
        self.cnt = {k: 0 for k in ENGS}
        for qn in ("SP", "ACT", "POOL"):
            for ln in range(self.NLANES):
                key = f"d{qn}:{ln}"
                self.dsem[key] = nc.alloc_semaphore(f"dma_{qn}_{ln}")
                self.cnt[key] = 0
        self.ccsem = nc.alloc_semaphore("sem_cc")
        self.cnt["CC"] = 0
        self.dma_issue = {qn: 0 for qn in ("SP", "ACT", "POOL")}
        self.q = {e: [] for e in ENGS}
        self.waited = {e: {} for e in ENGS}
        self.last_w = {}
        self.readers = {}
        self.nops = 0

    def _sobj(self, semkey):
        if semkey in self.sem:
            return self.sem[semkey]
        if semkey == "CC":
            return self.ccsem
        return self.dsem[semkey]

    def _wait(self, eng, semkey, val):
        if val <= self.waited[eng].get(semkey, 0):
            return
        assert val <= self.cnt[semkey], f"wait on future {semkey}:{val}"
        self.waited[eng][semkey] = val
        sobj = self._sobj(semkey)
        self.q[eng].append(lambda e, s=sobj, v=val: e.wait_ge(s, v))

    def _edges(self, eng, reads, writes):
        for k in reads:
            lw = self.last_w.get(k)
            if lw is not None and not (lw[0] == "PE" == eng):
                self._wait(eng, lw[0], lw[1])
        for k in writes:
            lw = self.last_w.get(k)
            if lw is not None and not (lw[0] == "PE" == eng):
                self._wait(eng, lw[0], lw[1])
            for skey, v in self.readers.get(k, ()):
                if not (skey == "PE" == eng):
                    self._wait(eng, skey, v)

    def _record(self, semkey, val, reads, writes):
        for k in writes:
            self.last_w[k] = (semkey, val)
            self.readers[k] = []
        for k in reads:
            self.readers.setdefault(k, []).append((semkey, val))

    def op(self, eng, fn, reads=(), writes=()):
        self._edges(eng, reads, writes)
        self.cnt[eng] += 1
        sobj = self.sem[eng]
        self.q[eng].append(lambda e, f=fn, s=sobj: f(e).then_inc(s, 1))
        self._record(eng, self.cnt[eng], reads, writes)
        self.nops += 1

    def ops(self, eng, fns, reads=(), writes=()):
        self._edges(eng, reads, writes)
        self.cnt[eng] += 1
        sobj = self.sem[eng]
        fl = list(fns)

        def run(e, fl=fl, s=sobj):
            for f in fl[:-1]:
                f(e)
            fl[-1](e).then_inc(s, 1)

        self.q[eng].append(run)
        self._record(eng, self.cnt[eng], reads, writes)
        self.nops += len(fl)

    def dma(self, queue, fn, reads=(), writes=()):
        lane = self.dma_issue[queue] % self.NLANES
        self.dma_issue[queue] += 1
        semkey = f"d{queue}:{lane}"
        self._wait(queue, semkey, self.cnt[semkey])
        self._edges(queue, reads, writes)
        self.cnt[semkey] += 16
        sobj = self.dsem[semkey]
        self.q[queue].append(lambda e, f=fn, s=sobj: f(e).then_inc(s, 16))
        self._record(semkey, self.cnt[semkey], reads, writes)
        self.nops += 1

    def cc(self, fn, reads=(), writes=()):
        self._wait("POOL", "CC", self.cnt["CC"])
        self._edges("POOL", reads, writes)
        self.cnt["CC"] += 16
        sobj = self.ccsem
        self.q["POOL"].append(lambda e, f=fn, s=sobj: f(e).then_inc(s, 16))
        self._record("CC", self.cnt["CC"], reads, writes)
        self.nops += 1

    def barrier(self):
        semkeys = list(self.sem) + list(self.dsem) + ["CC"]
        for consumer in ENGS:
            for sk in semkeys:
                if self.cnt.get(sk, 0) > 0:
                    self._wait(consumer, sk, self.cnt[sk])

    def finish(self, keys):
        self._edges("SP", keys, ())

    def emit(self, block):
        qs = self.q

        @block.tensor
        def _(e):
            for f in qs["PE"]:
                f(e)

        @block.vector
        def _(e):
            for f in qs["DVE"]:
                f(e)

        @block.scalar
        def _(e):
            for f in qs["ACT"]:
                f(e)

        @block.gpsimd
        def _(e):
            for f in qs["POOL"]:
                f(e)

        @block.sync
        def _(e):
            for f in qs["SP"]:
                f(e)


def bc_in(ap, n):
    return ap.unsqueeze(2).to_broadcast((ap.shape[0], ap.shape[1], n))


class Builder:
    def __init__(self, cfg: Cfg):
        self.cfg = cfg
        self.nc = bass.Bass("TRN2", target_bir_lowering=False, debug=False)
        self.declare_io()
        self.declare_sbuf()
        self.sc = Sched(self.nc)

    def declare_io(self):
        nc, cfg = self.nc, self.cfg
        L, NL = cfg.L, cfg.nlayer

        def D(name, shape, dt, kind=None):
            if kind:
                return nc.dram_tensor(name, shape, dt, kind=kind).ap()
            return nc.dram_tensor(name, shape, dt).ap()

        self.x_fm = D("x_fm", [D_MODEL, L], F32, "ExternalInput")
        self.wi = D("wi", [NL, D_MODEL, D_IN], F32, "ExternalInput")
        self.wo = D("wo", [NL, D_INNER, D_MODEL], F32, "ExternalInput")
        self.dcwL = D("dcwL", [NL, NXC * DCONV, 128, 128], F32, "ExternalInput")
        self.cbL = D("cbL", [NL, 128, NXC], F32, "ExternalInput")
        self.dtbL = D("dtbL", [NL, NH, 1], F32, "ExternalInput")
        self.alogL = D("alogL", [NL, NH, 1], F32, "ExternalInput")
        self.dL = D("dL", [NL, NH, 1], F32, "ExternalInput")
        self.gnwR = D("gnwR", [NL, 128, NZC], F32, "ExternalInput")
        self.bnwR = D("bnwR", [NL, 128, NDC], F32, "ExternalInput")
        self.fnwR = D("fnwR", [128, NDC], F32, "ExternalInput")
        self.w_gate = D("w_gate", [FFN_I, D_MODEL], F32, "ExternalInput")
        self.w_up = D("w_up", [FFN_I, D_MODEL], F32, "ExternalInput")
        self.w_down = D("w_down", [D_MODEL, FFN_I], F32, "ExternalInput")
        self.c_e64x = D("c_e64x", [NH, 2048], BF16, "ExternalInput")
        self.c_e64xb = D("c_e64xb", [NH, 2048], BF16, "ExternalInput")
        self.c_trid = D("c_trid", [32, 2048], BF16, "ExternalInput")
        self.c_tri16 = D("c_tri16", [32, 16 * 128], BF16, "ExternalInput")
        self.c_esel = D("c_esel", [NH, D_INNER], BF16, "ExternalInput")
        self.c_idb = D("c_idb", [128, 128], BF16, "ExternalInput")
        self.c_ones2 = D("c_ones2", [2, 128], BF16, "ExternalInput")
        self.c_ones64 = D("c_ones64", [NH, 128], BF16, "ExternalInput")
        self.c_onescol = D("c_onescol", [128, 1], BF16, "ExternalInput")
        self.out_fm = D("out_fm", [D_MODEL, L], F32, "ExternalOutput")
        st = cfg.stage
        if st in ("s1",):
            self.y_out = D("y_out", [D_MODEL, L], BF16, "ExternalOutput")
        if st in ("s1", "s2", "s3", "s4"):
            self.mx_out = D("mx_out", [1, 6], F32, "ExternalOutput")
        if st in ("s2", "s3", "s4", "s5"):
            self.mxcat = D("mxcat", [6, cfg.n_cores], F32, "ExternalInput")
        if st == "s2":
            self.y_in = D("y_in", [D_MODEL, L], BF16, "ExternalInput")
            self.upraw_out = D("upraw_out", [FFN_I, L], BF16, "ExternalOutput")
            self.gateraw_out = D("gateraw_out", [FFN_I, L], BF16, "ExternalOutput")
        if st == "s3":
            self.gateraw_in = D("gateraw_in", [FFN_I, L], BF16, "ExternalInput")
            self.t4_out = D("t4_out", [FFN_I, L], BF16, "ExternalOutput")
        if st == "s4":
            self.t4_in = D("t4_in", [FFN_I, L], BF16, "ExternalInput")
            self.upraw_in = D("upraw_in", [FFN_I, L], BF16, "ExternalInput")
            self.t5_out = D("t5_out", [FFN_I, L], BF16, "ExternalOutput")
        if st == "s5":
            self.t5_in = D("t5_in", [FFN_I, L], BF16, "ExternalInput")
            self.x1_in = D("x1_in", [D_MODEL, L], F32, "ExternalInput")
        self.gmax_d = D("gmax_d", [1, 6], F32)
        self.mxr_d = D("mxr_d", [6, cfg.n_cores], F32)
        self.wi_bf = D("wi_bf", [NL, D_MODEL, D_IN], BF16)
        self.wo_bf = D("wo_bf", [NL, D_INNER, D_MODEL], BF16)
        self.dcw_bf = D("dcw_bf", [NL, NXC * DCONV, 128, 128], BF16)
        self.xs_b = [D(f"xs_b{i}", [D_INNER, 128], BF16) for i in range(2)]
        self.lt_b = [D(f"lt_b{i}", [2, 128, 16], BF16) for i in range(2)]
        self.red_b = D("red_b", [128, 1], F32)
        self.cc_in = D("cc_in", [1, 4], F32)
        self.cc_out = D("cc_out", [1, 4], F32)
        if cfg.do_ffn:
            self.wg_dq = D("wg_dq", [FFN_I, D_MODEL], BF16)
            self.wu_dq = D("wu_dq", [FFN_I, D_MODEL], BF16)
            self.wd_dq = D("wd_dq", [D_MODEL, FFN_I], BF16)
            self.up_d = D("up_d", [FFN_I, L], BF16)
            self.gate_d = D("gate_d", [FFN_I, L], BF16)

    def declare_sbuf(self):
        nc, cfg = self.nc, self.cfg
        L, T = cfg.L, cfg.TSEG
        self._off = 16640

        def A(name, shape, dtype, at=None):
            if at is None:
                nbytes = shape[1] * mybir.dt.size(dtype)
                t = nc.alloc_sbuf_tensor_at(name, shape, dtype, offset=self._off)
                self._off += (nbytes + 31) // 32 * 32
                return t
            return nc.alloc_sbuf_tensor_at(name, shape, dtype, offset=at)

        self.h = A("h", [128, NDC * L], F32)
        self.sE64X = [A(f"sE64X{i}", [98, 2048], BF16) for i in range(2)]
        self.sESEL = A("sESEL", [NH, D_INNER], BF16)
        self.sIDB = A("sIDB", [128, 128], BF16)
        self.sONES2 = A("sONES2", [2, 128], BF16)
        self.sONES64 = A("sONES64", [NH, 128], BF16)
        self.sONESC = A("sONESC", [128, 1], BF16)
        self.sCB = A("sCB", [128, NXC], F32)
        self.sGNWC = A("sGNWC", [128, NZC], F32)
        self.sBNWC = A("sBNWC", [128, NDC], F32)
        self.sFNWC = A("sFNWC", [128, NDC], F32)
        self.sDTB = A("sDTB", [NH, 1], F32)
        self.sALOG = A("sALOG", [NH, 1], F32)
        self.sACOL = A("sACOL", [NH, 1], F32)
        self.sDCOL = A("sDCOL", [NH, 1], F32)
        self.sDBF = A("sDBF", [NH, 1], BF16)
        self.sDEXP = A("sDEXP", [128, NZC], F32)
        self.sCARRY = A("sCARRY", [128, NXC * (DCONV - 1)], BF16)
        self.sZERO64 = A("sZERO64", [NH, 128], F32)
        self._sEPS1 = A("sEPS1", [128, 2], F32)
        self.sEPS = self._sEPS1.ap()[:, 0:1]
        self.sONE1 = self._sEPS1.ap()[:, 1:2]
        self.sEVAC = A("sEVAC", [128, 516], F32)
        self.sXB = A("sXB", [128, 512], BF16)
        self.sXB2 = A("sXB2", [128, 512], BF16)
        self.sROW = A("sROW", [1, 512], F32)
        self.sROW2 = A("sROW2", [1, 512], F32)
        self.sROW3 = A("sROW3", [1, 512], F32)
        self.sRH = A("sRH", [1, 512], BF16)
        self.sRL = A("sRL", [1, 512], BF16)
        self.sRED = A("sRED", [1, 128], F32)
        self.sSCR = A("sSCR", [1, 16], F32)
        self.sWIB = [A(f"sWIB{i}", [128, 2048], BF16) for i in range(2)]
        self.sWOB = self.sWIB
        self.sXTILE = A("sXTILE", [128, 512], F32)
        self.sXBCB = A("sXBCB", [128, 515], BF16)
        self.sCACC = A("sCACC", [128, 512], F32)
        self.sCH1 = self.sCACC.ap()[:, 0:256].bitcast(BF16)
        self.sCH2 = self.sCACC.ap()[:, 256:512].bitcast(BF16)
        self.sMXC = A("sMXC", [128, 8], F32)
        self.sSC = A("sSC", [128, 8], F32)
        seg_base = self._off
        self.sS = A("sS", [128, D_INNER], F32)
        self.sSBF = A("sSBF", [128, D_INNER], BF16)
        self.sHN = A("sHN", [128, NDC * T], BF16)
        self.sZ = A("sZ", [128, NZC * T], BF16)
        self.sCO = A("sCO", [128, NXC * T], BF16)
        self.sY = A("sY", [128, NZC * T], BF16)
        self.sDT = A("sDT", [NH, T], F32)
        self.sAA = A("sAA", [NH, T], F32)
        self.sL = [A(f"sL{i}", [NH, 128], F32) for i in range(2)]
        self.sHL = [A(f"sHL{i}", [128, 128], BF16) for i in range(2)]
        self.sMLH = A("sMLH", [98, 16 * 128], BF16)
        self.sLEND = [A(f"sLEND{i}", [NH, 1], F32) for i in range(2)]
        self.sGCOL = [A(f"sGCOL{i}", [NH, 1], F32) for i in range(2)]
        self.sEV2 = A("sEV2", [NH, 128], F32)
        self.sDV = A("sDV", [128, 128], BF16)
        self.sDVT = A("sDVT", [128, 128], BF16)
        self.sEXPL = [A(f"sEXPL{i}", [NH, 128], BF16) for i in range(2)]
        self.sGAM = [A(f"sGAM{i}", [NH, 128], BF16) for i in range(2)]
        self.sG = [A(f"sG{i}", [128, 128], BF16) for i in range(2)]
        self.sBT = [A(f"sBT{i}", [128, 128], BF16) for i in range(2)]
        self.sLTB = [A(f"sLTB{i}", [128, 128], BF16) for i in range(2)]
        xt_base = self._off
        self.sXT = [A(f"sXT{i}", [128, D_INNER], BF16) for i in range(2)]
        self.sWXT = A("sWXT", [128, D_INNER], BF16)
        self.sMG2 = [A(f"sMG{i}", [128, 16 * 128], BF16) for i in range(2)]
        self.sDCW = A("sDCW", [128, NXC * DCONV * 128], BF16, at=xt_base)
        seg_top = self._off
        if cfg.do_ffn:
            off = seg_base
            def F(name, shape, dtype):
                nonlocal off
                t = nc.alloc_sbuf_tensor_at(name, shape, dtype, offset=off)
                off += (shape[1] * mybir.dt.size(dtype) + 31) // 32 * 32
                return t
            self.sFW = [F(f"sFW{i}", [128, 4096], BF16) for i in range(2)]
            self.sBLK = F("sBLK", [128, 96], F32)
            self.sRS = F("sRS", [128, 96], F32)
            self.sDQ1 = F("sDQ1", [128, 3072], F32)
            self.sO = F("sO", [128, 24 * 512], BF16)
            self.sT1 = [F(f"sT1_{i}", [128, 512], BF16) for i in range(2)]
            self.sT2 = [F(f"sT2_{i}", [128, 512], BF16) for i in range(2)]
            self.sT3 = [F(f"sT3_{i}", [128, 512], BF16) for i in range(2)]
            self.sQT = [F(f"sQT{i}", [128, 512], F32) for i in range(2)]
            if cfg.stage == "s2":
                self.sYQ = F("sYQ", [128, NDC * 2048], BF16)
            if cfg.stage == "all":
                assert off <= seg_top, (off, seg_top)

        P = nc.alloc_psum_tensor
        self.pPS = P("pPS", [128, 4096], F32)

    def pe_transpose_bf(self, src_ap, dst_sbuf_ap, rkeys, wkeys):
        sc = self.sc
        n_p, n_f = src_ap.shape[0], src_ap.shape[1]
        sc.ops("PE", [lambda e: e.transpose(out=self.pPS.ap()[:, 3968:4096].bitcast(BF16)[0:n_f, 0:n_p], in_=src_ap,
                                            identity=self.sIDB[0:n_p, 0:n_p])],
               reads=list(rkeys) + ["cIDB"], writes=["P7"])
        sc.op("DVE", lambda e: e.tensor_copy(out=dst_sbuf_ap, in_=self.pPS.ap()[:, 3968:4096].bitcast(BF16)[0:n_f, 0:n_p]),
              reads=["P7"], writes=list(wkeys))

    def row_hilo(self, row_ap, T, rkey):
        sc = self.sc
        sc.op("DVE", lambda e: e.tensor_copy(out=self.sRH[0:1, 0:T], in_=row_ap),
              reads=[rkey], writes=["rhl"])
        sc.op("DVE", lambda e: e.tensor_tensor(out=self.sRL[0:1, 0:T], in0=row_ap,
                                               in1=self.sRH[0:1, 0:T], op=ALU.subtract),
              reads=[rkey, "rhl"], writes=["rhl"])

    def bcast_row(self, lhsT_ap, lkeys, T, psum_ap, pkey):
        sc = self.sc
        sc.ops("PE", [lambda e: e.matmul(out=psum_ap, lhsT=lhsT_ap,
                                         rhs=self.sRH[0:1, 0:T], start=True, stop=False),
                      lambda e: e.matmul(out=psum_ap, lhsT=lhsT_ap,
                                         rhs=self.sRL[0:1, 0:T], start=False, stop=True)],
               reads=list(lkeys) + ["rhl"], writes=[pkey])

    def load_consts(self):
        sc = self.sc
        pairs = [(self.sESEL, self.c_esel, "cESEL"),
                 (self.sIDB, self.c_idb, "cIDB"), (self.sONES2, self.c_ones2, "cONES2"),
                 (self.sONES64, self.c_ones64, "cONES64"),
                 (self.sONESC, self.c_onescol, "cONESC")]
        for dst, src, key in pairs:
            sc.dma("SP", lambda e, d=dst, s=src: e.dma_start(out=d[:], in_=s), writes=[key])
        for i, csrc in ((0, self.c_e64x), (1, self.c_e64xb)):
            sc.dma("SP", lambda e, i=i, csrc=csrc: e.dma_start(
                out=self.sE64X[i].ap()[0:64, :], in_=csrc[:]), writes=["cE64X"])
            sc.dma("SP", lambda e, i=i: e.dma_start(
                out=self.sE64X[i].ap()[66:98, :], in_=self.c_trid[:]), writes=["cE64X"])
        sc.dma("SP", lambda e: e.dma_start(out=self.sMLH.ap()[66:98, :],
                                           in_=self.c_tri16[:]), writes=["cMLH"])
        sc.op("DVE", lambda e: e.memset(self.sMLH.ap()[64:66, :], 1.0),
              writes=["cMLH"])
        sc.dma("POOL", lambda e: e.dma_start(out=self.sFNWC[:], in_=self.fnwR[:]),
               writes=["fnwb"])
        sc.op("DVE", lambda e: e.memset(self.sZERO64[:], 0.0), writes=["zero64"])
        sc.op("DVE", lambda e: e.memset(self.sEPS[:], LN_EPS), writes=["ceps"])
        sc.op("DVE", lambda e: e.memset(self.sONE1[:], 1.0), writes=["cone"])
        sc.op("DVE", lambda e: e.memset(self.sCARRY[:], 0.0), writes=["carry"])
        sc.op("DVE", lambda e: e.memset(self.sMXC[:], 0.0), writes=["mxc"])
        sc.op("DVE", lambda e: e.memset(self.sSCR[:], 0.0), writes=["scr"])
        sc.dma("ACT", lambda e: e.dma_start(out=self.cc_in, in_=self.sSCR[0:1, 0:4]),
               reads=["scr"], writes=["ccin"])

    def preconvert_weights(self):
        sc = self.sc
        for l in range(self.cfg.nlayer):
            sc.dma("POOL", lambda e, l=l: e.dma_start(out=self.wi_bf[l], in_=self.wi[l]),
                   writes=[f"wi_bf:{l}"])
            sc.dma("POOL", lambda e, l=l: e.dma_start(out=self.wo_bf[l], in_=self.wo[l]),
                   writes=[f"wo_bf:{l}"])
            sc.dma("POOL", lambda e, l=l: e.dma_start(out=self.dcw_bf[l], in_=self.dcwL[l]),
                   writes=[f"dcw_bf:{l}"])

    def load_layer_params(self, l):
        sc = self.sc
        for dst, src, key in [(self.sCB, self.cbL[l], "cb"),
                              (self.sDTB, self.dtbL[l], "dtb"), (self.sALOG, self.alogL[l], "alog"),
                              (self.sDCOL, self.dL[l], "dcol")]:
            sc.dma("SP", lambda e, d=dst, s=src: e.dma_start(out=d[:], in_=s), writes=[key])
        sc.dma("POOL", lambda e, l=l: e.dma_start(out=self.sGNWC[:], in_=self.gnwR[l]),
               writes=["gnwb"])
        sc.dma("POOL", lambda e, l=l: e.dma_start(out=self.sBNWC[:], in_=self.bnwR[l]),
               writes=["bnwb"])
        sc.op("ACT", lambda e: e.activation(out=self.sACOL[:], in_=self.sALOG[:], func=AF.Exp),
              reads=["alog"], writes=["acol"])
        sc.op("DVE", lambda e: e.tensor_scalar(out=self.sACOL[:], in0=self.sACOL[:],
                                               scalar1=-1.0, scalar2=None, op0=ALU.mult),
              reads=["acol"], writes=["acol"])
        sc.op("DVE", lambda e: e.tensor_copy(out=self.sDBF[:], in_=self.sDCOL[:]),
              reads=["dcol"], writes=["dbf"])
        sc.ops("PE", [(lambda e, k=k: e.matmul(out=self.pPS[:, 3072 + k:3072 + k + 1],
                                               lhsT=self.sESEL[:, 128 * k:128 * (k + 1)],
                                               rhs=self.sDBF[:], start=True, stop=True))
                      for k in range(NZC)],
               reads=["dbf", "cESEL"], writes=["P6"])
        sc.op("DVE", lambda e: e.tensor_copy(out=self.sDEXP[:], in_=self.pPS[:, 3072 + 0:3072 + NZC]),
              reads=["P6"], writes=["dexp"])
        sc.op("DVE", lambda e: e.memset(self.sS[:], 0.0), writes=["S"])
        sc.op("DVE", lambda e: e.memset(self.sCARRY[:], 0.0), writes=["carry"])

    def rms_rows(self, T, get_tile, rkeys, nchunks, mean_div, prep=None):
        sc = self.sc
        for k in range(nchunks):
            if prep is not None:
                prep(k)
            ap = get_tile(k)
            sc.op("ACT", lambda e, ap=ap: e.activation(out=self.sXB[:, 0:T], in_=ap,
                                                       func=AF.Square),
                  reads=rkeys(k), writes=["sXB"])
            sc.ops("PE", [lambda e, k=k: e.matmul(out=self.pPS[0:1, 3072 + 0:3072 + T],
                                                  lhsT=self.sONESC[:], rhs=self.sXB[:, 0:T],
                                                  start=(k == 0), stop=(k == nchunks - 1))],
                   reads=["sXB", "cONESC"], writes=["P6"])
        sc.op("ACT", lambda e: e.activation(out=self.sROW2[0:1, 0:T], in_=self.pPS[0:1, 3072 + 0:3072 + T],
                                            func=AF.Ln, scale=1.0 / mean_div,
                                            bias=self.sEPS[0:1]),
              reads=["P6", "ceps"], writes=["rowB0"])
        sc.op("ACT", lambda e: e.activation(out=self.sROW[0:1, 0:T], in_=self.sROW2[0:1, 0:T],
                                            func=AF.Exp, scale=-0.5),
              reads=["rowB0"], writes=["rowA0"])

    def ln_rows(self, T, get_tile, rkeys, prep=None):
        sc = self.sc
        for k in range(NDC):
            if prep is not None:
                prep(k)
            ap = get_tile(k)
            sc.op("DVE", lambda e, ap=ap: e.tensor_copy(out=self.sXB2[:, 0:T], in_=ap),
                  reads=rkeys(k), writes=["sXB2"])
            sc.op("ACT", lambda e, ap=ap: e.activation(out=self.sXB[:, 0:T], in_=ap,
                                                       func=AF.Square),
                  reads=rkeys(k), writes=["sXB"])
            sc.ops("PE", [lambda e, k=k: e.matmul(out=self.pPS[0:1, 3072 + 0:3072 + T], lhsT=self.sONESC[:],
                                                  rhs=self.sXB2[:, 0:T], start=(k == 0),
                                                  stop=(k == NDC - 1))],
                   reads=["sXB2", "cONESC"], writes=["P6"])
            sc.ops("PE", [lambda e, k=k: e.matmul(out=self.pPS[0:1, 3584:3584 + T],
                                                  lhsT=self.sONESC[:],
                                                  rhs=self.sXB[:, 0:T], start=(k == 0),
                                                  stop=(k == NDC - 1))],
                   reads=["sXB", "cONESC"], writes=["P7"])
        sc.op("DVE", lambda e: e.tensor_scalar(out=self.sROW3[0:1, 0:T],
                                               in0=self.pPS[0:1, 3072 + 0:3072 + T],
                                               scalar1=1.0 / D_MODEL, scalar2=None, op0=ALU.mult),
              reads=["P6"], writes=["rowA1"])
        sc.op("DVE", lambda e: e.tensor_scalar(out=self.sROW2[0:1, 0:T],
                                               in0=self.pPS[0:1, 3584:3584 + T],
                                               scalar1=1.0 / D_MODEL, scalar2=None, op0=ALU.mult),
              reads=["P7"], writes=["rowB0"])
        sc.op("DVE", lambda e: e.tensor_tensor(out=self.sROW[0:1, 0:T],
                                               in0=self.sROW3[0:1, 0:T],
                                               in1=self.sROW3[0:1, 0:T], op=ALU.mult),
              reads=["rowA1"], writes=["rowA0"])
        sc.op("DVE", lambda e: e.tensor_tensor(out=self.sROW2[0:1, 0:T],
                                               in0=self.sROW2[0:1, 0:T],
                                               in1=self.sROW[0:1, 0:T], op=ALU.subtract),
              reads=["rowB0", "rowA0"], writes=["rowB0"])
        sc.op("ACT", lambda e: e.activation(out=self.sROW2[0:1, 0:T], in_=self.sROW2[0:1, 0:T],
                                            func=AF.Ln, bias=self.sEPS[0:1]),
              reads=["rowB0", "ceps"], writes=["rowB0"])
        sc.op("ACT", lambda e: e.activation(out=self.sROW[0:1, 0:T], in_=self.sROW2[0:1, 0:T],
                                            func=AF.Exp, scale=-0.5),
              reads=["rowB0"], writes=["rowA0"])
        sc.op("DVE", lambda e: e.tensor_tensor(out=self.sROW2[0:1, 0:T],
                                               in0=self.sROW3[0:1, 0:T],
                                               in1=self.sROW[0:1, 0:T], op=ALU.mult),
              reads=["rowA1", "rowA0"], writes=["rowB0"])

    def ln_bcasts(self, T):
        self.row_hilo(self.sROW[0:1, 0:T], T, "rowA0")
        self.bcast_row(self.sONES2[0:1, :], ["cONES2"], T, self.pPS[:, 1024 + 0:1024 + T], "P2")
        self.row_hilo(self.sROW2[0:1, 0:T], T, "rowB0")
        self.bcast_row(self.sONES2[0:1, :], ["cONES2"], T, self.pPS[:, 1024 + 512:1024 + 512 + T], "P3")

    def phase_init_ln(self):
        cfg, sc = self.cfg, self.sc
        L = cfg.L
        for j in range(L // 512):
            t0 = 512 * j

            def prep(k, t0=t0):
                sc.dma("SP", lambda e, k=k: e.dma_start(
                    out=self.sXTILE[:],
                    in_=self.x_fm[128 * k:128 * (k + 1), t0:t0 + 512]),
                    writes=["xt"])
            self.ln_rows(512, lambda k: self.sXTILE[:], lambda k: ["xt"], prep=prep)
            self.ln_bcasts(512)
            for k in range(NDC):
                prep(k)
                hsl = self.h[:, k * L + t0: k * L + t0 + 512]
                sc.op("DVE", lambda e: e.tensor_tensor(
                    out=self.sEVAC[:, 0:512], in0=self.sXTILE[:],
                    in1=self.pPS[:, 1024 + 0:1024 + 512], op=ALU.mult),
                    reads=["xt", "P2"], writes=["evac"])
                sc.op("DVE", lambda e, hsl=hsl: e.tensor_tensor(
                    out=hsl, in0=self.sEVAC[:, 0:512], in1=self.pPS[:, 1024 + 512:1024 + 1024],
                    op=ALU.subtract),
                    reads=["evac", "P3"], writes=[f"h:{k}:{j}"])

    def phase_A(self, l, seg):
        cfg, sc = self.cfg, self.sc
        T, L = cfg.TSEG, cfg.L
        t0 = seg * T
        hsl = lambda k: self.h[:, k * L + t0: k * L + t0 + T]
        self.rms_rows(T, hsl, lambda k: [f"h:{k}:{seg}"], NDC, D_MODEL)
        self.row_hilo(self.sROW[0:1, 0:T], T, "rowA0")
        self.bcast_row(self.sONES2[0:1, :], ["cONES2"], T,
                       self.pPS[:, 1024 + 0:1024 + T], "P2")
        for k in range(NDC):
            hap = hsl(k)
            sc.op("DVE", lambda e, k=k, hap=hap: e.scalar_tensor_tensor(
                out=self.sHN[:, k * T:(k + 1) * T], in0=hap,
                scalar=self.sBNWC[:, k:k + 1], in1=self.pPS[:, 1024 + 0:1024 + T],
                op0=ALU.mult, op1=ALU.mult),
                reads=[f"h:{k}:{seg}", "P2", "bnwb"], writes=[f"hn:{k}"])
        hn_keys = [f"hn:{k}" for k in range(NDC)]
        sc.dma("SP", lambda e, l=l: e.dma_start(
            out=self.sDCW.ap().rearrange("p (m n) -> p m n", m=NXC * DCONV),
            in_=self.dcw_bf[l].rearrange("m p n -> p m n")),
            reads=[f"dcw_bf:{l}", "xst:0", "xst:1", "dxt", "wxt", "mg:0", "mg:1"],
            writes=["dcw"])
        for pos, np_ in enumerate([17, 16] + list(range(8, 16)) + list(range(8))):
            buf = pos % 2
            wkey = f"wib:{buf}"
            w = 256 if np_ < 17 else 64
            nof = 256 * np_
            sc.dma("SP", lambda e, buf=buf, w=w, nof=nof, l=l: e.dma_start(
                out=self.sWIB[buf].ap()[:, 0:NDC * w].rearrange(
                    "p (k n) -> p k n", k=NDC),
                in_=self.wi_bf[l][:, nof:nof + w].rearrange("(k p) n -> p k n", p=128)),
                reads=[f"wi_bf:{l}"], writes=[wkey])
            for sub in range(2 if np_ < 17 else 1):
                n = 2 * np_ + sub
                ncols = 128 if n < 34 else 64
                pb = n % 2
                pk = f"P{pb}"
                psl = self.pPS[:, 512 * pb:512 * pb + T]
                sc.ops("PE", [(lambda e, k=k, buf=buf, w=w, sub=sub, ncols=ncols, psl=psl:
                               e.matmul(out=psl[0:ncols, :] if ncols < 128 else psl,
                                        lhsT=self.sWIB[buf][:, w * k + 128 * sub:
                                                            w * k + 128 * sub + ncols],
                                        rhs=self.sHN[:, k * T:(k + 1) * T],
                                        start=(k == 0), stop=(k == NDC - 1)))
                              for k in range(NDC)],
                       reads=[wkey] + hn_keys, writes=[pk])
                if n < NZC:
                    sc.op("ACT", lambda e, n=n, psl=psl: e.activation(
                        out=self.sZ[:, n * T:(n + 1) * T], in_=psl, func=AF.Silu),
                        reads=[pk], writes=[f"z:{n}"])
                elif n < 34:
                    self.conv_chunk(n - NZC, psl, pk)
                else:
                    self.dt_chunk(psl, pk)
        sc.op("DVE", lambda e: e.tensor_scalar(out=self.sAA[:, 0:T], in0=self.sDT[:, 0:T],
                                               scalar1=self.sACOL[:], scalar2=None,
                                               op0=ALU.mult),
              reads=["dt", "acol"], writes=["aa"])

    def conv_chunk(self, j, psl, pk):
        sc, T = self.sc, self.cfg.TSEG
        xb = self.sXBCB
        sc.op("DVE", lambda e, j=j: e.tensor_copy(out=xb[:, 0:3],
                                                  in_=self.sCARRY[:, 3 * j:3 * j + 3]),
              reads=["carry"], writes=["xbcb"])
        sc.op("DVE", lambda e: e.tensor_copy(out=xb[:, 3:3 + T], in_=psl),
              reads=[pk], writes=["xbcb"])
        sc.op("DVE", lambda e, j=j: e.tensor_copy(out=self.sCARRY[:, 3 * j:3 * j + 3],
                                                  in_=xb[:, T:T + 3]),
              reads=["xbcb"], writes=["carry"])
        cb_bank = 2 + (j % 2)
        cps = self.pPS[:, 512 * cb_bank:512 * cb_bank + T]

        def mk(e, j=j, cps=cps):
            r = None
            for kk in range(DCONV):
                r = e.matmul(out=cps, lhsT=self.sDCW[:, 128 * (4 * j + kk):
                                                     128 * (4 * j + kk) + 128],
                             rhs=xb[:, kk:kk + T],
                             start=(kk == 0), stop=(kk == DCONV - 1))
            return r
        sc.ops("PE", [mk], reads=["dcw", "xbcb"], writes=[f"P{cb_bank}"])
        sc.op("ACT", lambda e, j=j, cps=cps: e.activation(
            out=self.sCO[:, j * T:(j + 1) * T], in_=cps, func=AF.Silu,
            bias=self.sCB[:, j:j + 1]),
            reads=[f"P{cb_bank}", "cb"],
            writes=[f"co:{j}"] + [f"cox:{cc}" for cc in range(4)])

    def dt_chunk(self, psl, pk):
        sc, T = self.sc, self.cfg.TSEG
        t1 = self.sEVAC[0:64, 0:T]
        t4 = self.sXTILE[0:64, 0:T]
        ps = psl[0:64, :]
        sc.op("ACT", lambda e: e.activation(out=t1, in_=ps, func=AF.Abs, bias=self.sDTB[:]),
              reads=[pk, "dtb"], writes=["evac"])
        sc.op("ACT", lambda e: e.activation(out=t1, in_=t1, func=AF.Exp, scale=-1.0),
              reads=["evac"], writes=["evac"])
        sc.op("ACT", lambda e: e.activation(out=t1, in_=t1, func=AF.Ln,
                                            bias=self.sONE1[0:64]),
              reads=["evac", "cone"], writes=["evac"])
        sc.op("ACT", lambda e: e.activation(out=t4, in_=ps, func=AF.Relu, bias=self.sDTB[:]),
              reads=[pk, "dtb", "xt"], writes=["xt"])
        sc.op("DVE", lambda e: e.tensor_tensor(out=self.sDT[:, 0:T], in0=t1, in1=t4,
                                               op=ALU.add),
              reads=["evac", "xt"], writes=["dt"])

    def phase_B_prep_front(self, l, seg, c):
        cfg, sc = self.cfg, self.sc
        T, C = cfg.TSEG, cfg.C
        cof = c * C
        cbuf = c % 2
        sL, sHL = self.sL[cbuf], self.sHL[cbuf]
        kL, kHL = f"Lc:{cbuf}", f"hl:{cbuf}"
        kEND, kGC = f"Lend:{cbuf}", f"gcol:{cbuf}"
        sc.op("DVE", lambda e, cof=cof, sL=sL: e.tensor_tensor_scan(
            out=sL[:], data0=self.sAA[:, cof:cof + C], data1=self.sZERO64[:],
            initial=0.0, op0=ALU.add, op1=ALU.add),
            reads=["aa", "zero64"], writes=[kL])
        sc.op("DVE", lambda e, sL=sL, sHL=sHL: e.tensor_copy(
            out=sHL[0:64, :], in_=sL[:]), reads=[kL], writes=[kHL])
        sc.op("DVE", lambda e, sL=sL, sHL=sHL: e.tensor_tensor(
            out=sHL[64:128, :], in0=sL[:], in1=sHL[0:64, :], op=ALU.subtract),
            reads=[kL, kHL], writes=[kHL])
        for h2 in range(2):
            b0 = 8 * cbuf + 4 * h2
            wk = [f"mlh:{b0 + sl}" for sl in range(4)]
            out_hi = self.sMLH.ap()[0:32, 128 * b0:128 * b0 + 512].rearrange(
                "p (s n) -> p s n", s=4)
            out_lo = self.sMLH.ap()[32:64, 128 * b0:128 * b0 + 512].rearrange(
                "p (s n) -> p s n", s=4)
            sc.op("DVE", lambda e, out_hi=out_hi, h2=h2, sHL=sHL: e.tensor_copy(
                out=out_hi,
                in_=sHL[32 * h2:32 * h2 + 32, :].unsqueeze(1).to_broadcast((32, 4, 128))),
                reads=[kHL], writes=wk)
            sc.op("DVE", lambda e, out_lo=out_lo, h2=h2, sHL=sHL: e.tensor_copy(
                out=out_lo,
                in_=sHL[64 + 32 * h2:64 + 32 * h2 + 32, :].unsqueeze(1).to_broadcast((32, 4, 128))),
                reads=[kHL], writes=wk)
        sc.op("DVE", lambda e, sL=sL, cbuf=cbuf: e.tensor_copy(
            out=self.sLEND[cbuf][:], in_=sL[:, C - 1:C]), reads=[kL], writes=[kEND])
        sc.op("ACT", lambda e, cbuf=cbuf: e.activation(
            out=self.sGCOL[cbuf][:], in_=self.sLEND[cbuf][:], func=AF.Exp),
            reads=[kEND], writes=[kGC])
        sc.op("DVE", lambda e, cbuf=cbuf: e.tensor_scalar(
            out=self.sGAM[cbuf][:], in0=self.sONES64[:],
            scalar1=self.sGCOL[cbuf][:], scalar2=None, op0=ALU.mult),
            reads=["cONES64", kGC], writes=[f"gam:{cbuf}"])
        sc.op("ACT", lambda e, sL=sL, cbuf=cbuf: e.activation(
            out=self.sEXPL[cbuf][:], in_=sL[:], func=AF.Exp),
            reads=[kL], writes=[f"expl:{cbuf}"])
        sc.op("ACT", lambda e, sL=sL, cbuf=cbuf: e.activation(
            out=self.sEV2[:], in_=sL[:], func=AF.Exp,
            scale=-1.0, bias=self.sLEND[cbuf][:]),
            reads=[kL, kEND], writes=["ev2"])
        sc.op("DVE", lambda e, cof=cof: e.tensor_copy(
            out=self.sDV.ap()[0:64, :], in_=self.sDT[:, cof:cof + C]),
            reads=["dt"], writes=["dv"])
        sc.op("DVE", lambda e, cof=cof: e.tensor_tensor(
            out=self.sDV.ap()[64:128, :], in0=self.sEV2[:],
            in1=self.sDT[:, cof:cof + C], op=ALU.mult),
            reads=["ev2", "dt"], writes=["dv"])
        co3 = self.sCO.ap().rearrange("p (j t) -> p j t", j=NXC)
        xs_src = co3[:, 0:16, cof:cof + C]
        xbounce = self.xs_b[cbuf]
        sc.dma("SP", lambda e, xs_src=xs_src, xbounce=xbounce: e.dma_start(
            out=xbounce.rearrange("(j p) t -> p j t", p=128), in_=xs_src),
            reads=[f"cox:{c}"], writes=[f"xsb:{cbuf}"])
        sc.dma("SP", lambda e, cbuf=cbuf: e.dma_start_transpose(
            out=self.sXT[cbuf][:], in_=self.xs_b[cbuf]),
            reads=[f"xsb:{cbuf}"], writes=[f"xst:{cbuf}"])

    def phase_B_prep_back(self, l, seg, c):
        cfg, sc = self.cfg, self.sc
        T, C = cfg.TSEG, cfg.C
        cof = c * C
        cbuf = c % 2
        sHL = self.sHL[cbuf]
        kHL = f"hl:{cbuf}"
        co3 = self.sCO.ap().rearrange("p (j t) -> p j t", j=NXC)
        self.pe_transpose_bf(self.sDV[:], self.sDVT[:], ["dv"], ["dvt"])
        self.pe_transpose_bf(sHL[:], self.sLTB[cbuf][:], [kHL], [f"ltb:{cbuf}"])
        sc.op("DVE", lambda e, cbuf=cbuf: e.tensor_tensor(
            out=self.sWXT.ap().rearrange("p (h q) -> p h q", h=NH),
            in0=self.sXT[cbuf].ap().rearrange("p (h q) -> p h q", h=NH),
            in1=bc_in(self.sDVT.ap()[:, 64:128], HD), op=ALU.mult),
            reads=[f"xst:{cbuf}", "dvt"], writes=["wxt"])
        sc.op("DVE", lambda e, cbuf=cbuf: e.tensor_tensor(
            out=self.sXT[cbuf].ap().rearrange("p (h q) -> p h q", h=NH),
            in0=self.sXT[cbuf].ap().rearrange("p (h q) -> p h q", h=NH),
            in1=bc_in(self.sDVT.ap()[:, 0:64], HD), op=ALU.mult),
            reads=[f"xst:{cbuf}", "dvt", "wxt"], writes=["dxt", f"xst:{cbuf}"])
        bsl = co3[:, 16, cof:cof + C]
        csl = co3[:, 17, cof:cof + C]
        sc.ops("PE", [lambda e, bsl=bsl, csl=csl: e.matmul(
            out=self.pPS[:, 3584 + 0:3584 + 128], lhsT=bsl, rhs=csl, start=True, stop=True)],
            reads=[f"cox:{c}"], writes=["P7"])
        sc.op("DVE", lambda e, cbuf=cbuf: e.tensor_copy(
            out=self.sG[cbuf][:], in_=self.pPS[:, 3584 + 0:3584 + 128]),
            reads=["P7"], writes=[f"G:{cbuf}"])
        self.pe_transpose_bf(bsl, self.sBT[cbuf][:], [f"cox:{c}"], [f"BT:{cbuf}"])
        sc.op("DVE", lambda e, cof=cof, co3=co3: e.tensor_tensor(
            out=co3[:, 0:16, cof:cof + C], in0=co3[:, 0:16, cof:cof + C],
            in1=bc_in(self.sDEXP[:, 0:16], C), op=ALU.mult),
            reads=[f"cox:{c}", "dexp", f"xsb:{cbuf}"], writes=[f"cox:{c}"])

    def phase_B_compute(self, l, seg, c):
        cfg, sc = self.cfg, self.sc
        sc.op("DVE", lambda e: e.tensor_copy(out=self.sSBF[:], in_=self.sS[:]),
              reads=["S"], writes=["SBF"])
        self.state_update(c)
        if c + 1 < cfg.NCH:
            self.phase_B_prep_front(l, seg, c + 1)
        for hq in range(2):
            self.mask_quarter(c, hq)
            self.y_quarter(c, hq)
        if c + 1 < cfg.NCH:
            self.phase_B_prep_back(l, seg, c + 1)
        for hq in range(2, 4):
            self.mask_quarter(c, hq)
            self.y_quarter(c, hq)

    def mask_quarter(self, c, hq):
        sc = self.sc
        h2, s2 = hq // 2, hq % 2
        par = hq % 2
        cbuf = c % 2
        hgb = (4 * c + hq) % 2
        mgk = f"mg:{par}"
        sMG = self.sMG2[par]
        bb = self.lt_b[hgb]
        sLTB, kLTB = self.sLTB[cbuf], f"ltb:{cbuf}"
        sc.dma("SP", lambda e, hq=hq, bb=bb, sLTB=sLTB: e.dma_start(
            out=bb[0], in_=sLTB[:, 16 * hq:16 * hq + 16]),
            reads=[kLTB], writes=[f"ltbd:{hgb}"])
        sc.dma("SP", lambda e, hq=hq, bb=bb, sLTB=sLTB: e.dma_start(
            out=bb[1], in_=sLTB[:, 64 + 16 * hq:64 + 16 * hq + 16]),
            reads=[kLTB], writes=[f"ltbd:{hgb}"])
        sc.dma("SP", lambda e, bb=bb, s2=s2: e.dma_start(
            out=self.sE64X[s2].ap()[64:66, :], in_=bb.rearrange("r t h -> r (t h)")),
            reads=[f"ltbd:{hgb}"], writes=[f"flat:{s2}"])
        for sl in range(4):
            bank = (2, 3, 0, 1)[sl]
            po = self.pPS[:, 512 * bank:512 * bank + 512]
            i = 8 * cbuf + 4 * h2 + sl
            sc.ops("PE", [lambda e, po=po, i=i, sl=sl, s2=s2: e.matmul(
                out=po, lhsT=self.sMLH.ap()[0:98, 128 * i:128 * (i + 1)],
                rhs=self.sE64X[s2][0:98, 512 * sl:512 * (sl + 1)],
                start=True, stop=True)],
                reads=[f"mlh:{i}", "cMLH", "cE64X", f"flat:{s2}"],
                writes=[f"P{bank}"])
            sc.op("ACT", lambda e, sl=sl, po=po, sMG=sMG: e.activation(
                out=sMG[:, 512 * sl:512 * (sl + 1)], in_=po, func=AF.Exp),
                reads=[f"P{bank}"], writes=[mgk])
            mg3s = sMG.ap()[:, 512 * sl:512 * (sl + 1)].rearrange(
                "p (t h) -> p t h", h=16)
            sc.op("DVE", lambda e, mg3s=mg3s, sl=sl, cbuf=cbuf: e.tensor_tensor(
                out=mg3s, in0=mg3s,
                in1=bc_in(self.sG[cbuf].ap()[:, 32 * sl:32 * (sl + 1)], 16),
                op=ALU.mult),
                reads=[mgk, f"G:{cbuf}"], writes=[mgk])

    def y_quarter(self, c, hq):
        sc, cfg = self.sc, self.cfg
        T, C = cfg.TSEG, cfg.C
        cof = c * C
        cbuf = c % 2
        par = hq % 2
        mgk = f"mg:{par}"
        mg3 = self.sMG2[par].ap().rearrange("p (t h) -> p t h", h=16)
        co3 = self.sCO.ap().rearrange("p (j t) -> p j t", j=NXC)
        z3 = self.sZ.ap().rearrange("p (j t) -> p j t", j=NZC)
        y3 = self.sY.ap().rearrange("p (j t) -> p j t", j=NZC)
        csl = co3[:, 17, cof:cof + C]
        ks = list(range(4 * hq, 4 * hq + 4))
        pI = self.pPS[:, 2048:2560]
        pN = self.pPS[:, 2560:3072]
        pX = self.pPS[:, 3072:3584]
        sc.ops("PE", [(lambda e, k=k, i=i: e.matmul(
            out=pX[:, 128 * i:128 * (i + 1)],
            lhsT=self.sESEL[:, 128 * k:128 * (k + 1)], rhs=self.sEXPL[cbuf][:],
            start=True, stop=True)) for i, k in enumerate(ks)],
            reads=["cESEL", f"expl:{cbuf}"], writes=["P6"])
        sc.ops("PE", [(lambda e, k=k, i=i: e.matmul(
            out=pN[:, 128 * i:128 * (i + 1)],
            lhsT=self.sSBF[:, 128 * k:128 * (k + 1)], rhs=csl,
            start=True, stop=True)) for i, k in enumerate(ks)],
            reads=["SBF", f"cox:{c}"], writes=["P5"])

        def intra(e, ks=ks, hq=hq, cbuf=cbuf):
            r = None
            for i, k in enumerate(ks):
                for jj in range(4):
                    h_loc = (4 * k + jj) - 16 * hq
                    r = e.matmul(out=pI[32 * jj:32 * jj + 32, 128 * i:128 * (i + 1)],
                                 lhsT=self.sXT[cbuf][:, 128 * k + 32 * jj:
                                                     128 * k + 32 * jj + 32],
                                 rhs=mg3[:, :, h_loc],
                                 start=True, stop=True, tile_position=(0, 32 * jj))
            return r
        k0 = ks[0]
        xs_sl = co3[:, k0:k0 + 4, cof:cof + C]
        z_sl = z3[:, k0:k0 + 4, cof:cof + C]
        y_sl = y3[:, k0:k0 + 4, cof:cof + C]
        ch13 = self.sCH1.rearrange("p (j t) -> p j t", j=4)
        ch23 = self.sCH2.rearrange("p (j t) -> p j t", j=4)
        xs_keys = [f"cox:{c}"]
        z_keys = [f"z:{k}" for k in ks]
        y_keys = [f"y:{k}" for k in ks]
        sc.op("ACT", lambda e, pX=pX: e.activation(out=self.sCH1, in_=pX, func=AF.Copy),
              reads=["P6", "cacc"], writes=["ch1", "cacc"])
        sc.op("DVE", lambda e, pN=pN: e.tensor_tensor(out=self.sCH2, in0=pN,
                                                      in1=self.sCH1, op=ALU.mult),
              reads=["P5", "ch1", "cacc"], writes=["ch2", "cacc"])
        sc.ops("PE", [intra], reads=["dxt", mgk, f"xst:{cbuf}"], writes=["P4"])
        sc.op("DVE", lambda e, pI=pI: e.tensor_tensor(out=self.sCH1, in0=pI,
                                                      in1=self.sCH2, op=ALU.add),
              reads=["P4", "ch2", "cacc"], writes=["ch1", "cacc"])
        sc.op("POOL", lambda e, xs_sl=xs_sl, ch13=ch13, ch23=ch23: e.tensor_tensor(
            out=ch23, in0=ch13, in1=xs_sl, op=ALU.add),
            reads=["ch1", "cacc"] + xs_keys, writes=["ch2", "cacc"])
        sc.op("POOL", lambda e, z_sl=z_sl, y_sl=y_sl, ch23=ch23: e.tensor_tensor(
            out=y_sl, in0=ch23, in1=z_sl, op=ALU.mult),
            reads=["ch2", "cacc"] + z_keys, writes=y_keys)

    def state_update(self, c):
        sc = self.sc
        cbuf = c % 2
        for sl in range(4):
            pA = self.pPS[:, 0:512]
            pB = self.pPS[:, 512:1024]
            esl = self.sESEL[:, 512 * sl:512 * (sl + 1)]
            ssl = self.sS[:, 512 * sl:512 * (sl + 1)]
            wsl = self.sWXT[:, 512 * sl:512 * (sl + 1)]
            sc.ops("PE", [lambda e, esl=esl, pA=pA, cbuf=cbuf: e.matmul(
                out=pA, lhsT=self.sGAM[cbuf][:], rhs=esl, start=True, stop=True)],
                reads=[f"gam:{cbuf}", "cESEL"], writes=["P0"])
            sc.ops("PE", [lambda e, wsl=wsl, pB=pB, cbuf=cbuf: e.matmul(
                out=pB, lhsT=self.sBT[cbuf][:], rhs=wsl, start=True, stop=True)],
                reads=[f"BT:{cbuf}", "wxt"], writes=["P1"])
            sc.op("DVE", lambda e, ssl=ssl, pA=pA: e.tensor_tensor(
                out=ssl, in0=pA, in1=ssl, op=ALU.mult),
                reads=["P0", "S"], writes=["S"])
            sc.op("DVE", lambda e, ssl=ssl, pB=pB: e.tensor_tensor(
                out=ssl, in0=ssl, in1=pB, op=ALU.add),
                reads=["P1", "S"], writes=["S"])

    def phase_C(self, l, seg):
        cfg, sc = self.cfg, self.sc
        T, L = cfg.TSEG, cfg.L
        t0 = seg * T
        ysl = lambda k: self.sY[:, k * T:(k + 1) * T]
        self.rms_rows(T, ysl, lambda k: [f"y:{k}"], NZC, D_INNER)
        self.row_hilo(self.sROW[0:1, 0:T], T, "rowA0")
        self.bcast_row(self.sONES2[0:1, :], ["cONES2"], T,
                       self.pPS[:, 3072 + 0:3072 + T], "P6")
        for k in range(NZC):
            sc.op("DVE", lambda e, k=k: e.scalar_tensor_tensor(
                out=self.sY[:, k * T:(k + 1) * T], in0=self.sY[:, k * T:(k + 1) * T],
                scalar=self.sGNWC[:, k:k + 1], in1=self.pPS[:, 3072 + 0:3072 + T],
                op0=ALU.mult, op1=ALU.mult),
                reads=[f"y:{k}", "P6", "gnwb"], writes=[f"y:{k}"])
        y_keys = [f"y:{k}" for k in range(NZC)]
        allp = [f"P{b}" for b in range(8)]
        for f in range(NZC):
            buf = f % 2
            sc.dma("SP", lambda e, f=f, buf=buf, l=l: e.dma_start(
                out=self.sWOB[buf][:, 0:1024], in_=self.wo_bf[l][128 * f:128 * (f + 1), :]),
                reads=[f"wo_bf:{l}"], writes=[f"wib:{buf}"])

            def mk(e, f=f, buf=buf):
                r = None
                for dk in range(NDC):
                    po = self.pPS[:, 512 * dk:512 * dk + 512]
                    r = e.matmul(out=po, lhsT=self.sWOB[buf][:, 128 * dk:128 * (dk + 1)],
                                 rhs=self.sY[:, f * T:f * T + 512],
                                 start=(f == 0), stop=(f == NZC - 1))
                return r
            sc.ops("PE", [mk], reads=[f"wib:{buf}"] + y_keys, writes=allp)
        for dk in range(NDC):
            po = self.pPS[:, 512 * dk:512 * dk + 512]
            hsl = self.h[:, dk * L + t0: dk * L + t0 + 512]
            sc.op("DVE", lambda e, po=po, hsl=hsl: e.tensor_tensor(
                out=hsl, in0=hsl, in1=po, op=ALU.add),
                reads=[f"P{dk}", f"h:{dk}:{seg}"], writes=[f"h:{dk}:{seg}"])

    def phase_x1(self):
        cfg, sc = self.cfg, self.sc
        L = cfg.L
        for j in range(L // 512):
            t0 = 512 * j
            hsl = lambda k, t0=t0: self.h[:, k * L + t0: k * L + t0 + 512]
            self.rms_rows(512, hsl, lambda k, j=j: [f"h:{k}:{j}"], NDC, D_MODEL)
            self.row_hilo(self.sROW[0:1, 0:512], 512, "rowA0")
            self.bcast_row(self.sONES2[0:1, :], ["cONES2"], 512,
                           self.pPS[:, 1024 + 0:1024 + 512], "P2")
            for k in range(NDC):
                sc.dma("SP", lambda e, k=k, t0=t0: e.dma_start(
                    out=self.sXTILE[:],
                    in_=self.x_fm[128 * k:128 * (k + 1), t0:t0 + 512]),
                    writes=["xt"])
                hap = hsl(k)
                sc.op("DVE", lambda e, k=k, hap=hap: e.scalar_tensor_tensor(
                    out=self.sEVAC[:, 0:512], in0=hap,
                    scalar=self.sFNWC[:, k:k + 1], in1=self.pPS[:, 1024 + 0:1024 + 512],
                    op0=ALU.mult, op1=ALU.mult),
                    reads=[f"h:{k}:{j}", "P2", "fnwb"], writes=["evac"])
                sc.op("DVE", lambda e, hap=hap: e.tensor_tensor(
                    out=hap, in0=self.sEVAC[:, 0:512],
                    in1=self.sXTILE[:], op=ALU.add),
                    reads=["evac", "xt"], writes=[f"h:{k}:{j}"])

    def dequant_matrix(self, wsrc, wdst, n_i, n_d, wkey):
        sc = self.sc
        nb = n_d // 32
        for i in range(n_i // 128):
            sc.dma("SP", lambda e, i=i, wsrc=wsrc: e.dma_start(
                out=self.sDQ1[:, 0:n_d], in_=wsrc[128 * i:128 * (i + 1), :]),
                writes=["dq1"])
            w3 = self.sDQ1.ap()[:, 0:n_d].rearrange("p (b k) -> p b k", k=32)
            sc.op("DVE", lambda e, w3=w3: e.reduce_max(
                out=self.sBLK[:, 0:nb].unsqueeze(2), in_=w3, axis=AXX,
                apply_absolute_value=True),
                reads=["dq1"], writes=["blk"])
            sc.op("DVE", lambda e: e.tensor_scalar(
                out=self.sBLK[:, 0:nb], in0=self.sBLK[:, 0:nb], scalar1=1.0 / 7.0,
                scalar2=1e-12, op0=ALU.mult, op1=ALU.max),
                reads=["blk"], writes=["blk"])
            sc.op("DVE", lambda e: e.reciprocal(out=self.sRS[:, 0:nb], in_=self.sBLK[:, 0:nb]),
                  reads=["blk"], writes=["rs"])
            sc.op("DVE", lambda e, w3=w3: e.tensor_tensor(
                out=w3, in0=w3, in1=bc_in(self.sRS[:, 0:nb], 32), op=ALU.mult),
                reads=["dq1", "rs"], writes=["dq1"])
            sc.op("DVE", lambda e: e.tensor_scalar(
                out=self.sDQ1[:, 0:n_d], in0=self.sDQ1[:, 0:n_d], scalar1=MAGIC,
                scalar2=MAGIC, op0=ALU.add, op1=ALU.subtract),
                reads=["dq1"], writes=["dq1"])
            sc.op("DVE", lambda e, w3=w3: e.tensor_tensor(
                out=w3, in0=w3, in1=bc_in(self.sBLK[:, 0:nb], 32), op=ALU.mult),
                reads=["dq1", "blk"], writes=["dq1"])
            sc.op("DVE", lambda e: e.tensor_copy(out=self.sFW[0][:, 0:n_d],
                                                 in_=self.sDQ1[:, 0:n_d]),
                  reads=["dq1"], writes=["fw:0"])
            sc.dma("SP", lambda e, i=i, wdst=wdst: e.dma_start(
                out=wdst[128 * i:128 * (i + 1), :], in_=self.sFW[0][:, 0:n_d]),
                reads=["fw:0"], writes=[wkey])

    def absmax_to_scalar(self, col_ap, rkey, idx):
        sc = self.sc
        sc.dma("ACT", lambda e, col_ap=col_ap: e.dma_start(out=self.red_b, in_=col_ap),
               reads=[rkey], writes=["redb"])
        sc.dma("ACT", lambda e: e.dma_start(
            out=self.sRED[:], in_=self.red_b.rearrange("p o -> (p o)").unsqueeze(0)),
            reads=["redb"], writes=["redrow"])
        sc.op("DVE", lambda e, idx=idx: e.reduce_max(
            out=self.sSCR[0:1, idx:idx + 1], in_=self.sRED[:], axis=AXX),
            reads=["redrow"], writes=["scr"])
        sc.dma("ACT", lambda e, idx=idx: e.dma_start(
            out=self.cc_in[0:1, idx:idx + 1], in_=self.sSCR[0:1, idx:idx + 1]),
            reads=["scr"], writes=["ccin"])

    def allreduce_max(self):
        sc = self.sc
        sc.cc(lambda e: e.collective_compute(
            "AllReduce", ALU.max, replica_groups=[list(range(self.cfg.n_cores))],
            ins=[self.cc_in.opt()], outs=[self.cc_out.opt()]),
            reads=["ccin"], writes=["ccout"])

    def scale_cols(self, idx, sidx, qmax):
        sc = self.sc
        sc.dma("ACT", lambda e, idx=idx, sidx=sidx: e.dma_start(
            out=self.sSC[:, sidx:sidx + 1],
            in_=self.cc_out[0:1, idx:idx + 1].to_broadcast((128, 1))),
            reads=["ccout"], writes=["ssc"])
        sc.op("DVE", lambda e, sidx=sidx, qmax=qmax: e.tensor_scalar(
            out=self.sSC[:, sidx:sidx + 1], in0=self.sSC[:, sidx:sidx + 1],
            scalar1=1.0 / qmax, scalar2=1e-12, op0=ALU.mult, op1=ALU.max),
            reads=["ssc"], writes=["ssc"])
        sc.op("DVE", lambda e, sidx=sidx: e.reciprocal(
            out=self.sSC[:, sidx + 1:sidx + 2], in_=self.sSC[:, sidx:sidx + 1]),
            reads=["ssc"], writes=["ssc"])

    def qdq_tile(self, out_ap, in_ap, sidx, rkeys, wkeys, par=None):
        sc = self.sc
        W = in_ap.shape[1]
        if par is None:
            tmp, tk = self.sEVAC[:, 0:W], "evac"
        else:
            tmp, tk = self.sQT[par][:, 0:W], f"qt:{par}"
        sc.op("DVE", lambda e: e.tensor_scalar(
            out=tmp, in0=in_ap, scalar1=self.sSC[:, sidx + 1:sidx + 2],
            scalar2=MAGIC, op0=ALU.mult, op1=ALU.add),
            reads=list(rkeys) + ["ssc"], writes=[tk])
        sc.op("DVE", lambda e: e.tensor_scalar(
            out=out_ap, in0=tmp, scalar1=MAGIC, scalar2=self.sSC[:, sidx:sidx + 1],
            op0=ALU.subtract, op1=ALU.mult),
            reads=[tk, "ssc"], writes=list(wkeys))

    def phase_ffn(self):
        cfg, sc = self.cfg, self.sc
        L = cfg.L
        assert L == 2048
        self.dequant_matrix(self.w_gate, self.wg_dq, FFN_I, D_MODEL, "wgdq")
        self.dequant_matrix(self.w_up, self.wu_dq, FFN_I, D_MODEL, "wudq")
        self.dequant_matrix(self.w_down, self.wd_dq, D_MODEL, FFN_I, "wddq")

        def yq_t(k):
            return (self.sY[:, (k % 4) * 2048:(k % 4 + 1) * 2048] if k < 4
                    else self.sZ[:, (k - 4) * 2048:(k - 3) * 2048])

        def yq_keys(k):
            return [f"yq:{k}", (f"y:{4 * (k % 4)}" if k < 4 else f"z:{4 * (k - 4)}")]

        for j in range(L // 512):
            t0 = 512 * j
            hsl = lambda k, t0=t0: self.h[:, k * L + t0: k * L + t0 + 512]
            self.ln_rows(512, hsl, lambda k, j=j: [f"h:{k}:{j}"])
            self.ln_bcasts(512)
            for k in range(NDC):
                ysl = yq_t(k)[:, t0:t0 + 512]
                hap = hsl(k)
                sc.op("DVE", lambda e, hap=hap: e.tensor_tensor(
                    out=self.sEVAC[:, 0:512], in0=hap, in1=self.pPS[:, 1024 + 0:1024 + 512],
                    op=ALU.mult),
                    reads=[f"h:{k}:{j}", "P2"], writes=["evac"])
                sc.op("DVE", lambda e, ysl=ysl: e.tensor_tensor(
                    out=ysl, in0=self.sEVAC[:, 0:512], in1=self.pPS[:, 1024 + 512:1024 + 1024],
                    op=ALU.subtract),
                    reads=["evac", "P3"], writes=yq_keys(k) + ["yq_all"])
                sc.op("DVE", lambda e, ysl=ysl: e.reduce_max(
                    out=self.sEVAC[:, 512:513], in_=ysl, axis=AXX,
                    apply_absolute_value=True),
                    reads=yq_keys(k), writes=["evac"])
                sc.op("DVE", lambda e: e.tensor_tensor(
                    out=self.sMXC[:, 0:1], in0=self.sMXC[:, 0:1],
                    in1=self.sEVAC[:, 512:513], op=ALU.max),
                    reads=["evac", "mxc"], writes=["mxc"])
        self.absmax_to_scalar(self.sMXC[:, 0:1], "mxc", 0)
        self.allreduce_max()
        self.scale_cols(0, 0, QMAX)
        for k in range(NDC):
            for j in range(L // 512):
                sl = yq_t(k)[:, 512 * j:512 * (j + 1)]
                self.qdq_tile(sl, sl, 0, yq_keys(k), yq_keys(k) + ["yq_all"])
        sc.op("DVE", lambda e: e.memset(self.sMXC[:, 1:3], 0.0),
              reads=["mxc"], writes=["mxc"])
        all_yq = ["yq_all"]
        for which, wdq, dst, mcol, wdqk, dkey in (
                (0, self.wu_dq, self.up_d, 1, "wudq", "uprawd"),
                (1, self.wg_dq, self.gate_d, 2, "wgdq", "gaterawd")):
            for ig in range(FFN_I // 512):
                buf = ig % 2
                for k in range(NDC):
                    sc.dma("SP", lambda e, ig=ig, buf=buf, k=k, wdq=wdq: e.dma_start_transpose(
                        out=self.sFW[buf][:, 512 * k:512 * (k + 1)],
                        in_=wdq[512 * ig:512 * (ig + 1), 128 * k:128 * (k + 1)]),
                        reads=[wdqk], writes=[f"fw:{buf}"])
                for isub in range(4):
                    iof = 512 * ig + 128 * isub
                    for j in range(L // 512):
                        pb = (4 * ig + isub + j) % 2
                        psl = self.pPS[:, 512 * pb:512 * (pb + 1)]
                        t0 = 512 * j

                        def mk(e, buf=buf, isub=isub, t0=t0, psl=psl):
                            r = None
                            for k in range(NDC):
                                r = e.matmul(out=psl,
                                             lhsT=self.sFW[buf][:, 512 * k + 128 * isub:
                                                                512 * k + 128 * isub + 128],
                                             rhs=yq_t(k)[:, t0:t0 + 512],
                                             start=(k == 0), stop=(k == NDC - 1))
                            return r
                        sc.ops("PE", [mk], reads=[f"fw:{buf}"] + all_yq,
                               writes=[f"P{pb}"])
                        sc.op("DVE", lambda e, psl=psl: e.tensor_copy(
                            out=self.sT1[0][:], in_=psl),
                            reads=[f"P{pb}"], writes=["t1"])
                        sc.op("DVE", lambda e: e.reduce_max(
                            out=self.sEVAC[:, 512:513], in_=self.sT1[0][:], axis=AXX,
                            apply_absolute_value=True),
                            reads=["t1"], writes=["evac"])
                        sc.op("DVE", lambda e, mcol=mcol: e.tensor_tensor(
                            out=self.sMXC[:, mcol:mcol + 1], in0=self.sMXC[:, mcol:mcol + 1],
                            in1=self.sEVAC[:, 512:513], op=ALU.max),
                            reads=["evac", "mxc"], writes=["mxc"])
                        sc.dma("SP", lambda e, iof=iof, t0=t0, dst=dst: e.dma_start(
                            out=dst[iof:iof + 128, t0:t0 + 512], in_=self.sT1[0][:]),
                            reads=["t1"], writes=[dkey])
        self.absmax_to_scalar(self.sMXC[:, 1:2], "mxc", 1)
        self.absmax_to_scalar(self.sMXC[:, 2:3], "mxc", 2)
        self.allreduce_max()
        self.scale_cols(1, 2, QMAX)
        self.scale_cols(2, 4, QMAX)
        sc.op("DVE", lambda e: e.memset(self.sMXC[:, 3:4], 0.0),
              reads=["mxc"], writes=["mxc"])
        for ic in range(FFN_I // 128):
            for j in range(L // 512):
                t0 = 512 * j
                sc.dma("SP", lambda e, ic=ic, t0=t0: e.dma_start(
                    out=self.sT1[0][:], in_=self.gate_d[128 * ic:128 * (ic + 1), t0:t0 + 512]),
                    reads=["gaterawd"], writes=["t1"])
                self.qdq_tile(self.sT1[0][:], self.sT1[0][:], 4, ["t1"], ["t1"])
                sc.op("ACT", lambda e: e.activation(out=self.sT2[0][:], in_=self.sT1[0][:],
                                                    func=AF.Sigmoid),
                      reads=["t1"], writes=["t2"])
                sc.op("DVE", lambda e: e.tensor_scalar(
                    out=self.sEVAC[:, 0:512], in0=self.sT2[0][:], scalar1=U16S, scalar2=MAGIC,
                    op0=ALU.mult, op1=ALU.add),
                    reads=["t2"], writes=["evac"])
                sc.op("DVE", lambda e: e.tensor_scalar(
                    out=self.sT2[0][:], in0=self.sEVAC[:, 0:512], scalar1=MAGIC,
                    scalar2=1.0 / U16S, op0=ALU.subtract, op1=ALU.mult),
                    reads=["evac"], writes=["t2"])
                sc.op("DVE", lambda e: e.tensor_tensor(out=self.sT3[0][:], in0=self.sT1[0][:],
                                                       in1=self.sT2[0][:], op=ALU.mult),
                      reads=["t1", "t2"], writes=["t3"])
                sc.op("DVE", lambda e: e.reduce_max(
                    out=self.sEVAC[:, 512:513], in_=self.sT3[0][:], axis=AXX,
                    apply_absolute_value=True),
                    reads=["t3"], writes=["evac"])
                sc.op("DVE", lambda e: e.tensor_tensor(
                    out=self.sMXC[:, 3:4], in0=self.sMXC[:, 3:4],
                    in1=self.sEVAC[:, 512:513], op=ALU.max),
                    reads=["evac", "mxc"], writes=["mxc"])
                sc.dma("SP", lambda e, ic=ic, t0=t0: e.dma_start(
                    out=self.gate_d[128 * ic:128 * (ic + 1), t0:t0 + 512], in_=self.sT3[0][:]),
                    reads=["t3"], writes=["t4d"])
                sc.dma("SP", lambda e, ic=ic, t0=t0: e.dma_start(
                    out=self.sT1[0][:], in_=self.up_d[128 * ic:128 * (ic + 1), t0:t0 + 512]),
                    reads=["uprawd"], writes=["t1"])
                self.qdq_tile(self.sT1[0][:], self.sT1[0][:], 2, ["t1"], ["t1"])
                sc.dma("SP", lambda e, ic=ic, t0=t0: e.dma_start(
                    out=self.up_d[128 * ic:128 * (ic + 1), t0:t0 + 512], in_=self.sT1[0][:]),
                    reads=["t1"], writes=["upqd"])
        self.absmax_to_scalar(self.sMXC[:, 3:4], "mxc", 3)
        self.allreduce_max()
        self.scale_cols(3, 6, QMAX)
        sc.op("DVE", lambda e: e.memset(self.sMXC[:, 4:5], 0.0),
              reads=["mxc"], writes=["mxc"])
        for ic in range(FFN_I // 128):
            for j in range(L // 512):
                t0 = 512 * j
                sc.dma("SP", lambda e, ic=ic, t0=t0: e.dma_start(
                    out=self.sT1[0][:], in_=self.gate_d[128 * ic:128 * (ic + 1), t0:t0 + 512]),
                    reads=["t4d"], writes=["t1"])
                self.qdq_tile(self.sT1[0][:], self.sT1[0][:], 6, ["t1"], ["t1"])
                sc.dma("SP", lambda e, ic=ic, t0=t0: e.dma_start(
                    out=self.sT2[0][:], in_=self.up_d[128 * ic:128 * (ic + 1), t0:t0 + 512]),
                    reads=["upqd"], writes=["t2"])
                sc.op("DVE", lambda e: e.tensor_tensor(out=self.sT3[0][:], in0=self.sT1[0][:],
                                                       in1=self.sT2[0][:], op=ALU.mult),
                      reads=["t1", "t2"], writes=["t3"])
                sc.op("DVE", lambda e: e.reduce_max(
                    out=self.sEVAC[:, 512:513], in_=self.sT3[0][:], axis=AXX,
                    apply_absolute_value=True),
                    reads=["t3"], writes=["evac"])
                sc.op("DVE", lambda e: e.tensor_tensor(
                    out=self.sMXC[:, 4:5], in0=self.sMXC[:, 4:5],
                    in1=self.sEVAC[:, 512:513], op=ALU.max),
                    reads=["evac", "mxc"], writes=["mxc"])
                sc.dma("SP", lambda e, ic=ic, t0=t0: e.dma_start(
                    out=self.gate_d[128 * ic:128 * (ic + 1), t0:t0 + 512], in_=self.sT3[0][:]),
                    reads=["t3"], writes=["t5d"])
        self.absmax_to_scalar(self.sMXC[:, 4:5], "mxc", 0)
        self.allreduce_max()
        self.scale_cols(0, 0, QMAX)
        allp = [f"P{b}" for b in range(8)]
        for j in range(L // 512):
            t0 = 512 * j
            for ic in range(FFN_I // 128):
                sc.dma("SP", lambda e, ic=ic, t0=t0: e.dma_start(
                    out=self.sT1[0][:],
                    in_=self.gate_d[128 * ic:128 * (ic + 1), t0:t0 + 512]),
                    reads=["t5d"], writes=["t1"])
                self.qdq_tile(self.sO[:, 512 * ic:512 * (ic + 1)], self.sT1[0][:],
                              0, ["t1"], [f"o:{ic}"])
            for ic in range(FFN_I // 128):
                buf = ic % 2
                for dh in range(2):
                    sc.dma("SP", lambda e, ic=ic, buf=buf, dh=dh: e.dma_start_transpose(
                        out=self.sFW[buf][:, 512 * dh:512 * (dh + 1)],
                        in_=self.wd_dq[512 * dh:512 * (dh + 1), 128 * ic:128 * (ic + 1)]),
                        reads=["wddq"], writes=[f"fw:{buf}"])

                def mk(e, ic=ic, buf=buf):
                    r = None
                    for dk in range(NDC):
                        po = self.pPS[:, 512 * dk:512 * dk + 512]
                        r = e.matmul(out=po, lhsT=self.sFW[buf][:, 128 * dk:128 * (dk + 1)],
                                     rhs=self.sO[:, 512 * ic:512 * (ic + 1)],
                                     start=(ic == 0), stop=(ic == FFN_I // 128 - 1))
                    return r
                sc.ops("PE", [mk], reads=[f"fw:{buf}", f"o:{ic}"], writes=allp)
            for dk in range(NDC):
                po = self.pPS[:, 512 * dk:512 * dk + 512]
                hsl = self.h[:, dk * L + t0: dk * L + t0 + 512]
                sc.op("DVE", lambda e, po=po, hsl=hsl: e.tensor_tensor(
                    out=self.sEVAC[:, 0:512], in0=hsl, in1=po, op=ALU.add),
                    reads=[f"P{dk}", f"h:{dk}:{j}"], writes=["evac"])
                sc.dma("SP", lambda e, dk=dk, t0=t0: e.dma_start(
                    out=self.out_fm[128 * dk:128 * (dk + 1), t0:t0 + 512],
                    in_=self.sEVAC[:, 0:512]),
                    reads=["evac"], writes=["OUT"])

    def output_no_ffn(self):
        sc, L = self.sc, self.cfg.L
        for k in range(NDC):
            for j in range(L // 512):
                sc.dma("SP", lambda e, k=k, j=j: e.dma_start(
                    out=self.out_fm[128 * k:128 * (k + 1), 512 * j:512 * (j + 1)],
                    in_=self.h[:, k * L + 512 * j: k * L + 512 * (j + 1)]),
                    reads=[f"h:{k}:{j}"], writes=["OUT"])

    def absmax_to_out(self, col_ap, rkey, idx):
        sc = self.sc
        sc.dma("ACT", lambda e, col_ap=col_ap: e.dma_start(out=self.red_b, in_=col_ap),
               reads=[rkey], writes=["redb"])
        sc.dma("ACT", lambda e: e.dma_start(
            out=self.sRED[:], in_=self.red_b.rearrange("p o -> (p o)").unsqueeze(0)),
            reads=["redb"], writes=["redrow"])
        sc.op("DVE", lambda e, idx=idx: e.reduce_max(
            out=self.sSCR[0:1, idx:idx + 1], in_=self.sRED[:], axis=AXX),
            reads=["redrow"], writes=["scr"])
        sc.dma("ACT", lambda e, idx=idx: e.dma_start(
            out=self.mx_out[0:1, idx:idx + 1], in_=self.sSCR[0:1, idx:idx + 1]),
            reads=["scr"], writes=["OUTMX"])

    def reduce_mxcat(self):
        sc, NCC = self.sc, self.cfg.n_cores
        sc.dma("ACT", lambda e: e.dma_start(out=self.mxr_d, in_=self.mxcat),
               writes=["mxrd"])
        sc.dma("ACT", lambda e: e.dma_start(
            out=self.sRED[0:1, 0:6 * NCC],
            in_=self.mxr_d.rearrange("a b -> (a b)").unsqueeze(0)),
            reads=["mxrd"], writes=["redrow"])
        sc.op("DVE", lambda e, NCC=NCC: e.reduce_max(
            out=self.sSCR[0:1, 8:14].unsqueeze(2),
            in_=self.sRED.ap()[0:1, 0:6 * NCC].rearrange("o (a b) -> o a b", a=6),
            axis=AXX),
            reads=["redrow"], writes=["scr"])
        sc.dma("ACT", lambda e: e.dma_start(out=self.gmax_d[0:1, 0:6],
                                            in_=self.sSCR[0:1, 8:14]),
               reads=["scr"], writes=["gmaxd"])

    def scale_cols_g(self, idx, sidx, qmax):
        sc = self.sc
        sc.dma("ACT", lambda e, idx=idx, sidx=sidx: e.dma_start(
            out=self.sSC[:, sidx:sidx + 1],
            in_=self.gmax_d[0:1, idx:idx + 1].to_broadcast((128, 1))),
            reads=["gmaxd"], writes=["ssc"])
        sc.op("DVE", lambda e, sidx=sidx, qmax=qmax: e.tensor_scalar(
            out=self.sSC[:, sidx:sidx + 1], in0=self.sSC[:, sidx:sidx + 1],
            scalar1=1.0 / qmax, scalar2=1e-12, op0=ALU.mult, op1=ALU.max),
            reads=["ssc"], writes=["ssc"])
        sc.op("DVE", lambda e, sidx=sidx: e.reciprocal(
            out=self.sSC[:, sidx + 1:sidx + 2], in_=self.sSC[:, sidx:sidx + 1]),
            reads=["ssc"], writes=["ssc"])

    def stage1_tail(self):
        cfg, sc = self.cfg, self.sc
        L = cfg.L
        for j in range(L // 512):
            t0 = 512 * j
            hsl = lambda k, t0=t0: self.h[:, k * L + t0: k * L + t0 + 512]
            self.ln_rows(512, hsl, lambda k, j=j: [f"h:{k}:{j}"])
            self.ln_bcasts(512)
            for k in range(NDC):
                hap = hsl(k)
                sc.op("DVE", lambda e, hap=hap: e.tensor_tensor(
                    out=self.sEVAC[:, 0:512], in0=hap, in1=self.pPS[:, 1024 + 0:1024 + 512],
                    op=ALU.mult),
                    reads=[f"h:{k}:{j}", "P2"], writes=["evac"])
                sc.op("DVE", lambda e: e.tensor_tensor(
                    out=self.sXB[:, 0:512], in0=self.sEVAC[:, 0:512],
                    in1=self.pPS[:, 1024 + 512:1024 + 1024], op=ALU.subtract),
                    reads=["evac", "P3"], writes=["sXB"])
                sc.op("DVE", lambda e: e.reduce_max(
                    out=self.sEVAC[:, 512:513], in_=self.sXB[:, 0:512], axis=AXX,
                    apply_absolute_value=True),
                    reads=["sXB"], writes=["evac"])
                sc.op("DVE", lambda e: e.tensor_tensor(
                    out=self.sMXC[:, 0:1], in0=self.sMXC[:, 0:1],
                    in1=self.sEVAC[:, 512:513], op=ALU.max),
                    reads=["evac", "mxc"], writes=["mxc"])
                sc.dma("SP", lambda e, k=k, t0=t0: e.dma_start(
                    out=self.y_out[128 * k:128 * (k + 1), t0:t0 + 512],
                    in_=self.sXB[:, 0:512]),
                    reads=["sXB"], writes=["OUT"])
        self.absmax_to_out(self.sMXC[:, 0:1], "mxc", 0)
        self.output_no_ffn()

    def stage2(self):
        cfg, sc = self.cfg, self.sc
        L = cfg.L
        self.dequant_matrix(self.w_gate, self.wg_dq, FFN_I, D_MODEL, "wgdq")
        self.dequant_matrix(self.w_up, self.wu_dq, FFN_I, D_MODEL, "wudq")
        self.reduce_mxcat()
        self.scale_cols_g(0, 0, QMAX)

        def yq_t(k):
            return self.sYQ.ap()[:, 2048 * k:2048 * (k + 1)]
        yq_keys_all = [f"yq:{k}" for k in range(NDC)]
        for ti, (k, j) in enumerate((k, j) for k in range(NDC)
                                    for j in range(L // 512)):
            p = ti % 2
            t0 = 512 * j
            sl = yq_t(k)[:, t0:t0 + 512]
            sc.dma("SP", lambda e, k=k, t0=t0, sl=sl: e.dma_start(
                out=sl, in_=self.y_in[128 * k:128 * (k + 1), t0:t0 + 512]),
                writes=[f"yq:{k}"])
            self.qdq_tile(sl, sl, 0, [f"yq:{k}"], [f"yq:{k}"], par=p)
        for which, wdq, dst, mcol, wdqk, dkey in (
                (0, self.wu_dq, self.upraw_out, 1, "wudq", "uprawd"),
                (1, self.wg_dq, self.gateraw_out, 2, "wgdq", "gaterawd")):
            for ig in range(FFN_I // 512):
                buf = ig % 2
                for k in range(NDC):
                    sc.dma("SP", lambda e, ig=ig, buf=buf, k=k, wdq=wdq: e.dma_start_transpose(
                        out=self.sFW[buf][:, 512 * k:512 * (k + 1)],
                        in_=wdq[512 * ig:512 * (ig + 1), 128 * k:128 * (k + 1)]),
                        reads=[wdqk], writes=[f"fw:{buf}"])
                for isub in range(4):
                    iof = 512 * ig + 128 * isub
                    for j in range(L // 512):
                        pb = (4 * ig + isub + j) % 2
                        psl = self.pPS[:, 512 * pb:512 * (pb + 1)]
                        t0 = 512 * j

                        def mk(e, buf=buf, isub=isub, t0=t0, psl=psl):
                            r = None
                            for k in range(NDC):
                                r = e.matmul(out=psl,
                                             lhsT=self.sFW[buf][:, 512 * k + 128 * isub:
                                                                512 * k + 128 * isub + 128],
                                             rhs=yq_t(k)[:, t0:t0 + 512],
                                             start=(k == 0), stop=(k == NDC - 1))
                            return r
                        p = (4 * ig + isub + j) % 2
                        t1 = self.sT1[p]
                        k1 = f"t1:{p}"
                        mxs = self.sEVAC[:, 512 + p:513 + p]
                        kx = f"mxs:{p}"
                        sc.ops("PE", [mk], reads=[f"fw:{buf}"] + yq_keys_all,
                               writes=[f"P{pb}"])
                        sc.op("DVE", lambda e, psl=psl, t1=t1: e.tensor_copy(
                            out=t1[:], in_=psl),
                            reads=[f"P{pb}"], writes=[k1])
                        sc.op("DVE", lambda e, t1=t1, mxs=mxs: e.reduce_max(
                            out=mxs, in_=t1[:], axis=AXX,
                            apply_absolute_value=True),
                            reads=[k1], writes=[kx])
                        sc.op("DVE", lambda e, mcol=mcol, mxs=mxs: e.tensor_tensor(
                            out=self.sMXC[:, mcol:mcol + 1], in0=self.sMXC[:, mcol:mcol + 1],
                            in1=mxs, op=ALU.max),
                            reads=[kx, "mxc"], writes=["mxc"])
                        sc.dma("POOL", lambda e, iof=iof, t0=t0, dst=dst, t1=t1: e.dma_start(
                            out=dst[iof:iof + 128, t0:t0 + 512], in_=t1[:]),
                            reads=[k1], writes=["OUT"])
        self.absmax_to_out(self.sMXC[:, 1:2], "mxc", 1)
        self.absmax_to_out(self.sMXC[:, 2:3], "mxc", 2)

    def stage3(self):
        cfg, sc = self.cfg, self.sc
        L = cfg.L
        self.reduce_mxcat()
        self.scale_cols_g(2, 4, QMAX)
        for ti, (ic, j) in enumerate((ic, j) for ic in range(FFN_I // 128)
                                     for j in range(L // 512)):
            p = ti % 2
            t0 = 512 * j
            t1, t2, t3 = self.sT1[p], self.sT2[p], self.sT3[p]
            k1, k2, k3 = f"t1:{p}", f"t2:{p}", f"t3:{p}"
            mxs = self.sEVAC[:, 512 + p:513 + p]
            kx = f"mxs:{p}"
            sc.dma("SP", lambda e, ic=ic, t0=t0, t1=t1: e.dma_start(
                out=t1[:], in_=self.gateraw_in[128 * ic:128 * (ic + 1), t0:t0 + 512]),
                writes=[k1])
            self.qdq_tile(t1[:], t1[:], 4, [k1], [k1], par=p)
            sc.op("ACT", lambda e, t1=t1, t2=t2: e.activation(
                out=t2[:], in_=t1[:], func=AF.Sigmoid), reads=[k1], writes=[k2])
            sc.op("DVE", lambda e, t2=t2, p=p: e.tensor_scalar(
                out=self.sQT[p][:], in0=t2[:], scalar1=U16S, scalar2=MAGIC,
                op0=ALU.mult, op1=ALU.add),
                reads=[k2], writes=[f"qt:{p}"])
            sc.op("DVE", lambda e, t2=t2, p=p: e.tensor_scalar(
                out=t2[:], in0=self.sQT[p][:], scalar1=MAGIC,
                scalar2=1.0 / U16S, op0=ALU.subtract, op1=ALU.mult),
                reads=[f"qt:{p}"], writes=[k2])
            sc.op("DVE", lambda e, t1=t1, t2=t2, t3=t3: e.tensor_tensor(
                out=t3[:], in0=t1[:], in1=t2[:], op=ALU.mult),
                reads=[k1, k2], writes=[k3])
            sc.op("DVE", lambda e, t3=t3, mxs=mxs: e.reduce_max(
                out=mxs, in_=t3[:], axis=AXX, apply_absolute_value=True),
                reads=[k3], writes=[kx])
            sc.op("DVE", lambda e, mxs=mxs: e.tensor_tensor(
                out=self.sMXC[:, 3:4], in0=self.sMXC[:, 3:4], in1=mxs, op=ALU.max),
                reads=[kx, "mxc"], writes=["mxc"])
            sc.dma("POOL", lambda e, ic=ic, t0=t0, t3=t3: e.dma_start(
                out=self.t4_out[128 * ic:128 * (ic + 1), t0:t0 + 512], in_=t3[:]),
                reads=[k3], writes=["OUT"])
        self.absmax_to_out(self.sMXC[:, 3:4], "mxc", 3)

    def stage4(self):
        cfg, sc = self.cfg, self.sc
        L = cfg.L
        self.reduce_mxcat()
        self.scale_cols_g(3, 6, QMAX)
        self.scale_cols_g(1, 2, QMAX)
        for ti, (ic, j) in enumerate((ic, j) for ic in range(FFN_I // 128)
                                     for j in range(L // 512)):
            p = ti % 2
            t0 = 512 * j
            t1, t2, t3 = self.sT1[p], self.sT2[p], self.sT3[p]
            k1, k2, k3 = f"t1:{p}", f"t2:{p}", f"t3:{p}"
            mxs = self.sEVAC[:, 512 + p:513 + p]
            kx = f"mxs:{p}"
            sc.dma("SP", lambda e, ic=ic, t0=t0, t1=t1: e.dma_start(
                out=t1[:], in_=self.t4_in[128 * ic:128 * (ic + 1), t0:t0 + 512]),
                writes=[k1])
            self.qdq_tile(t1[:], t1[:], 6, [k1], [k1], par=p)
            sc.dma("ACT", lambda e, ic=ic, t0=t0, t2=t2: e.dma_start(
                out=t2[:], in_=self.upraw_in[128 * ic:128 * (ic + 1), t0:t0 + 512]),
                writes=[k2])
            sc.op("DVE", lambda e, t2=t2, p=p: e.tensor_scalar(
                out=self.sQT[p][:], in0=t2[:], scalar1=self.sSC[:, 3:4],
                scalar2=MAGIC, op0=ALU.mult, op1=ALU.add),
                reads=[k2, "ssc"], writes=[f"qt:{p}"])
            sc.op("DVE", lambda e, t2=t2, p=p: e.tensor_scalar(
                out=t2[:], in0=self.sQT[p][:], scalar1=MAGIC, scalar2=self.sSC[:, 2:3],
                op0=ALU.subtract, op1=ALU.mult),
                reads=[f"qt:{p}", "ssc"], writes=[k2])
            sc.op("DVE", lambda e, t1=t1, t2=t2, t3=t3: e.tensor_tensor(
                out=t3[:], in0=t1[:], in1=t2[:], op=ALU.mult),
                reads=[k1, k2], writes=[k3])
            sc.op("DVE", lambda e, t3=t3, mxs=mxs: e.reduce_max(
                out=mxs, in_=t3[:], axis=AXX, apply_absolute_value=True),
                reads=[k3], writes=[kx])
            sc.op("DVE", lambda e, mxs=mxs: e.tensor_tensor(
                out=self.sMXC[:, 4:5], in0=self.sMXC[:, 4:5], in1=mxs, op=ALU.max),
                reads=[kx, "mxc"], writes=["mxc"])
            sc.dma("POOL", lambda e, ic=ic, t0=t0, t3=t3: e.dma_start(
                out=self.t5_out[128 * ic:128 * (ic + 1), t0:t0 + 512], in_=t3[:]),
                reads=[k3], writes=["OUT"])
        self.absmax_to_out(self.sMXC[:, 4:5], "mxc", 4)

    def stage5(self):
        cfg, sc = self.cfg, self.sc
        L = cfg.L
        self.dequant_matrix(self.w_down, self.wd_dq, D_MODEL, FFN_I, "wddq")
        self.reduce_mxcat()
        self.scale_cols_g(4, 0, QMAX)
        allp = [f"P{b}" for b in range(8)]
        for j in range(L // 512):
            t0 = 512 * j
            for ic in range(FFN_I // 128):
                p = ic % 2
                t1 = self.sT1[p]
                k1 = f"t1:{p}"
                sc.dma("SP", lambda e, ic=ic, t0=t0, t1=t1: e.dma_start(
                    out=t1[:], in_=self.t5_in[128 * ic:128 * (ic + 1), t0:t0 + 512]),
                    writes=[k1])
                self.qdq_tile(self.sO[:, 512 * ic:512 * (ic + 1)], t1[:],
                              0, [k1], [f"o:{ic}"], par=p)
            for ic in range(FFN_I // 128):
                buf = ic % 2
                for dh in range(2):
                    sc.dma("SP", lambda e, ic=ic, buf=buf, dh=dh: e.dma_start_transpose(
                        out=self.sFW[buf][:, 512 * dh:512 * (dh + 1)],
                        in_=self.wd_dq[512 * dh:512 * (dh + 1), 128 * ic:128 * (ic + 1)]),
                        reads=["wddq"], writes=[f"fw:{buf}"])

                def mk(e, ic=ic, buf=buf):
                    r = None
                    for dk in range(NDC):
                        po = self.pPS[:, 512 * dk:512 * dk + 512]
                        r = e.matmul(out=po, lhsT=self.sFW[buf][:, 128 * dk:128 * (dk + 1)],
                                     rhs=self.sO[:, 512 * ic:512 * (ic + 1)],
                                     start=(ic == 0), stop=(ic == FFN_I // 128 - 1))
                    return r
                sc.ops("PE", [mk], reads=[f"fw:{buf}", f"o:{ic}"], writes=allp)
            for dk in range(NDC):
                p = dk % 2
                po = self.pPS[:, 512 * dk:512 * dk + 512]
                qt = self.sQT[p]
                kq = f"qt:{p}"
                sc.dma("SP", lambda e, dk=dk, t0=t0, qt=qt: e.dma_start(
                    out=qt[:], in_=self.x1_in[128 * dk:128 * (dk + 1), t0:t0 + 512]),
                    writes=[kq])
                sc.op("DVE", lambda e, po=po, qt=qt: e.tensor_tensor(
                    out=qt[:], in0=qt[:], in1=po, op=ALU.add),
                    reads=[f"P{dk}", kq], writes=[kq])
                sc.dma("POOL", lambda e, dk=dk, t0=t0, qt=qt: e.dma_start(
                    out=self.out_fm[128 * dk:128 * (dk + 1), t0:t0 + 512],
                    in_=qt[:]),
                    reads=[kq], writes=["OUT"])

    def build(self):
        cfg, sc = self.cfg, self.sc
        st = cfg.stage
        self.load_consts()
        if st in ("all", "s1"):
            self.preconvert_weights()
            self.phase_init_ln()
            for l in range(cfg.nlayer):
                self.load_layer_params(l)
                for seg in range(cfg.NSEG):
                    self.phase_A(l, seg)
                    self.phase_B_prep_front(l, seg, 0)
                    self.phase_B_prep_back(l, seg, 0)
                    for c in range(cfg.NCH):
                        self.phase_B_compute(l, seg, c)
                    self.phase_C(l, seg)
            self.phase_x1()
        if st == "all":
            if cfg.do_ffn:
                sc.barrier()
                self.phase_ffn()
            else:
                self.output_no_ffn()
        elif st == "s1":
            sc.barrier()
            if cfg.do_ffn:
                self.stage1_tail()
            else:
                self.output_no_ffn()
        elif st == "s2":
            self.stage2()
        elif st == "s3":
            self.stage3()
        elif st == "s4":
            self.stage4()
        elif st == "s5":
            self.stage5()
        sc.finish(["OUT", "OUTMX"] if (cfg.do_ffn and st != "s5" and st != "all")
                  else ["OUT"])
        with self.nc.Block() as block:
            sc.emit(block)
        return self.nc


def host_constants():
    e64s = np.zeros((NH, 512), np.float32)
    e64sb = np.zeros((NH, 512), np.float32)
    for r in range(NH):
        hr = r % 32
        for t in range(32):
            if hr < 16:
                e64s[r, t * 16 + hr] = -1.0
            else:
                e64sb[r, t * 16 + (hr - 16)] = -1.0
    e64x = np.tile(e64s, (1, 4))
    e64xb = np.tile(e64sb, (1, 4))
    trid = np.zeros((32, 512), np.float32)
    for r in range(32):
        trid[r, r * 16:(r + 1) * 16] = -BIG
    trid = np.tile(trid, (1, 4))
    tri16 = np.zeros((32, 16 * 128), np.float32)
    p = np.arange(128)
    for i in range(16):
        sl = i % 4
        for r in range(32):
            tri16[r, 128 * i:128 * (i + 1)] = (p >= 32 * sl + r + 1)
    esel = np.zeros((NH, D_INNER), np.float32)
    for k in range(NZC):
        for q in range(128):
            esel[4 * k + q // 32, 128 * k + q] = 1.0
    idb = np.eye(128, dtype=np.float32)
    bf = lambda a: a.astype(ml_dtypes.bfloat16)
    return dict(c_e64x=bf(e64x), c_e64xb=bf(e64xb), c_trid=bf(trid), c_tri16=bf(tri16),
                c_esel=bf(esel), c_idb=bf(idb),
                c_ones2=bf(np.ones((2, 128), np.float32)),
                c_ones64=bf(np.ones((NH, 128), np.float32)),
                c_onescol=bf(np.ones((128, 1), np.float32)))


def host_layout(inputs, cfg: Cfg):
    NL = cfg.nlayer
    cw = np.asarray(inputs["L0_attn_conv_w"], np.float32)[:NL]
    cb = np.asarray(inputs["L0_attn_conv_b"], np.float32)[:NL]
    dcwL = np.zeros((NL, NXC * DCONV, 128, 128), np.float32)
    idx = np.arange(128)
    for l in range(NL):
        for j in range(NXC):
            for k in range(DCONV):
                dcwL[l, 4 * j + k, idx, idx] = cw[l, 128 * j + idx, k]
    cbL = np.ascontiguousarray(cb.reshape(NL, NXC, 128).transpose(0, 2, 1))
    colmaj = lambda v, n: np.ascontiguousarray(v.reshape(v.shape[:-1] + (n, 128)).swapaxes(-1, -2))
    gnwR = colmaj(np.asarray(inputs["L0_attn_gate_norm_w"], np.float32)[:NL], NZC)
    bnwR = colmaj(np.asarray(inputs["L0_attn_block_norm_w"], np.float32)[:NL], NDC)
    fnwR = colmaj(np.asarray(inputs["L0_attn_final_norm_w"], np.float32), NDC)
    col = lambda v: np.ascontiguousarray(np.asarray(v, np.float32)[:NL][:, :, None])
    m = dict(
        wi=np.ascontiguousarray(np.asarray(inputs["L0_attn_W_in"], np.float32)[:NL]),
        wo=np.ascontiguousarray(np.asarray(inputs["L0_attn_W_out"], np.float32)[:NL]),
        dcwL=dcwL, cbL=cbL,
        dtbL=col(inputs["L0_attn_dt_bias"]),
        alogL=col(inputs["L0_attn_A_log"]),
        dL=col(inputs["L0_attn_D"]),
        gnwR=gnwR, bnwR=bnwR, fnwR=fnwR,
        w_gate=np.ascontiguousarray(np.asarray(inputs["L0_ffn_w_gate"], np.float32)),
        w_up=np.ascontiguousarray(np.asarray(inputs["L0_ffn_w_up"], np.float32)),
        w_down=np.ascontiguousarray(np.asarray(inputs["L0_ffn_w_down"], np.float32)),
    )
    m.update(host_constants())
    return m


def kernel(**inputs):
    n_cores = 8
    x = np.asarray(inputs["L0_attn_x"], np.float32)
    L = x.shape[1]
    shared = host_layout(inputs, Cfg(L=L, n_cores=n_cores))
    x_fms = [np.ascontiguousarray(x[i].T) for i in range(n_cores)]
    mx = np.zeros((6, n_cores), np.float32)

    def run(stage, extra_names, per_core_extra):
        cfg = Cfg(L=L, nlayer=4, do_ffn=True, n_cores=n_cores, stage=stage)
        b = Builder(cfg)
        nc = b.build()
        in_names = {t.name for t in
                    (a for a in nc.m.functions[0].allocations
                     if getattr(a, "kind", None) == "ExternalInput")}
        maps = []
        for i in range(n_cores):
            m = {k: v for k, v in shared.items()}
            m["x_fm"] = x_fms[i]
            for name, arrs in per_core_extra.items():
                m[name] = arrs[i]
            for name in extra_names:
                m[name] = extra_names[name]
            maps.append(m)
        res = run_bass_kernel_spmd(nc, maps, core_ids=list(range(n_cores)))
        return res

    r1 = run("s1", {}, {})
    x1s = [np.asarray(r1.results[i]["out_fm"]) for i in range(n_cores)]
    ys = [np.asarray(r1.results[i]["y_out"]) for i in range(n_cores)]
    for i in range(n_cores):
        mx[0, i] = np.asarray(r1.results[i]["mx_out"]).ravel()[0]

    r2 = run("s2", {"mxcat": mx}, {"y_in": ys})
    ups = [np.asarray(r2.results[i]["upraw_out"]) for i in range(n_cores)]
    gates = [np.asarray(r2.results[i]["gateraw_out"]) for i in range(n_cores)]
    for i in range(n_cores):
        v = np.asarray(r2.results[i]["mx_out"]).ravel()
        mx[1, i] = v[1]
        mx[2, i] = v[2]

    r3 = run("s3", {"mxcat": mx}, {"gateraw_in": gates})
    t4s = [np.asarray(r3.results[i]["t4_out"]) for i in range(n_cores)]
    for i in range(n_cores):
        mx[3, i] = np.asarray(r3.results[i]["mx_out"]).ravel()[3]

    r4 = run("s4", {"mxcat": mx}, {"t4_in": t4s, "upraw_in": ups})
    t5s = [np.asarray(r4.results[i]["t5_out"]) for i in range(n_cores)]
    for i in range(n_cores):
        mx[4, i] = np.asarray(r4.results[i]["mx_out"]).ravel()[4]

    r5 = run("s5", {"mxcat": mx}, {"t5_in": t5s, "x1_in": x1s})
    out = np.stack([np.asarray(r5.results[i]["out_fm"]).T for i in range(n_cores)],
                   axis=0)
    return out.astype(np.float32)
